# version 11
# baseline (speedup 1.0000x reference)
import numpy as np
import ml_dtypes
import concourse.bass as bass
import concourse.mybir as mybir
from concourse.bass_utils import run_bass_kernel_spmd

F32 = mybir.dt.float32
BF16 = mybir.dt.bfloat16
I8 = mybir.dt.int8
U8 = mybir.dt.uint8
AF = mybir.ActivationFunctionType
ALU = mybir.AluOpType
AX = mybir.AxisListType

NCORE = 8
D = 2048
NT = 1024
H = 16
NF = 16
QOC = 48
POC = 16
EPS32 = float(np.finfo(np.float32).eps)
SCALE = float(128.0 ** -0.5)
QKV_N = float(6144 * 2048)
PROJ_N = float(2048 * 2048)


class Arena:
    def __init__(self, nc, backing):
        self.nc = nc
        self.base = nc.lookup_mloc(backing).addr
        self.cap = backing.shape[1]

    def at(self, name, shape, dtype, off):
        per = 1
        for s in shape[1:]:
            per *= s
        per *= mybir.dt.size(dtype)
        assert off % 32 == 0 and off + per <= self.cap, (name, off, per, self.cap)
        return self.nc.alloc_sbuf_tensor_at(name, shape, dtype,
                                            offset=self.base + off)


def build_program():
    nc = bass.Bass("TRN2", target_bir_lowering=False,
                   detect_race_conditions=False)

    d_x = nc.dram_tensor("xT", [D, NT], F32, kind="ExternalInput")
    d_xp = nc.dram_tensor("xprevT", [D, NT], F32, kind="ExternalInput")
    d_qkv = nc.dram_tensor("qkvt", [QOC * 128, 2048], F32, kind="ExternalInput")
    d_proj = nc.dram_tensor("projt", [POC * 128, 2048], F32, kind="ExternalInput")
    d_wg = nc.dram_tensor("wgr", [64 * 128, 2048], I8, kind="ExternalInput")
    d_wu = nc.dram_tensor("wur", [64 * 128, 2048], I8, kind="ExternalInput")
    d_wd = nc.dram_tensor("wdr", [64 * 128, 2048], I8, kind="ExternalInput")
    d_qn = nc.dram_tensor("qn", [128, 16], F32, kind="ExternalInput")
    d_nw = nc.dram_tensor("nw", [128, 16], F32, kind="ExternalInput")
    d_pnw = nc.dram_tensor("pnw", [128, 16], F32, kind="ExternalInput")
    d_sd = nc.dram_tensor("sdv", [128, 16], F32, kind="ExternalInput")
    d_sg = nc.dram_tensor("sgv", [128, 64], F32, kind="ExternalInput")
    d_su = nc.dram_tensor("suv", [128, 64], F32, kind="ExternalInput")
    d_s16 = nc.dram_tensor("s16", [128, 16], F32, kind="ExternalInput")
    d_cD = nc.dram_tensor("cD", [128, 128], F32, kind="ExternalInput")
    d_cPEN = nc.dram_tensor("cPEN", [128, 128], F32, kind="ExternalInput")
    d_cT = nc.dram_tensor("cT", [128, 128], F32, kind="ExternalInput")
    d_cR = nc.dram_tensor("cR", [128, 128], F32, kind="ExternalInput")
    d_idn = nc.dram_tensor("idn", [128, 128], BF16, kind="ExternalInput")
    d_cst = nc.dram_tensor("cstv", [128, 8], F32, kind="ExternalInput")
    d_out = nc.dram_tensor("outT", [D, NT], F32, kind="ExternalOutput")
    d_scq = nc.dram_tensor("scr_q", [QOC * 128, 2048], BF16)
    d_scp = nc.dram_tensor("scr_p", [POC * 128, 2048], BF16)
    d_scf = nc.dram_tensor("scr_f", [4 * 2048, NT], F32)

    from contextlib import ExitStack
    ctx = ExitStack()
    sb = lambda name, shape, dt: ctx.enter_context(nc.sbuf_tensor(name, shape, dt))
    ps = lambda name, shape, dt: ctx.enter_context(nc.psum_tensor(name, shape, dt))
    sem = lambda name: ctx.enter_context(nc.semaphore(name))

    xn = sb("xn", [128, NF * NT], BF16)
    bcRQ = sb("bcRQ", [128, NT], F32)
    S = sb("S", [128, H * 128], F32)
    Sb = sb("Sbuf16", [128, H * 128], BF16)
    maskT = sb("maskT", [128, H * 128], BF16)
    lamTt = sb("lamTt", [128, H * 128], BF16)
    lamRt = sb("lamRt", [128, H * 128], BF16)
    lamTc = sb("lamTc", [128, H], F32)
    negs = sb("negsv", [128, H], F32)
    s16 = sb("s16v", [128, H], F32)
    qn_s = sb("qn_s", [128, 16], F32)
    nw_s = sb("nw_s", [128, 16], F32)
    pnw_s = sb("pnw_s", [128, 16], F32)
    fw_s = sb("fw_s", [128, 16], F32)
    sd_s = sb("sd_s", [128, 16], F32)
    sg_s = sb("sg_s", [128, 64], F32)
    su_s = sb("su_s", [128, 64], F32)
    cD = sb("cDs", [128, 128], F32)
    cPEN = sb("cPENs", [128, 128], F32)
    cT = sb("cTs", [128, 128], F32)
    cR = sb("cRs", [128, 128], F32)
    idn = sb("idns", [128, 128], BF16)
    ones_t = sb("ones_t", [128, 128], F32)
    onescol = sb("onescol", [128, 1], F32)
    onesrow = sb("onesrow", [1, 128], F32)
    cst = sb("cstt", [128, 8], F32)
    lamRc = sb("lamRc", [128, H], F32)
    rows = sb("rowscr", [1, 1024], F32)
    c_qp = sb("c_qp", [128, 2], F32)
    negc_qp = sb("negc_qp", [128, 2], F32)
    absq = sb("absq", [128, 48], F32)
    absp = sb("absp", [128, 16], F32)
    absr = sb("absr", [128, 2], F32)
    invrow = sb("invrow", [1, 2], F32)
    bcRQp = sb("bcRQp", [128, NT], F32)

    AR_BYTES = 138240
    backing = sb("arena", [128, AR_BYTES], U8)
    ar = Arena(nc, backing)
    a_x16 = ar.at("x16", [128, NF * NT], F32, 0)
    a_sq = [ar.at("sq", [128, NT], F32, 65536),
            ar.at("sq2", [128, NT], F32, 69632)]
    a_bcA = ar.at("bcA", [128, NT], F32, 73728)
    a_bcB = ar.at("bcB", [128, NT], F32, 77824)
    a_fuse = ar.at("fuse", [128, 16 * NT], BF16, 0)
    a_tb = [ar.at("tb", [128, 2048], F32, 32768),
            ar.at("tb2", [128, 2048], F32, 40960)]
    a_tern = [ar.at("tern", [128, 2048], BF16, 49152),
              ar.at("tern2", [128, 2048], BF16, 53248)]
    a_gs = ar.at("gsil", [128, NT], BF16, 57344)
    a_dt = [ar.at("dt", [128, NT], F32, 59392),
            ar.at("dt2", [128, NT], F32, 63488)]
    a_abs = ar.at("absdump", [128, 2048], BF16, 67584)
    a_wd = [ar.at("wd", [128, 2048], BF16, 71680),
            ar.at("wd2", [128, 2048], BF16, 75776)]
    a_wg = [ar.at("wg", [128, 2048], BF16, 81920),
            ar.at("wg2", [128, 2048], BF16, 86016)]
    a_wu = [ar.at("wu", [128, 2048], BF16, 90112),
            ar.at("wu2", [128, 2048], BF16, 94208)]
    a_xpr = [ar.at("xpr", [128, NT], F32, 98304),
             ar.at("xpr2", [128, NT], F32, 102400)]
    a_sqp = [ar.at("sqp", [128, NT], F32, 106496),
             ar.at("sqp2", [128, NT], F32, 110592)]
    a_ksb = [ar.at("ksb0", [128, 512], BF16, 98304),
             ar.at("ksb1", [128, 512], BF16, 99328)]
    a_vsb = [ar.at("vsb0", [128, 512], BF16, 100352),
             ar.at("vsb1", [128, 512], BF16, 101376)]
    a_xpb = [ar.at("xpb", [128, 512], F32, 102400),
             ar.at("xpb2", [128, 512], F32, 104448)]
    a_xnp = ar.at("xnp", [128, 16 * 512], BF16, 106496)
    a_ts2 = [ar.at("ts2a", [128, 2048], BF16, 122880),
             ar.at("ts2b", [128, 2048], BF16, 126976)]
    a_bcAp = ar.at("bcAp", [128, NT], F32, 122880)
    a_bcBp = ar.at("bcBp", [128, NT], F32, 126976)
    a_patq = ar.at("patq", [128, 2048], BF16, 131072)
    a_vT = ar.at("vT", [128, 512], BF16, 135168)
    a_kT = ar.at("kT", [128, 512], BF16, 136192)
    a_mtmp = [ar.at("mtmp", [128, 128], F32, 137216),
              ar.at("mtmp2", [128, 128], F32, 137728)]
    a_o16 = ar.at("o16", [128, NF * NT], BF16, 0)
    a_qkv = ar.at("qkvb", [128, QOC * 512], BF16, 32768)
    a_ts3 = [ar.at("ts3a", [128, 2048], BF16, 81920),
             ar.at("ts3b", [128, 2048], BF16, 86016)]
    a_qt = ar.at("qtmp", [128, 512], BF16, 90112)
    a_sm = ar.at("scoresm", [128, 512], BF16, 91136)
    a_vT3 = ar.at("vT3", [128, 512], BF16, 92160)
    a_kT3 = ar.at("kT3", [128, 512], BF16, 93184)
    a_bcRT = ar.at("bcRT", [128, NT], F32, 32768)
    a_st = [ar.at("stmp", [128, NT], F32, 36864),
            ar.at("stmp2", [128, NT], F32, 40960)]
    a_ob = [ar.at("ob", [128, NT], F32, 36864),
            ar.at("ob2", [128, NT], F32, 40960)]
    a_p5 = [ar.at("p5a", [128, 5 * NT], F32, 45056),
            ar.at("p5b", [128, 5 * NT], F32, 65536)]
    a_ts4 = [ar.at("ts4a", [128, 2048], BF16, 86016),
             ar.at("ts4b", [128, 2048], BF16, 90112)]
    a_pt1 = ar.at("ptmp1", [128, 512], F32, 94208)
    a_pt2 = ar.at("ptmp2", [128, 512], F32, 96256)

    b = [ps(f"pb{i}", [128, 512], F32) for i in range(7)]
    b7 = b[6]
    pt = ps("ptr", [128, 1024], BF16)

    s0 = {k: sem("s0_" + k) for k in
          ["ld", "ini", "xq0", "xq1", "xq2", "xq3", "sq", "mm", "r1", "r2",
           "row", "pb", "bc", "xn", "neg"]}
    s1 = {k: sem("s1_" + k) for k in
          ["tl0", "tl1", "ta", "ar2", "ab", "cr", "rqr", "pb", "bce", "tt",
           "ts0", "ts1", "gw0", "gw1", "uw0", "uw1", "dw0", "dw1",
           "g", "gel", "gev", "u", "uev", "dmm", "dev", "dst0", "dst1"]}
    s2 = {k: sem("s2_" + k) for k in
          ["mskd", "mska", "sq", "mm", "r1", "r2", "row", "pb", "bc",
           "xd0", "xd1", "pd0", "pd1", "xn", "tl0", "tl1",
           "kmm", "kev", "tr", "te", "kvm", "su", "sb", "iv", "pe0", "pe1"]}
    s3 = {k: sem("s3_" + k) for k in
          ["tl0", "tl1", "qmm", "qev", "kp", "sc", "me", "tr", "te", "ic",
           "oe", "kv", "su"]}
    s4 = {k: sem("s4_" + k) for k in
          ["sqa", "sqb", "mma", "mmb", "r1", "r2", "row", "pb", "bc",
           "tl0", "tl1", "pm", "ld0", "ld1", "cmb", "st0", "st1"]}

    def par(sd, base, i):
        return sd[base + str(i % 2)]

    def stats_mms(tensor, smm, ssq, x_of, sq_of):
        for f in range(NF):
            tensor.wait_ge(ssq, f + 1)
            st = (f == 0)
            sp = (f == NF - 1)
            xs = x_of(f)
            ss_ = sq_of(f)
            tensor.matmul(b[0][0:1, 0:512], onescol[:, :], xs[:, 0:512],
                          start=st, stop=sp)
            tensor.matmul(b[1][0:1, 0:512], onescol[:, :], xs[:, 512:1024],
                          start=st, stop=sp)
            tensor.matmul(b[2][0:1, 0:512], onescol[:, :], ss_[:, 0:512],
                          start=st, stop=sp)
            tensor.matmul(b[3][0:1, 0:512], onescol[:, :], ss_[:, 512:1024],
                          start=st, stop=sp).then_inc(smm, 1)

    def row_chain_dve(vector, ss, bcA_, bcB_, bcRQ_):
        R0 = rows[0:1, 0:512]
        R1 = rows[0:1, 512:1024]
        R2 = rows[0:1, 0:512]
        for half in range(2):
            sx = (b[0] if half == 0 else b[1])[0:1, 0:512]
            sxx = (b[2] if half == 0 else b[3])[0:1, 0:512]
            hs = slice(half * 512, half * 512 + 512)
            mrow = bcB_[0:1, hs]
            rln = bcA_[0:1, hs]
            vector.tensor_scalar(mrow, sx, 1.0 / 2048.0, None, ALU.mult)
            vector.tensor_tensor(R0, mrow, mrow, ALU.mult)
            vector.scalar_tensor_tensor(R1, sxx, 1.0 / 2048.0, R0,
                                        ALU.mult, ALU.subtract).then_inc(ss["r1"], 3)
            vector.wait_ge(ss["r2"], 1 + 2 * half)
            vector.reciprocal(rln, R2)
            vector.tensor_tensor(R0, R1, rln, ALU.mult)
            vector.tensor_tensor(R1, R0, rln, ALU.mult).then_inc(ss["r1"], 3)
            vector.wait_ge(ss["r2"], 2 + 2 * half)
            vector.reciprocal(bcRQ_[0:1, hs], R2)
            vector.tensor_tensor(R0, mrow, rln, ALU.mult)
            vector.tensor_copy(mrow, R0)

    def row_chain_act(scalar, ss, epscol):
        R1 = rows[0:1, 512:1024]
        R2 = rows[0:1, 0:512]
        for half in range(2):
            scalar.wait_ge(ss["r1"], 3 + 6 * half)
            scalar.activation(R2, R1, AF.Sqrt,
                              bias=cst[0:1, epscol:epscol + 1]).then_inc(ss["r2"], 1)
            scalar.wait_ge(ss["r1"], 6 + 6 * half)
            scalar.activation(R2, R1, AF.Sqrt,
                              bias=cst[0:1, 1:2]).then_inc(ss["r2"], 1)

    with nc.Block() as blk:
        @blk.sync
        def _(sync):
            for dd, ss_ in [(d_qn, qn_s), (d_nw, nw_s), (d_pnw, pnw_s),
                            (d_sd, sd_s), (d_sg, sg_s), (d_su, su_s),
                            (d_s16, s16), (d_cD, cD), (d_cPEN, cPEN),
                            (d_cT, cT), (d_cR, cR), (d_idn, idn),
                            (d_cst, cst)]:
                sync.dma_start(ss_[:, :], dd.ap()).then_inc(s0["ld"], 16)
            for f in range(NF):
                sync.dma_start(a_x16[:, f * NT:(f + 1) * NT],
                               d_x.ap()[f * 128:(f + 1) * 128, :]).then_inc(
                                   s0["xq" + str(f // 4)], 16)
            for f in range(NF):
                if f >= 2:
                    sync.wait_ge(s2["mm"], f - 1)
                sync.dma_start(a_xpr[f % 2][:, :],
                               d_xp.ap()[f * 128:(f + 1) * 128, :]).then_inc(par(s2, "xd", f), 16)

        @blk.gpsimd
        def _(gpsimd):
            gpsimd.memset(ones_t[:, :], 1.0)
            gpsimd.memset(onescol[:, :], 1.0)
            gpsimd.memset(onesrow[:, :], 1.0)
            gpsimd.memset(S[:, :], 0.0)
            gpsimd.memset(Sb[:, :], 0.0).then_inc(s0["ini"], 1)
            gpsimd.dma_start(a_wg[0][:, :],
                             d_wg.ap()[0:128, :]).then_inc(s1["gw0"], 16)
            gpsimd.dma_start(a_wu[0][:, :],
                             d_wu.ap()[0:128, :]).then_inc(s1["uw0"], 16)

        @blk.scalar
        def _(scalar):
            for f in range(NF):
                scalar.wait_ge(s0["xq" + str(f // 4)], 64)
                if f >= 2:
                    scalar.wait_ge(s0["mm"], f - 1)
                scalar.activation(a_sq[f % 2][:, :], a_x16[:, f * NT:(f + 1) * NT],
                                  AF.Square).then_inc(s0["sq"], 1)
            scalar.wait_ge(s0["ld"], 208)
            row_chain_act(scalar, s0, 0)
            for f in range(NF):
                scalar.wait_ge(par(s2, "xd", f), 16 * (f // 2 + 1))
                if f >= 2:
                    scalar.wait_ge(s2["mm"], f - 1)
                scalar.activation(a_sqp[f % 2][:, :], a_xpr[f % 2][:, :],
                                  AF.Square).then_inc(s2["sq"], 1)
            row_chain_act(scalar, s2, 0)
            scalar.wait_ge(s0["neg"], 1)
            for h in range(H):
                scalar.activation(lamRc[:, h:h + 1], cst[:, 5:6], AF.Exp,
                                  scale=negs[:, h:h + 1])
                scalar.activation(lamTc[:, h:h + 1], s16[:, h:h + 1], AF.Exp,
                                  scale=cst[:, 3:4])

        @blk.tensor
        def _(tensor):
            tensor.wait_ge(s0["ini"], 1)
            stats_mms(tensor, s0["mm"], s0["sq"],
                      lambda f: a_x16[:, f * NT:(f + 1) * NT],
                      lambda f: a_sq[f % 2])
            tensor.wait_ge(s0["row"], 1)
            tensor.matmul(b[4][:, 0:512], onesrow[:, :], a_bcA[0:1, 0:512],
                          start=True, stop=True)
            tensor.matmul(b[5][:, 0:512], onesrow[:, :], a_bcA[0:1, 512:1024],
                          start=True, stop=True).then_inc(s0["pb"], 1)
            tensor.wait_ge(s0["bc"], 1)
            tensor.matmul(b[4][:, 0:512], onesrow[:, :], a_bcB[0:1, 0:512],
                          start=True, stop=True)
            tensor.matmul(b[5][:, 0:512], onesrow[:, :], a_bcB[0:1, 512:1024],
                          start=True, stop=True).then_inc(s0["pb"], 1)
            stats_mms(tensor, s2["mm"], s2["sq"],
                      lambda f: a_xpr[f % 2], lambda f: a_sqp[f % 2])
            tensor.wait_ge(s2["row"], 1)
            for i, tile in enumerate([a_bcAp, a_bcBp]):
                if i > 0:
                    tensor.wait_ge(s2["bc"], 2 * i - 1)
                tensor.matmul(b[4][:, 0:512], onesrow[:, :], tile[0:1, 0:512],
                              start=True, stop=True).then_inc(s2["pb"], 1)
                tensor.wait_ge(s2["bc"], 2 * i)
                tensor.matmul(b[5][:, 0:512], onesrow[:, :], tile[0:1, 512:1024],
                              start=True, stop=True).then_inc(s2["pb"], 1)

        @blk.vector
        def _(vector):
            vector.wait_ge(s0["ld"], 208)
            vector.wait_ge(s0["ini"], 1)
            vector.tensor_tensor(fw_s[:, :], nw_s[:, :], pnw_s[:, :], ALU.mult)
            vector.tensor_scalar(negs[:, :], s16[:, :], -1.0, None,
                                 ALU.mult).then_inc(s0["neg"], 1)
            vector.wait_ge(s0["mm"], 16)
            row_chain_dve(vector, s0, a_bcA, a_bcB, bcRQ)
            vector.tensor_copy(absr[:, 0:1], onescol[:, :]).then_inc(s0["row"], 1)
            vector.wait_ge(s0["pb"], 1)
            vector.tensor_copy(a_bcA[:, 0:512], b[4][:, 0:512])
            vector.tensor_copy(a_bcA[:, 512:1024], b[5][:, 0:512]).then_inc(s0["bc"], 1)
            vector.wait_ge(s0["pb"], 2)
            vector.tensor_copy(a_bcB[:, 0:512], b[4][:, 0:512])
            vector.tensor_copy(a_bcB[:, 512:1024], b[5][:, 0:512]).then_inc(s0["bc"], 1)
            for f in range(NF):
                vector.tensor_tensor(a_sq[f % 2][:, :], a_x16[:, f * NT:(f + 1) * NT],
                                     a_bcA[:, :], ALU.mult).then_inc(s0["xn"], 1)
                vector.tensor_tensor(xn[:, f * NT:(f + 1) * NT],
                                     a_sq[f % 2][:, :], a_bcB[:, :],
                                     ALU.subtract).then_inc(s0["xn"], 1)
            vector.wait_ge(s2["mm"], 16)
            row_chain_dve(vector, s2, a_bcAp, a_bcBp, bcRQp)
            vector.tensor_copy(absr[:, 1:2], onescol[:, :]).then_inc(s2["row"], 1)
            for i, tile in enumerate([a_bcAp, a_bcBp]):
                vector.wait_ge(s2["pb"], 2 * i + 1)
                vector.tensor_copy(tile[:, 0:512], b[4][:, 0:512]).then_inc(s2["bc"], 1)
                vector.wait_ge(s2["pb"], 2 * i + 2)
                vector.tensor_copy(tile[:, 512:1024], b[5][:, 0:512]).then_inc(s2["bc"], 1)

    TSL = []
    for h_ in range(16):
        TSL.append(("q", 16 + h_))
        TSL.append(("q", 32 + h_))
    TSL += [("q", i) for i in range(16)] + [("p", i) for i in range(POC)]
    NSL = len(TSL)
    TPOS = {kv_: j for j, kv_ in enumerate(TSL)}

    def slab_src(j):
        kind, i = TSL[j]
        dd = d_qkv if kind == "q" else d_proj
        return dd.ap()[i * 128:(i + 1) * 128, :]

    def slab_dst(j):
        kind, i = TSL[j]
        dd = d_scq if kind == "q" else d_scp
        return dd.ap()[i * 128:(i + 1) * 128, :]

    SCHED = [[] for _ in range(129)]
    tern_slots = []
    for q_ in range(4):
        for mi_ in range(16):
            if q_ * 16 + mi_ >= 27:
                tern_slots.append(q_ * 32 + mi_)
        for o2_ in range(16):
            if q_ == 1:
                tern_slots.append(q_ * 32 + 16 + o2_)
            elif q_ >= 2:
                tern_slots += [q_ * 32 + 16 + o2_] * 2
    for i in range(16):
        SCHED[4 + i // 2].append(("xnp", i))
    kv_slot = {}
    for n in range(32):
        kv_slot[n] = tern_slots[n] + 1
        SCHED[kv_slot[n]].append(("kv", n))
    for n in range(32):
        kv_slot[32 + n] = min(94 + n, 127)
        SCHED[kv_slot[32 + n]].append(("kv", 32 + n))
    for i in range(16):
        SCHED[80 + i // 2].append(("xnp", 16 + i))
    ptr = 20
    for P in range(32):
        if P == 16:
            ptr = 96
        g_ = P // 16
        ready = kv_slot[g_ * 32 + 2 * (P % 16) + 1] + 1
        ptr = max(ptr, ready)
        SCHED[min(ptr, 128)].append(("ch", P))
        ptr += 1
    _ORD = {"xnp": 0, "ch": 1, "kv": 2}
    for sl_ in SCHED:
        sl_.sort(key=lambda a: _ORD[a[0]])

    with nc.Block() as blk:
        @blk.sync
        def _(sync):
            for j in range(NSL):
                if j < 2:
                    sync.wait_ge(s0["xn"], 32)
                if j >= 2:
                    sync.wait_ge(s1["ta"], j - 1)
                sync.dma_start(a_tb[j % 2][:, :], slab_src(j)).then_inc(par(s1, "tl", j), 16)
            def ts2_load(n):
                sync.wait_ge(par(s1, "ts", n), 16 * (n // 2 + 1))
                if n < 2:
                    sync.wait_ge(s2["xn"], 64)
                if n >= 2:
                    sync.wait_ge(s2["kmm"], 32 + n - 1)
                kind, oc = TSL[n]
                sync.dma_start(a_ts2[n % 2][:, :],
                               d_scq.ap()[oc * 128:(oc + 1) * 128, :]).then_inc(par(s2, "tl", n), 16)

            for j in range(NSL):
                sync.wait_ge(s1["ta"], min(j + 63, NSL))
                if j >= 2:
                    sync.wait_ge(s1["tt"], j - 1)
                sync.dma_start(a_tb[j % 2][:, :], slab_src(j)).then_inc(par(s1, "tl", j), 16)
            for n in range(32):
                ts2_load(n)

        @blk.gpsimd
        def _(gpsimd):
            for q in range(4):
                for mi in range(16):
                    m = q * 16 + mi
                    if m >= 1:
                        if m >= 2:
                            gpsimd.wait_ge(s1["g"], m - 1)
                        gpsimd.dma_start(a_wg[m % 2][:, :],
                                         d_wg.ap()[m * 128:(m + 1) * 128, :]).then_inc(par(s1, "gw", m), 16)
                        if m >= 2:
                            gpsimd.wait_ge(s1["u"], m - 1)
                        gpsimd.dma_start(a_wu[m % 2][:, :],
                                         d_wu.ap()[m * 128:(m + 1) * 128, :]).then_inc(par(s1, "uw", m), 16)
                    if q == 0 and 3 <= mi <= 10:
                        for i2 in (2 * (mi - 3), 2 * (mi - 3) + 1):
                            gpsimd.wait_ge(s2["mm"], 16)
                            if i2 >= 2:
                                gpsimd.wait_ge(s2["xn"], 2 * (i2 - 1))
                            gpsimd.dma_start(
                                a_xpb[i2 % 2][:, :],
                                d_xp.ap()[i2 * 128:(i2 + 1) * 128,
                                          0:512]).then_inc(par(s2, "pd", i2), 16)
                for o2 in range(16):
                    d_idx = q * 16 + o2
                    if d_idx < 2:
                        gpsimd.wait_ge(s0["xn"], 32)
                    if d_idx >= 2:
                        gpsimd.wait_ge(s1["dmm"], d_idx - 1)
                    gpsimd.dma_start(a_wd[o2 % 2][:, :],
                                     d_wd.ap()[d_idx * 128:(d_idx + 1) * 128, :]).then_inc(par(s1, "dw", d_idx), 16)


        @blk.scalar
        def _(scalar):
            n_abs = [0]
            n_st = [0]

            def abs_unit(k):
                for _ in range(k):
                    if n_abs[0] >= NSL:
                        return
                    j = n_abs[0]
                    scalar.wait_ge(par(s1, "tl", j), 16 * (j // 2 + 1))
                    kind, i = TSL[j]
                    acc = absq[:, i:i + 1] if kind == "q" else absp[:, i:i + 1]
                    scalar.activation(a_abs[:, :], a_tb[j % 2][:, :], AF.Abs,
                                      accum_out=acc).then_inc(s1["ta"], 1)
                    n_abs[0] += 1

            def store_unit(k):
                for _ in range(k):
                    if n_st[0] >= NSL:
                        return
                    j = n_st[0]
                    scalar.wait_ge(s1["tt"], j + 1)
                    scalar.dma_start(slab_dst(j), a_tern[j % 2][:, :]).then_inc(par(s1, "ts", j), 16)
                    n_st[0] += 1

            for q in range(4):
                for mi in range(16):
                    m = q * 16 + mi
                    if m >= 1:
                        scalar.wait_ge(s1["uev"], m)
                    scalar.wait_ge(s1["g"], m + 1)
                    scalar.activation(a_gs[:, 0:512], b[0][:, 0:512], AF.Silu,
                                      scale=sg_s[:, m:m + 1]).then_inc(s1["gel"], 1)
                    scalar.activation(a_gs[:, 512:1024], b[1][:, 0:512], AF.Silu,
                                      scale=sg_s[:, m:m + 1]).then_inc(s1["gev"], 1)
                    if m < 16:
                        abs_unit(3)
                    elif m >= 28:
                        store_unit(1)
                    if 28 <= m < 44:
                        for h in (m - 28,):
                            scalar.wait_ge(s2["mskd"], h + 1)
                            scalar.activation(maskT[:, h * 128:(h + 1) * 128],
                                              a_mtmp[h % 2][:, :],
                                              AF.Exp).then_inc(s2["mska"], 1)
                            scalar.activation(lamTt[:, h * 128:(h + 1) * 128],
                                              cT[:, :], AF.Exp,
                                              scale=negs[:, h:h + 1])
                            scalar.activation(lamRt[:, h * 128:(h + 1) * 128],
                                              cR[:, :], AF.Exp,
                                              scale=negs[:, h:h + 1])
                for o2 in range(16):
                    d_idx = q * 16 + o2
                    scalar.wait_ge(s1["dev"], d_idx + 1)
                    scalar.dma_start(d_scf.ap()[d_idx * 128:(d_idx + 1) * 128, :],
                                     a_dt[d_idx % 2][:, :]).then_inc(par(s1, "dst", d_idx), 16)
                    if q == 0:
                        abs_unit(1)
                    else:
                        store_unit(1)
                        if q >= 2:
                            store_unit(1)
                    if q == 2 and o2 < 8:
                        for i2 in (16 + 2 * o2, 16 + 2 * o2 + 1):
                            i = i2 - 16
                            scalar.wait_ge(s2["xn"], 2 * (i2 - 1))
                            scalar.dma_start(
                                a_xpb[i2 % 2][:, :],
                                d_xp.ap()[i * 128:(i + 1) * 128,
                                          512:1024]).then_inc(par(s2, "pe", i2), 16)
            store_unit(NSL)

        @blk.tensor
        def _(tensor):
            def t_kv(n):
                g, sl = n // 32, n % 32
                if g == 0:
                    tensor.wait_ge(s1["tt"], sl + 1)
                else:
                    tensor.wait_ge(par(s2, "tl", sl), 16 * (sl // 2 + 1))
                tensor.wait_ge(s2["xn"], 32 * (g + 1))
                if n == 0:
                    tensor.wait_ge(s2["bc"], 4)
                if n >= 1:
                    tensor.wait_ge(s2["kev"], n)
                src = a_tern[sl % 2] if g == 0 else a_ts2[sl % 2]
                for fc in range(NF):
                    mm = tensor.matmul(b[5][:, :], src[:, fc * 128:(fc + 1) * 128],
                                       a_xnp[:, fc * 512:(fc + 1) * 512],
                                       start=(fc == 0), stop=(fc == NF - 1))
                mm.then_inc(s2["kmm"], 1)

            def t_ch(P):
                tensor.wait_ge(s2["kev"], 2 * P + 2)
                if P >= 1:
                    tensor.wait_ge(s2["te"], P)
                for cl in range(4):
                    tensor.transpose(pt[:, cl * 128:(cl + 1) * 128],
                                     a_vsb[P % 2][:, cl * 128:(cl + 1) * 128],
                                     idn[:, :])
                    mm = tensor.transpose(pt[:, 512 + cl * 128:512 + (cl + 1) * 128],
                                          a_ksb[P % 2][:, cl * 128:(cl + 1) * 128],
                                          idn[:, :])
                mm.then_inc(s2["tr"], 1)
                tensor.wait_ge(s2["te"], P + 1)
                if P >= 1:
                    tensor.wait_ge(s2["su"], P)
                for cl in range(4):
                    mm = tensor.matmul(b[6][:, cl * 128:(cl + 1) * 128],
                                       a_kT[:, cl * 128:(cl + 1) * 128],
                                       a_vT[:, cl * 128:(cl + 1) * 128],
                                       start=True, stop=True)
                mm.then_inc(s2["kvm"], 1)

            def t_inj(act):
                if act[0] == "kv":
                    t_kv(act[1])
                elif act[0] == "ch":
                    t_ch(act[1])

            for q in range(4):
                for mi in range(16):
                    m = q * 16 + mi
                    tensor.wait_ge(par(s1, "gw", m), 16 * (m // 2 + 1))
                    if m == 0:
                        tensor.wait_ge(s0["xn"], 32)
                        tensor.wait_ge(s2["row"], 1)
                    if m >= 1:
                        tensor.wait_ge(s1["gev"], m)
                    for f in range(NF):
                        st = (f == 0)
                        sp = (f == NF - 1)
                        stat = a_wg[m % 2][:, f * 128:(f + 1) * 128]
                        mm = tensor.matmul(b[1][:, :], stat,
                                           xn[:, f * NT + 512:f * NT + 1024],
                                           start=st, stop=sp)
                    if m >= 1:
                        tensor.wait_ge(s1["uev"], m)
                    for f in range(NF):
                        st = (f == 0)
                        sp = (f == NF - 1)
                        stat = a_wg[m % 2][:, f * 128:(f + 1) * 128]
                        mm = tensor.matmul(b[0][:, :], stat, xn[:, f * NT:f * NT + 512],
                                           start=st, stop=sp)
                    mm.then_inc(s1["g"], 1)
                    tensor.wait_ge(par(s1, "uw", m), 16 * (m // 2 + 1))
                    if m >= 1:
                        tensor.wait_ge(s1["uev"], m)
                    for f in range(NF):
                        st = (f == 0)
                        sp = (f == NF - 1)
                        stat = a_wu[m % 2][:, f * 128:(f + 1) * 128]
                        mm = tensor.matmul(b[2][:, :], stat, xn[:, f * NT:f * NT + 512],
                                           start=st, stop=sp)
                    tensor.wait_ge(s1["gel"], m + 1)
                    for f in range(NF):
                        st = (f == 0)
                        sp = (f == NF - 1)
                        stat = a_wu[m % 2][:, f * 128:(f + 1) * 128]
                        mm = tensor.matmul(b[0][:, :], stat,
                                           xn[:, f * NT + 512:f * NT + 1024],
                                           start=st, stop=sp)
                    mm.then_inc(s1["u"], 1)
                    if m == 21:
                        tensor.wait_ge(s1["ar2"], 1)
                        tensor.matmul(b[6][0:1, 0:1], onescol[:, :],
                                      absr[:, 0:1], start=True, stop=True)
                        tensor.matmul(b[6][0:1, 1:2], onescol[:, :],
                                      absr[:, 1:2], start=True,
                                      stop=True).then_inc(s1["ab"], 1)
                    if m == 22:
                        tensor.wait_ge(s1["cr"], 1)
                        tensor.matmul(b[6][:, 0:2], onesrow[:, :],
                                      invrow[0:1, 0:2], start=True,
                                      stop=True).then_inc(s1["pb"], 1)
                    if m == 23:
                        tensor.wait_ge(s1["rqr"], 2)
                        tensor.wait_ge(s1["bce"], 1)
                        tensor.matmul(b[6][:, 0:512], onesrow[:, :],
                                      bcRQ[0:1, 0:512], start=True,
                                      stop=True).then_inc(s1["pb"], 1)
                    if m == 24:
                        tensor.wait_ge(s1["bce"], 2)
                        tensor.matmul(b[6][:, 0:512], onesrow[:, :],
                                      bcRQ[0:1, 512:1024], start=True,
                                      stop=True).then_inc(s1["pb"], 1)
                    if m == 25:
                        tensor.wait_ge(s1["bce"], 3)
                        tensor.matmul(b[6][:, 0:512], onesrow[:, :],
                                      bcRQp[0:1, 0:512], start=True,
                                      stop=True).then_inc(s2["pb"], 1)
                    if m == 26:
                        tensor.wait_ge(s2["iv"], 1)
                        tensor.matmul(b[6][:, 0:512], onesrow[:, :],
                                      bcRQp[0:1, 512:1024], start=True,
                                      stop=True).then_inc(s2["pb"], 1)
                    for act in SCHED[q * 32 + mi]:
                        t_inj(act)
                tensor.wait_ge(s1["uev"], 16 * (q + 1))
                for o2 in range(16):
                    d_idx = q * 16 + o2
                    tensor.wait_ge(par(s1, "dw", d_idx), 16 * (d_idx // 2 + 1))
                    if d_idx == 0:
                        tensor.wait_ge(s2["bc"], 4)
                    if d_idx >= 1:
                        tensor.wait_ge(s1["dev"], d_idx)
                    for mi2 in range(16):
                        st = (mi2 == 0)
                        sp = (mi2 == 15)
                        stat = a_wd[o2 % 2][:, mi2 * 128:(mi2 + 1) * 128]
                        tensor.matmul(b[3][:, :], stat,
                                      a_fuse[:, mi2 * NT:mi2 * NT + 512],
                                      start=st, stop=sp)
                        mm = tensor.matmul(b[4][:, :], stat,
                                           a_fuse[:, mi2 * NT + 512:mi2 * NT + 1024],
                                           start=st, stop=sp)
                    mm.then_inc(s1["dmm"], 1)
                    for act in SCHED[q * 32 + 16 + o2]:
                        t_inj(act)
            for act in SCHED[128]:
                t_inj(act)

        @blk.vector
        def _(vector):
            n_tern = [0]

            def tern_unit(k):
                for _ in range(k):
                    if n_tern[0] >= NSL:
                        return
                    j = n_tern[0]
                    vector.wait_ge(par(s1, "tl", j), 16 * (32 + j // 2 + 1))
                    if j >= 2:
                        vector.wait_ge(par(s1, "ts", j), 16 * (j // 2))
                        if j - 2 < 32:
                            vector.wait_ge(s2["kmm"], j - 1)
                    kind, i = TSL[j]
                    col = 0 if kind == "q" else 1
                    vector.tensor_scalar(a_abs[:, :], a_tb[j % 2][:, :],
                                         negc_qp[:, col:col + 1], None, ALU.is_lt)
                    vector.scalar_tensor_tensor(a_tb[j % 2][:, :], a_tb[j % 2][:, :],
                                                c_qp[:, col:col + 1], a_abs[:, :],
                                                ALU.is_gt, ALU.subtract)
                    if kind == "q":
                        vector.tensor_tensor(a_tern[j % 2][:, :], a_tb[j % 2][:, :],
                                             a_patq[:, :], ALU.mult).then_inc(s1["tt"], 1)
                    else:
                        for fc in range(NF):
                            tt_ = vector.tensor_scalar(
                                a_tern[j % 2][:, fc * 128:(fc + 1) * 128],
                                a_tb[j % 2][:, fc * 128:(fc + 1) * 128],
                                fw_s[:, fc:fc + 1], None, ALU.mult)
                        tt_.then_inc(s1["tt"], 1)
                    n_tern[0] += 1

            def v_kv(n):
                g, sl = n // 32, n % 32
                P = g * 16 + sl // 2
                vector.wait_ge(s2["kmm"], n + 1)
                if n == 0:
                    vector.wait_ge(s2["iv"], 2)
                if sl % 2 == 0 and P >= 2:
                    vector.wait_ge(s2["tr"], P - 1)
                dst = a_ksb[P % 2] if sl % 2 == 0 else a_vsb[P % 2]
                vector.tensor_tensor(dst[:, :], b[5][:, :],
                                     bcRQp[:, g * 512:g * 512 + 512],
                                     ALU.mult).then_inc(s2["kev"], 1)

            def v_ch(P):
                h = P % 16
                vector.wait_ge(s2["tr"], P + 1)
                vector.tensor_copy(a_vT[:, :], pt[:, 0:512])
                vector.tensor_scalar(a_kT[:, :], pt[:, 512:1024], lamRc[:, h:h + 1],
                                     None, ALU.mult).then_inc(s2["te"], 1)
                vector.wait_ge(s2["kvm"], P + 1)
                for cl in range(4):
                    tt_ = vector.scalar_tensor_tensor(
                        S[:, h * 128:(h + 1) * 128],
                        S[:, h * 128:(h + 1) * 128],
                        lamTc[:, h:h + 1], b[6][:, cl * 128:(cl + 1) * 128],
                        ALU.mult, ALU.add)
                tt_.then_inc(s2["su"], 1)
                if P >= 16:
                    vector.tensor_copy(Sb[:, h * 128:(h + 1) * 128],
                                       S[:, h * 128:(h + 1) * 128]).then_inc(s2["sb"], 1)

            def v_xnp(i2):
                g, i = i2 // 16, i2 % 16
                if i2 < 16:
                    vector.wait_ge(par(s2, "pd", i2), 16 * (i2 // 2 + 1))
                else:
                    vector.wait_ge(par(s2, "pe", i2), 16 * ((i2 - 16) // 2 + 1))
                if i2 == 16:
                    vector.wait_ge(s2["kmm"], 32)
                hs = slice(g * 512, g * 512 + 512)
                vector.tensor_tensor(a_xpb[i2 % 2][:, :], a_xpb[i2 % 2][:, :],
                                     a_bcAp[:, hs], ALU.mult)
                vector.tensor_tensor(a_xnp[:, i * 512:(i + 1) * 512],
                                     a_xpb[i2 % 2][:, :], a_bcBp[:, hs],
                                     ALU.subtract).then_inc(s2["xn"], 2)

            def v_inj(act):
                if act[0] == "kv":
                    v_kv(act[1])
                elif act[0] == "ch":
                    v_ch(act[1])
                else:
                    v_xnp(act[1])

            for fc in range(16):
                vector.tensor_scalar(a_patq[:, fc * 128:(fc + 1) * 128], ones_t[:, :],
                                     qn_s[:, fc:fc + 1], None, ALU.mult)

            for q in range(4):
                for mi in range(16):
                    m = q * 16 + mi
                    vector.wait_ge(s1["u"], m + 1)
                    vector.wait_ge(s1["gev"], m + 1)
                    if mi == 0 and q >= 1:
                        vector.wait_ge(s1["dmm"], 16 * q)
                    vector.scalar_tensor_tensor(a_fuse[:, mi * NT:mi * NT + 512],
                                                b[2][:, :], su_s[:, m:m + 1],
                                                a_gs[:, 0:512], ALU.mult, ALU.mult)
                    vector.scalar_tensor_tensor(a_fuse[:, mi * NT + 512:(mi + 1) * NT],
                                                b[0][:, :], su_s[:, m:m + 1],
                                                a_gs[:, 512:1024], ALU.mult,
                                                ALU.mult).then_inc(s1["uev"], 1)
                    if m == 20:
                        vector.wait_ge(s1["ta"], NSL)
                        vector.reduce_sum(absr[:, 0:1], absq[:, :], axis=AX.X)
                        vector.reduce_sum(absr[:, 1:2], absp[:, :],
                                          axis=AX.X).then_inc(s1["ar2"], 1)
                    if m == 21:
                        vector.wait_ge(s1["ab"], 1)
                        vector.tensor_scalar(invrow[0:1, 0:1], b[6][0:1, 0:1],
                                             1.0 / QKV_N, 1e-5, ALU.mult, ALU.max)
                        vector.tensor_scalar(invrow[0:1, 1:2], b[6][0:1, 1:2],
                                             1.0 / PROJ_N, 1e-5, ALU.mult,
                                             ALU.max).then_inc(s1["cr"], 1)
                        vector.tensor_scalar(bcRQ[0:1, :], bcRQ[0:1, :],
                                             invrow[0:1, 0:1], None,
                                             ALU.mult).then_inc(s1["rqr"], 1)
                        vector.tensor_scalar(bcRQp[0:1, :], bcRQp[0:1, :],
                                             invrow[0:1, 0:1], None,
                                             ALU.mult).then_inc(s1["rqr"], 1)
                    if m == 22:
                        vector.wait_ge(s1["pb"], 1)
                        vector.tensor_scalar(c_qp[:, :], b[6][:, 0:2], 0.5, None,
                                             ALU.mult)
                        vector.tensor_scalar(negc_qp[:, :], c_qp[:, :], -1.0,
                                             None, ALU.mult).then_inc(s1["bce"], 1)
                    if m == 23:
                        vector.wait_ge(s1["pb"], 2)
                        vector.tensor_copy(bcRQ[:, 0:512],
                                           b[6][:, 0:512]).then_inc(s1["bce"], 1)
                    if m == 24:
                        vector.wait_ge(s1["pb"], 3)
                        vector.tensor_copy(bcRQ[:, 512:1024],
                                           b[6][:, 0:512]).then_inc(s1["bce"], 1)
                    if m == 25:
                        vector.wait_ge(s2["pb"], 5)
                        vector.tensor_copy(bcRQp[:, 0:512],
                                           b[6][:, 0:512]).then_inc(s2["iv"], 1)
                    if m == 26:
                        vector.wait_ge(s2["pb"], 6)
                        vector.tensor_copy(bcRQp[:, 512:1024],
                                           b[6][:, 0:512]).then_inc(s2["iv"], 1)
                    if m >= 27:
                        tern_unit(1)
                    if 27 <= m < 43:
                        for h in (m - 27,):
                            if h >= 2:
                                vector.wait_ge(s2["mska"], h - 1)
                            mt = a_mtmp[h % 2]
                            vector.tensor_scalar(mt[:, :], cD[:, :],
                                                 negs[:, h:h + 1], None, ALU.mult)
                            vector.tensor_tensor(mt[:, :], mt[:, :],
                                                 cPEN[:, :],
                                                 ALU.add).then_inc(s2["mskd"], 1)
                    for act in SCHED[q * 32 + mi]:
                        v_inj(act)
                for o2 in range(16):
                    d_idx = q * 16 + o2
                    vector.wait_ge(s1["dmm"], d_idx + 1)
                    if d_idx >= 2:
                        vector.wait_ge(par(s1, "dst", d_idx), 16 * (d_idx // 2))
                    vector.tensor_copy(a_dt[d_idx % 2][:, 0:512], b[3][:, :])
                    vector.tensor_copy(a_dt[d_idx % 2][:, 512:1024],
                                       b[4][:, :]).then_inc(s1["dev"], 1)
                    if q >= 1:
                        tern_unit(1)
                        if q >= 2:
                            tern_unit(1)
                    for act in SCHED[q * 32 + 16 + o2]:
                        v_inj(act)
            tern_unit(NSL)
            for act in SCHED[128]:
                v_inj(act)

    with nc.Block() as blk:
        @blk.sync
        def _(sync):
            for g in range(2):
                for oc in range(QOC):
                    jj = g * QOC + oc
                    if jj < 2:
                        sync.wait_ge(s1["g"], 64)
                    if jj >= 2:
                        sync.wait_ge(s3["qmm"], jj - 1)
                    sync.dma_start(a_ts3[jj % 2][:, :],
                                   d_scq.ap()[oc * 128:(oc + 1) * 128, :]).then_inc(par(s3, "tl", jj), 16)

        @blk.tensor
        def _(tensor):
            for g in range(2):
                for oc in range(QOC):
                    jj = g * QOC + oc
                    tensor.wait_ge(par(s3, "tl", jj), 16 * (jj // 2 + 1))
                    if jj >= 2:
                        tensor.wait_ge(s3["qev"], jj - 1)
                    bank = b[0] if jj % 2 == 0 else b[1]
                    for fc in range(NF):
                        mm = tensor.matmul(bank[:, :],
                                           a_ts3[jj % 2][:, fc * 128:(fc + 1) * 128],
                                           xn[:, fc * NT + g * 512:fc * NT + g * 512 + 512],
                                           start=(fc == 0), stop=(fc == NF - 1))
                    mm.then_inc(s3["qmm"], 1)
                tensor.wait_ge(s3["qev"], (g + 1) * QOC)
                for cl in range(4):
                    for hb in range(4):
                        bi = (g * 4 + cl) * 4 + hb
                        hs = [4 * hb + i_ for i_ in range(4)]
                        if bi >= 1:
                            tensor.wait_ge(s3["me"], bi)
                        for i, h in enumerate(hs):
                            ksl = a_qkv[:, (16 + h) * 512 + cl * 128:(16 + h) * 512 + cl * 128 + 128]
                            qsl = a_qkv[:, h * 512 + cl * 128:h * 512 + cl * 128 + 128]
                            mm = tensor.matmul(b[2][:, i * 128:(i + 1) * 128],
                                               ksl, qsl, start=True, stop=True)
                        mm.then_inc(s3["sc"], 1)
                        tensor.wait_ge(s3["kp"], bi + 1)
                        if bi >= 1:
                            tensor.wait_ge(s3["te"], bi)
                        for i, h in enumerate(hs):
                            vsl = a_qkv[:, (32 + h) * 512 + cl * 128:(32 + h) * 512 + cl * 128 + 128]
                            ksl = a_qkv[:, (16 + h) * 512 + cl * 128:(16 + h) * 512 + cl * 128 + 128]
                            tensor.transpose(pt[:, i * 128:(i + 1) * 128], vsl,
                                             idn[:, :])
                            mm = tensor.transpose(pt[:, 512 + i * 128:512 + (i + 1) * 128],
                                                  ksl, idn[:, :])
                        mm.then_inc(s3["tr"], 1)
                        tensor.wait_ge(s3["me"], bi + 1)
                        tensor.wait_ge(s3["te"], bi + 1)
                        if g == 0:
                            tensor.wait_ge(s2["sb"], 4 * hb + 4)
                        if bi >= 1:
                            tensor.wait_ge(s3["oe"], bi)
                        for i, h in enumerate(hs):
                            tensor.matmul(b[4][:, i * 128:(i + 1) * 128],
                                          a_vT3[:, i * 128:(i + 1) * 128],
                                          a_sm[:, i * 128:(i + 1) * 128],
                                          start=True, stop=False)
                            mm = tensor.matmul(b[4][:, i * 128:(i + 1) * 128],
                                               Sb[:, h * 128:(h + 1) * 128],
                                               a_qt[:, i * 128:(i + 1) * 128],
                                               start=False, stop=True)
                        mm.then_inc(s3["ic"], 1)
                        if bi >= 1:
                            tensor.wait_ge(s3["su"], bi)
                        for i in range(4):
                            mm = tensor.matmul(b[3][:, i * 128:(i + 1) * 128],
                                               a_kT3[:, i * 128:(i + 1) * 128],
                                               a_vT3[:, i * 128:(i + 1) * 128],
                                               start=True, stop=True)
                        mm.then_inc(s3["kv"], 1)

        @blk.vector
        def _(vector):
            for g in range(2):
                for oc in range(QOC):
                    jj = g * QOC + oc
                    vector.wait_ge(s3["qmm"], jj + 1)
                    if g == 1 and oc == 0:
                        vector.wait_ge(s3["kv"], 16)
                    bank = b[0] if jj % 2 == 0 else b[1]
                    vector.tensor_tensor(a_qkv[:, oc * 512:(oc + 1) * 512],
                                         bank[:, :],
                                         bcRQ[:, g * 512:g * 512 + 512],
                                         ALU.mult).then_inc(s3["qev"], 1)
                for cl in range(4):
                    for hb in range(4):
                        bi = (g * 4 + cl) * 4 + hb
                        hs = [4 * hb + i_ for i_ in range(4)]
                        for i, h in enumerate(hs):
                            qsl = a_qkv[:, h * 512 + cl * 128:h * 512 + cl * 128 + 128]
                            tt_ = vector.tensor_tensor(a_qt[:, i * 128:(i + 1) * 128],
                                                       qsl,
                                                       lamTt[:, h * 128:(h + 1) * 128],
                                                       ALU.mult)
                        tt_.then_inc(s3["kp"], 1)
                        vector.wait_ge(s3["sc"], bi + 1)
                        for i, h in enumerate(hs):
                            tt_ = vector.tensor_tensor(a_sm[:, i * 128:(i + 1) * 128],
                                                       b[2][:, i * 128:(i + 1) * 128],
                                                       maskT[:, h * 128:(h + 1) * 128],
                                                       ALU.mult)
                        tt_.then_inc(s3["me"], 1)
                        vector.wait_ge(s3["tr"], bi + 1)
                        vector.tensor_copy(a_vT3[:, :], pt[:, 0:512])
                        for i, h in enumerate(hs):
                            tt_ = vector.tensor_scalar(a_kT3[:, i * 128:(i + 1) * 128],
                                                       pt[:, 512 + i * 128:512 + (i + 1) * 128],
                                                       lamRc[:, h:h + 1], None,
                                                       ALU.mult)
                        tt_.then_inc(s3["te"], 1)
                        vector.wait_ge(s3["kv"], bi + 1)
                        for i, h in enumerate(hs):
                            vector.scalar_tensor_tensor(S[:, h * 128:(h + 1) * 128],
                                                        S[:, h * 128:(h + 1) * 128],
                                                        lamTc[:, h:h + 1],
                                                        b[3][:, i * 128:(i + 1) * 128],
                                                        ALU.mult, ALU.add)
                        vector.tensor_copy(Sb[:, 4 * hb * 128:(4 * hb + 4) * 128],
                                           S[:, 4 * hb * 128:(4 * hb + 4) * 128]).then_inc(s3["su"], 1)

        @blk.scalar
        def _(scalar):
            for g in range(2):
                for cl in range(4):
                    for hb in range(4):
                        bi = (g * 4 + cl) * 4 + hb
                        scalar.wait_ge(s3["ic"], bi + 1)
                        for i, h in enumerate(4 * hb + i_ for i_ in range(4)):
                            dst = a_o16[:, h * NT + g * 512 + cl * 128:
                                        h * NT + g * 512 + cl * 128 + 128]
                            aa = scalar.activation(dst, b[4][:, i * 128:(i + 1) * 128],
                                                   AF.Copy, scale=cst[:, 4:5])
                        aa.then_inc(s3["oe"], 1)

    with nc.Block() as blk:
        @blk.scalar
        def _(scalar):
            for f in range(NF):
                if f >= 1:
                    scalar.wait_ge(s4["mma"], f)
                scalar.activation(a_st[0][:, :], a_o16[:, f * NT:(f + 1) * NT],
                                  AF.Square).then_inc(s4["sqa"], 1)
                if f >= 1:
                    scalar.wait_ge(s4["mmb"], f)
                scalar.activation(a_st[1][:, :], a_o16[:, f * NT:(f + 1) * NT],
                                  AF.Square,
                                  scale=nw_s[:, f:f + 1]).then_inc(s4["sqb"], 1)
            R1 = rows[0:1, 512:1024]
            R2 = rows[0:1, 0:512]
            for half in range(2):
                scalar.wait_ge(s4["r1"], 1 + 4 * half)
                scalar.activation(R2, R1, AF.Sqrt,
                                  bias=cst[0:1, 2:3]).then_inc(s4["r2"], 1)
                scalar.wait_ge(s4["r1"], 4 + 4 * half)
                scalar.activation(R2, R1, AF.Sqrt,
                                  bias=cst[0:1, 1:2]).then_inc(s4["r2"], 1)

        @blk.tensor
        def _(tensor):
            for f in range(NF):
                tensor.wait_ge(s4["sqa"], f + 1)
                st = (f == 0)
                sp = (f == NF - 1)
                tensor.matmul(b[0][0:1, 0:512], onescol[:, :], a_st[0][:, 0:512],
                              start=st, stop=sp)
                tensor.matmul(b[1][0:1, 0:512], onescol[:, :], a_st[0][:, 512:1024],
                              start=st, stop=sp).then_inc(s4["mma"], 1)
                tensor.wait_ge(s4["sqb"], f + 1)
                tensor.matmul(b[2][0:1, 0:512], onescol[:, :], a_st[1][:, 0:512],
                              start=st, stop=sp)
                tensor.matmul(b[3][0:1, 0:512], onescol[:, :], a_st[1][:, 512:1024],
                              start=st, stop=sp).then_inc(s4["mmb"], 1)
            tensor.wait_ge(s4["row"], 1)
            tensor.matmul(b[4][:, 0:512], onesrow[:, :], a_bcRT[0:1, 0:512],
                          start=True, stop=True).then_inc(s4["pb"], 1)
            tensor.wait_ge(s4["bc"], 1)
            tensor.matmul(b[5][:, 0:512], onesrow[:, :], a_bcRT[0:1, 512:1024],
                          start=True, stop=True).then_inc(s4["pb"], 1)
            tensor.wait_ge(s4["bc"], 2)
            for o2 in range(POC):
                tensor.wait_ge(par(s4, "tl", o2), 16 * (o2 // 2 + 1))
                if o2 >= 2:
                    tensor.wait_ge(s4["cmb"], o2 - 1)
                blo = b[0] if o2 % 2 == 0 else b[2]
                bhi = b[1] if o2 % 2 == 0 else b[3]
                for fc in range(NF):
                    st = (fc == 0)
                    sp = (fc == NF - 1)
                    stat = a_ts4[o2 % 2][:, fc * 128:(fc + 1) * 128]
                    tensor.matmul(blo[:, :], stat,
                                  a_o16[:, fc * NT:fc * NT + 512], start=st, stop=sp)
                    mm = tensor.matmul(bhi[:, :], stat,
                                       a_o16[:, fc * NT + 512:(fc + 1) * NT],
                                       start=st, stop=sp)
                mm.then_inc(s4["pm"], 1)

        @blk.vector
        def _(vector):
            vector.wait_ge(s4["mma"], 16)
            vector.wait_ge(s4["mmb"], 16)
            R0 = rows[0:1, 0:512]
            R1 = rows[0:1, 512:1024]
            R2 = rows[0:1, 0:512]
            for half in range(2):
                sA = (b[0] if half == 0 else b[1])[0:1, 0:512]
                sB = (b[2] if half == 0 else b[3])[0:1, 0:512]
                hs = slice(half * 512, half * 512 + 512)
                rrow = a_bcRT[0:1, hs]
                vector.tensor_scalar(R1, sA, 1.0 / 2048.0, None,
                                     ALU.mult).then_inc(s4["r1"], 1)
                vector.wait_ge(s4["r2"], 1 + 2 * half)
                vector.reciprocal(rrow, R2)
                vector.tensor_tensor(R0, rrow, rrow, ALU.mult)
                vector.scalar_tensor_tensor(R1, sB, 1.0 / 2048.0, R0,
                                            ALU.mult, ALU.mult).then_inc(s4["r1"], 3)
                vector.wait_ge(s4["r2"], 2 + 2 * half)
                vector.reciprocal(R0, R2)
                vector.tensor_tensor(R1, rrow, R0, ALU.mult)
                vector.tensor_scalar(rrow, R1, invrow[0:1, 1:2], None, ALU.mult)
            vector.tensor_copy(absr[:, 1:2], onescol[:, :]).then_inc(s4["row"], 1)
            vector.wait_ge(s4["pb"], 1)
            vector.tensor_copy(a_bcRT[:, 0:512], b[4][:, 0:512]).then_inc(s4["bc"], 1)
            vector.wait_ge(s4["pb"], 2)
            vector.tensor_copy(a_bcRT[:, 512:1024], b[5][:, 0:512]).then_inc(s4["bc"], 1)
            for o2 in range(POC):
                vector.wait_ge(s4["pm"], o2 + 1)
                vector.wait_ge(par(s4, "ld", o2), 80 * (o2 // 2 + 1))
                if o2 >= 2:
                    vector.wait_ge(par(s4, "st", o2), 16 * (o2 // 2))
                pb5 = a_p5[o2 % 2]
                blo = b[0] if o2 % 2 == 0 else b[2]
                bhi = b[1] if o2 % 2 == 0 else b[3]
                for half in range(2):
                    bank = blo if half == 0 else bhi
                    hs = slice(half * 512, half * 512 + 512)
                    vector.tensor_tensor(
                        a_pt1[:, :], pb5[:, NT + half * 512:NT + half * 512 + 512],
                        pb5[:, 2 * NT + half * 512:2 * NT + half * 512 + 512], ALU.add)
                    vector.tensor_tensor(
                        a_pt2[:, :], pb5[:, 3 * NT + half * 512:3 * NT + half * 512 + 512],
                        pb5[:, 4 * NT + half * 512:4 * NT + half * 512 + 512], ALU.add)
                    vector.tensor_tensor(a_pt1[:, :], a_pt1[:, :], a_pt2[:, :],
                                         ALU.add)
                    vector.tensor_tensor(a_pt2[:, :], bank[:, :], a_bcRT[:, hs],
                                         ALU.mult)
                    vector.scalar_tensor_tensor(a_pt1[:, :], a_pt1[:, :],
                                                sd_s[:, o2:o2 + 1], a_pt2[:, :],
                                                ALU.mult, ALU.add)
                    mm = vector.tensor_tensor(a_ob[o2 % 2][:, hs], a_pt1[:, :],
                                              pb5[:, half * 512:half * 512 + 512],
                                              ALU.add)
                mm.then_inc(s4["cmb"], 1)

        @blk.gpsimd
        def _(gpsimd):
            gpsimd.wait_ge(s3["su"], 32)
            for o2 in range(1, POC, 2):
                if o2 >= 3:
                    gpsimd.wait_ge(s4["cmb"], o2 - 1)
                pb5 = a_p5[o2 % 2]
                gpsimd.dma_start(pb5[:, 0:NT],
                                 d_x.ap()[o2 * 128:(o2 + 1) * 128, :]).then_inc(par(s4, "ld", o2), 16)
                for q in range(4):
                    gpsimd.dma_start(pb5[:, (q + 1) * NT:(q + 2) * NT],
                                     d_scf.ap()[q * 2048 + o2 * 128:
                                                q * 2048 + (o2 + 1) * 128, :]).then_inc(par(s4, "ld", o2), 16)

        @blk.sync
        def _(sync):
            sync.wait_ge(s1["dst0"], 16 * 32)
            sync.wait_ge(s1["dst1"], 16 * 32)
            for o2 in range(POC + 2):
                if o2 < POC:
                    if o2 % 2 == 0:
                        if o2 < 2:
                            sync.wait_ge(s3["su"], 32)
                        if o2 >= 2:
                            sync.wait_ge(s4["cmb"], o2 - 1)
                        pb5 = a_p5[o2 % 2]
                        sync.dma_start(pb5[:, 0:NT],
                                       d_x.ap()[o2 * 128:(o2 + 1) * 128, :]).then_inc(par(s4, "ld", o2), 16)
                        for q in range(4):
                            sync.dma_start(pb5[:, (q + 1) * NT:(q + 2) * NT],
                                           d_scf.ap()[q * 2048 + o2 * 128:
                                                      q * 2048 + (o2 + 1) * 128, :]).then_inc(par(s4, "ld", o2), 16)
                    if o2 < 2:
                        sync.wait_ge(s3["qmm"], 96)
                    if o2 >= 2:
                        sync.wait_ge(s4["pm"], o2 - 1)
                    pos = TPOS[("p", o2)]
                    sync.wait_ge(par(s1, "ts", pos), 16 * (pos // 2 + 1))
                    sync.dma_start(a_ts4[o2 % 2][:, :],
                                   d_scp.ap()[o2 * 128:(o2 + 1) * 128, :]).then_inc(par(s4, "tl", o2), 16)
                if o2 >= 2:
                    j = o2 - 2
                    sync.wait_ge(s4["cmb"], j + 1)
                    sync.dma_start(d_out.ap()[j * 128:(j + 1) * 128, :],
                                   a_ob[j % 2][:, :]).then_inc(par(s4, "st", j), 16)
            sync.wait_ge(s4["st0"], 16 * 8)
            sync.wait_ge(s4["st1"], 16 * 8)

    ctx.pop_all()
    nc.finalize()
    return nc


_PROG = None


def _host_inputs(L0_attn_x, L0_attn_qkv_w, L0_attn_qkv_norm_w, L0_attn_proj_w,
                 L0_attn_proj_norm_w, L0_attn_norm_w, L0_attn_s, L0_ffn_w_gate,
                 L0_ffn_w_up, L0_ffn_w_down, L0_ffn_s_gate, L0_ffn_s_up,
                 L0_ffn_s_down):
    f32 = np.float32
    x = np.asarray(L0_attn_x, f32)
    qkv_w = np.asarray(L0_attn_qkv_w, f32)
    proj_w = np.asarray(L0_attn_proj_w, f32)
    wg = np.asarray(L0_ffn_w_gate, np.int8)
    wu = np.asarray(L0_ffn_w_up, np.int8)
    wd = np.asarray(L0_ffn_w_down, np.int8)

    qkvt = np.ascontiguousarray(
        qkv_w.reshape(48, 128, 16, 128).transpose(0, 3, 2, 1).reshape(48 * 128, 2048))
    projt = np.ascontiguousarray(
        proj_w.reshape(16, 128, 16, 128).transpose(0, 3, 2, 1).reshape(16 * 128, 2048))
    wgr = np.ascontiguousarray(
        wg.reshape(16, 128, 64, 128).transpose(2, 1, 0, 3).reshape(64 * 128, 2048))
    wur = np.ascontiguousarray(
        wu.reshape(16, 128, 64, 128).transpose(2, 1, 0, 3).reshape(64 * 128, 2048))
    wdr = np.ascontiguousarray(
        wd.reshape(4, 16, 128, 16, 128).transpose(0, 3, 2, 1, 4).reshape(64 * 128, 2048))

    col16 = lambda v: np.ascontiguousarray(np.asarray(v, f32).reshape(16, 128).T)
    qn = col16(L0_attn_qkv_norm_w)
    nw = col16(L0_attn_norm_w)
    pnw = col16(L0_attn_proj_norm_w)
    sd = np.ascontiguousarray(np.asarray(L0_ffn_s_down, f32).reshape(16, 128).T)
    sg = np.ascontiguousarray(np.asarray(L0_ffn_s_gate, f32).reshape(64, 128).T)
    su = np.ascontiguousarray(np.asarray(L0_ffn_s_up, f32).reshape(64, 128).T)
    s16 = np.ascontiguousarray(np.tile(np.asarray(L0_attn_s, f32), (128, 1)))

    ar = np.arange(128, dtype=f32)
    cDm = np.ascontiguousarray(ar[None, :] - ar[:, None])
    cPENm = np.ascontiguousarray(np.where(cDm >= 0, 0.0, -1e30).astype(f32))
    cTm = np.ascontiguousarray(np.broadcast_to(ar, (128, 128)))
    cRm = np.ascontiguousarray((128.0 - cTm).astype(f32))
    idn = np.eye(128, dtype=f32).astype(ml_dtypes.bfloat16)
    cstm = np.zeros((128, 8), f32)
    cstm[:, 0] = 1e-5
    cstm[:, 1] = EPS32
    cstm[:, 2] = 1e-6
    cstm[:, 3] = -128.0
    cstm[:, 4] = SCALE
    cstm[:, 5] = 128.0 - ar

    shared = dict(qkvt=qkvt, projt=projt, wgr=wgr, wur=wur, wdr=wdr, qn=qn,
                  nw=nw, pnw=pnw, sdv=sd, sgv=sg, suv=su, s16=s16, cD=cDm,
                  cPEN=cPENm, cT=cTm, cR=cRm, idn=idn, cstv=cstm)
    zeros_prev = np.zeros((2048, 1024), f32)
    in_maps = []
    for c in range(NCORE):
        bb, half = c // 2, c % 2
        xT = np.ascontiguousarray(x[bb, half * 1024:(half + 1) * 1024, :].T)
        if half == 1:
            xpT = np.ascontiguousarray(x[bb, 0:1024, :].T)
        else:
            xpT = zeros_prev
        m = dict(shared)
        m["xT"] = xT
        m["xprevT"] = xpT
        in_maps.append(m)
    return in_maps


def kernel(**inputs):
    global _PROG
    in_maps = _host_inputs(**inputs)
    if _PROG is None:
        _PROG = build_program()
    res = run_bass_kernel_spmd(_PROG, in_maps, core_ids=list(range(NCORE)))
    out = np.empty((4, 2048, 2048), np.float32)
    for c in range(NCORE):
        bb, half = c // 2, c % 2
        out[bb, half * 1024:(half + 1) * 1024, :] = res.results[c]["outT"].T
    return out


# version 12
# speedup vs baseline: 1.0491x; 1.0491x over previous
import numpy as np
import ml_dtypes
import concourse.bass as bass
import concourse.mybir as mybir
from concourse.bass_utils import run_bass_kernel_spmd

F32 = mybir.dt.float32
BF16 = mybir.dt.bfloat16
I8 = mybir.dt.int8
U8 = mybir.dt.uint8
AF = mybir.ActivationFunctionType
ALU = mybir.AluOpType
AX = mybir.AxisListType

NCORE = 8
D = 2048
NT = 1024
H = 16
NF = 16
QOC = 48
POC = 16
EPS32 = float(np.finfo(np.float32).eps)
SCALE = float(128.0 ** -0.5)
QKV_N = float(6144 * 2048)
PROJ_N = float(2048 * 2048)


class Arena:
    def __init__(self, nc, backing):
        self.nc = nc
        self.base = nc.lookup_mloc(backing).addr
        self.cap = backing.shape[1]

    def at(self, name, shape, dtype, off):
        per = 1
        for s in shape[1:]:
            per *= s
        per *= mybir.dt.size(dtype)
        assert off % 32 == 0 and off + per <= self.cap, (name, off, per, self.cap)
        return self.nc.alloc_sbuf_tensor_at(name, shape, dtype,
                                            offset=self.base + off)


def build_program():
    nc = bass.Bass("TRN2", target_bir_lowering=False,
                   detect_race_conditions=False)

    d_x = nc.dram_tensor("xT", [D, NT], F32, kind="ExternalInput")
    d_xp = nc.dram_tensor("xprevT", [D, NT], F32, kind="ExternalInput")
    d_qkv = nc.dram_tensor("qkvt", [QOC * 128, 2048], F32, kind="ExternalInput")
    d_proj = nc.dram_tensor("projt", [POC * 128, 2048], F32, kind="ExternalInput")
    d_wg = nc.dram_tensor("wgr", [64 * 128, 2048], I8, kind="ExternalInput")
    d_wu = nc.dram_tensor("wur", [64 * 128, 2048], I8, kind="ExternalInput")
    d_wd = nc.dram_tensor("wdr", [64 * 128, 2048], I8, kind="ExternalInput")
    d_qn = nc.dram_tensor("qn", [128, 16], F32, kind="ExternalInput")
    d_nw = nc.dram_tensor("nw", [128, 16], F32, kind="ExternalInput")
    d_pnw = nc.dram_tensor("pnw", [128, 16], F32, kind="ExternalInput")
    d_sd = nc.dram_tensor("sdv", [128, 16], F32, kind="ExternalInput")
    d_sg = nc.dram_tensor("sgv", [128, 64], F32, kind="ExternalInput")
    d_su = nc.dram_tensor("suv", [128, 64], F32, kind="ExternalInput")
    d_s16 = nc.dram_tensor("s16", [128, 16], F32, kind="ExternalInput")
    d_cD = nc.dram_tensor("cD", [128, 128], F32, kind="ExternalInput")
    d_cPEN = nc.dram_tensor("cPEN", [128, 128], F32, kind="ExternalInput")
    d_cT = nc.dram_tensor("cT", [128, 128], F32, kind="ExternalInput")
    d_cR = nc.dram_tensor("cR", [128, 128], F32, kind="ExternalInput")
    d_idn = nc.dram_tensor("idn", [128, 128], BF16, kind="ExternalInput")
    d_cst = nc.dram_tensor("cstv", [128, 8], F32, kind="ExternalInput")
    d_out = nc.dram_tensor("outT", [D, NT], F32, kind="ExternalOutput")
    d_scq = nc.dram_tensor("scr_q", [QOC * 128, 2048], BF16)
    d_scp = nc.dram_tensor("scr_p", [POC * 128, 2048], BF16)
    d_scf = nc.dram_tensor("scr_f", [4 * 2048, NT], F32)

    from contextlib import ExitStack
    ctx = ExitStack()
    sb = lambda name, shape, dt: ctx.enter_context(nc.sbuf_tensor(name, shape, dt))
    ps = lambda name, shape, dt: ctx.enter_context(nc.psum_tensor(name, shape, dt))
    sem = lambda name: ctx.enter_context(nc.semaphore(name))

    xn = sb("xn", [128, NF * NT], BF16)
    bcRQ = sb("bcRQ", [128, NT], F32)
    S = sb("S", [128, H * 128], F32)
    Sb = sb("Sbuf16", [128, H * 128], BF16)
    maskT = sb("maskT", [128, H * 128], BF16)
    lamTt = sb("lamTt", [128, H * 128], BF16)
    lamRt = sb("lamRt", [128, H * 128], BF16)
    lamTc = sb("lamTc", [128, H], F32)
    negs = sb("negsv", [128, H], F32)
    s16 = sb("s16v", [128, H], F32)
    qn_s = sb("qn_s", [128, 16], F32)
    nw_s = sb("nw_s", [128, 16], F32)
    pnw_s = sb("pnw_s", [128, 16], F32)
    fw_s = sb("fw_s", [128, 16], F32)
    sd_s = sb("sd_s", [128, 16], F32)
    sg_s = sb("sg_s", [128, 64], F32)
    su_s = sb("su_s", [128, 64], F32)
    cD = sb("cDs", [128, 128], F32)
    cPEN = sb("cPENs", [128, 128], F32)
    cT = sb("cTs", [128, 128], F32)
    cR = sb("cRs", [128, 128], F32)
    idn = sb("idns", [128, 128], BF16)
    ones_t = sb("ones_t", [128, 128], F32)
    onescol = sb("onescol", [128, 1], F32)
    onesrow = sb("onesrow", [1, 128], F32)
    cst = sb("cstt", [128, 8], F32)
    lamRc = sb("lamRc", [128, H], F32)
    rows = sb("rowscr", [1, 1024], F32)
    c_qp = sb("c_qp", [128, 2], F32)
    negc_qp = sb("negc_qp", [128, 2], F32)
    absq = sb("absq", [128, 48], F32)
    absp = sb("absp", [128, 16], F32)
    absr = sb("absr", [128, 2], F32)
    invrow = sb("invrow", [1, 2], F32)
    bcRQp = sb("bcRQp", [128, NT], F32)

    AR_BYTES = 138240
    backing = sb("arena", [128, AR_BYTES], U8)
    ar = Arena(nc, backing)
    a_x16 = ar.at("x16", [128, NF * NT], F32, 0)
    a_sq = [ar.at("sq", [128, NT], F32, 65536),
            ar.at("sq2", [128, NT], F32, 69632)]
    a_bcA = ar.at("bcA", [128, NT], F32, 73728)
    a_bcB = ar.at("bcB", [128, NT], F32, 77824)
    a_fuse = ar.at("fuse", [128, 16 * NT], BF16, 0)
    a_tb = [ar.at("tb", [128, 2048], F32, 32768),
            ar.at("tb2", [128, 2048], F32, 40960)]
    a_tern = [ar.at("tern", [128, 2048], BF16, 49152),
              ar.at("tern2", [128, 2048], BF16, 53248)]
    a_gs = ar.at("gsil", [128, NT], BF16, 57344)
    a_dt = [ar.at("dt", [128, NT], F32, 59392),
            ar.at("dt2", [128, NT], F32, 63488)]
    a_abs = ar.at("absdump", [128, 2048], BF16, 67584)
    a_wd = [ar.at("wd", [128, 2048], BF16, 71680),
            ar.at("wd2", [128, 2048], BF16, 75776)]
    a_wg = [ar.at("wg", [128, 2048], BF16, 81920),
            ar.at("wg2", [128, 2048], BF16, 86016)]
    a_wu = [ar.at("wu", [128, 2048], BF16, 90112),
            ar.at("wu2", [128, 2048], BF16, 94208)]
    a_xpr = [ar.at("xpr", [128, NT], F32, 98304),
             ar.at("xpr2", [128, NT], F32, 102400)]
    a_sqp = [ar.at("sqp", [128, NT], F32, 106496),
             ar.at("sqp2", [128, NT], F32, 110592)]
    a_ksb = [ar.at("ksb0", [128, 512], BF16, 98304),
             ar.at("ksb1", [128, 512], BF16, 99328)]
    a_vsb = [ar.at("vsb0", [128, 512], BF16, 100352),
             ar.at("vsb1", [128, 512], BF16, 101376)]
    a_xpb = [ar.at("xpb", [128, 512], F32, 102400),
             ar.at("xpb2", [128, 512], F32, 104448)]
    a_xnp = ar.at("xnp", [128, 16 * 512], BF16, 106496)
    a_ts2 = [ar.at("ts2a", [128, 2048], BF16, 122880),
             ar.at("ts2b", [128, 2048], BF16, 126976)]
    a_bcAp = ar.at("bcAp", [128, NT], F32, 122880)
    a_bcBp = ar.at("bcBp", [128, NT], F32, 126976)
    a_patq = ar.at("patq", [128, 2048], BF16, 131072)
    a_vT = ar.at("vT", [128, 512], BF16, 135168)
    a_kT = ar.at("kT", [128, 512], BF16, 136192)
    a_mtmp = [ar.at("mtmp", [128, 128], F32, 137216),
              ar.at("mtmp2", [128, 128], F32, 137728)]
    a_o16 = ar.at("o16", [128, NF * NT], BF16, 0)
    a_qkv = ar.at("qkvb", [128, QOC * 512], BF16, 32768)
    a_ts3 = [ar.at("ts3a", [128, 2048], BF16, 81920),
             ar.at("ts3b", [128, 2048], BF16, 86016)]
    a_qt = ar.at("qtmp", [128, 512], BF16, 90112)
    a_sm = ar.at("scoresm", [128, 512], BF16, 91136)
    a_vT3 = ar.at("vT3", [128, 512], BF16, 92160)
    a_kT3 = ar.at("kT3", [128, 512], BF16, 93184)
    a_bcRT = ar.at("bcRT", [128, NT], F32, 32768)
    a_st = [ar.at("stmp", [128, NT], F32, 36864),
            ar.at("stmp2", [128, NT], F32, 40960)]
    a_ob = [ar.at("ob", [128, NT], F32, 36864),
            ar.at("ob2", [128, NT], F32, 40960)]
    a_p5 = [ar.at("p5a", [128, 5 * NT], F32, 45056),
            ar.at("p5b", [128, 5 * NT], F32, 65536)]
    a_ts4 = [ar.at("ts4a", [128, 2048], BF16, 86016),
             ar.at("ts4b", [128, 2048], BF16, 90112)]
    a_pt1 = ar.at("ptmp1", [128, 512], F32, 94208)
    a_pt2 = ar.at("ptmp2", [128, 512], F32, 96256)

    b = [ps(f"pb{i}", [128, 512], F32) for i in range(7)]
    b7 = b[6]
    pt = ps("ptr", [128, 1024], BF16)

    s0 = {k: sem("s0_" + k) for k in
          ["ld", "ini", "xq0", "xq1", "xq2", "xq3", "sq", "mm", "r1", "r2",
           "row", "pb", "bc", "xn", "neg"]}
    s1 = {k: sem("s1_" + k) for k in
          ["tl0", "tl1", "ta", "ar2", "ab", "cr", "rqr", "pb", "bce", "tt",
           "ts0", "ts1", "gw0", "gw1", "uw0", "uw1", "dw0", "dw1",
           "g", "gel", "gev", "u", "uev", "dmm", "dev", "dst0", "dst1"]}
    s2 = {k: sem("s2_" + k) for k in
          ["mskd", "mska", "sq", "mm", "r1", "r2", "row", "pb", "bc",
           "xd0", "xd1", "pd0", "pd1", "xn", "tl0", "tl1",
           "kmm", "kev", "tr", "te", "kvm", "su", "sb", "iv", "pe0", "pe1"]}
    s3 = {k: sem("s3_" + k) for k in
          ["tl0", "tl1", "qmm", "qev", "kp", "sc", "me", "tr", "te", "ic",
           "oe", "kv", "su"]}
    s4 = {k: sem("s4_" + k) for k in
          ["sqa", "sqb", "mma", "mmb", "r1", "r2", "row", "pb", "bc",
           "tl0", "tl1", "pm", "ld0", "ld1", "cmb", "st0", "st1"]}

    def par(sd, base, i):
        return sd[base + str(i % 2)]

    def stats_mms(tensor, smm, ssq, x_of, sq_of):
        for f in range(NF):
            tensor.wait_ge(ssq, f + 1)
            st = (f == 0)
            sp = (f == NF - 1)
            xs = x_of(f)
            ss_ = sq_of(f)
            tensor.matmul(b[0][0:1, 0:512], onescol[:, :], xs[:, 0:512],
                          start=st, stop=sp)
            tensor.matmul(b[1][0:1, 0:512], onescol[:, :], xs[:, 512:1024],
                          start=st, stop=sp)
            tensor.matmul(b[2][0:1, 0:512], onescol[:, :], ss_[:, 0:512],
                          start=st, stop=sp)
            tensor.matmul(b[3][0:1, 0:512], onescol[:, :], ss_[:, 512:1024],
                          start=st, stop=sp).then_inc(smm, 1)

    def row_chain_dve(vector, ss, bcA_, bcB_, bcRQ_):
        R0 = rows[0:1, 0:512]
        R1 = rows[0:1, 512:1024]
        R2 = rows[0:1, 0:512]
        for half in range(2):
            sx = (b[0] if half == 0 else b[1])[0:1, 0:512]
            sxx = (b[2] if half == 0 else b[3])[0:1, 0:512]
            hs = slice(half * 512, half * 512 + 512)
            mrow = bcB_[0:1, hs]
            rln = bcA_[0:1, hs]
            vector.tensor_scalar(mrow, sx, 1.0 / 2048.0, None, ALU.mult)
            vector.tensor_tensor(R0, mrow, mrow, ALU.mult)
            vector.scalar_tensor_tensor(R1, sxx, 1.0 / 2048.0, R0,
                                        ALU.mult, ALU.subtract).then_inc(ss["r1"], 3)
            vector.wait_ge(ss["r2"], 1 + 2 * half)
            vector.reciprocal(rln, R2)
            vector.tensor_tensor(R0, R1, rln, ALU.mult)
            vector.tensor_tensor(R1, R0, rln, ALU.mult).then_inc(ss["r1"], 3)
            vector.wait_ge(ss["r2"], 2 + 2 * half)
            vector.reciprocal(bcRQ_[0:1, hs], R2)
            vector.tensor_tensor(R0, mrow, rln, ALU.mult)
            vector.tensor_copy(mrow, R0)

    def row_chain_act(scalar, ss, epscol):
        R1 = rows[0:1, 512:1024]
        R2 = rows[0:1, 0:512]
        for half in range(2):
            scalar.wait_ge(ss["r1"], 3 + 6 * half)
            scalar.activation(R2, R1, AF.Sqrt,
                              bias=cst[0:1, epscol:epscol + 1]).then_inc(ss["r2"], 1)
            scalar.wait_ge(ss["r1"], 6 + 6 * half)
            scalar.activation(R2, R1, AF.Sqrt,
                              bias=cst[0:1, 1:2]).then_inc(ss["r2"], 1)

    with nc.Block() as blk:
        @blk.sync
        def _(sync):
            for dd, ss_ in [(d_qn, qn_s), (d_nw, nw_s), (d_pnw, pnw_s),
                            (d_sd, sd_s), (d_sg, sg_s), (d_su, su_s),
                            (d_s16, s16), (d_cD, cD), (d_cPEN, cPEN),
                            (d_cT, cT), (d_cR, cR), (d_idn, idn),
                            (d_cst, cst)]:
                sync.dma_start(ss_[:, :], dd.ap()).then_inc(s0["ld"], 16)
            for f in range(NF):
                sync.dma_start(a_x16[:, f * NT:(f + 1) * NT],
                               d_x.ap()[f * 128:(f + 1) * 128, :]).then_inc(
                                   s0["xq" + str(f // 4)], 16)
            for f in range(NF):
                if f >= 2:
                    sync.wait_ge(s2["mm"], f - 1)
                sync.dma_start(a_xpr[f % 2][:, :],
                               d_xp.ap()[f * 128:(f + 1) * 128, :]).then_inc(par(s2, "xd", f), 16)

        @blk.gpsimd
        def _(gpsimd):
            gpsimd.memset(ones_t[:, :], 1.0)
            gpsimd.memset(onescol[:, :], 1.0)
            gpsimd.memset(onesrow[:, :], 1.0)
            gpsimd.memset(S[:, :], 0.0)
            gpsimd.memset(Sb[:, :], 0.0).then_inc(s0["ini"], 1)
            gpsimd.dma_start(a_wg[0][:, :],
                             d_wg.ap()[0:128, :]).then_inc(s1["gw0"], 16)
            gpsimd.dma_start(a_wu[0][:, :],
                             d_wu.ap()[0:128, :]).then_inc(s1["uw0"], 16)

        @blk.scalar
        def _(scalar):
            for f in range(NF):
                scalar.wait_ge(s0["xq" + str(f // 4)], 64)
                if f >= 2:
                    scalar.wait_ge(s0["mm"], f - 1)
                scalar.activation(a_sq[f % 2][:, :], a_x16[:, f * NT:(f + 1) * NT],
                                  AF.Square).then_inc(s0["sq"], 1)
            scalar.wait_ge(s0["ld"], 208)
            row_chain_act(scalar, s0, 0)
            for f in range(NF):
                scalar.wait_ge(par(s2, "xd", f), 16 * (f // 2 + 1))
                if f >= 2:
                    scalar.wait_ge(s2["mm"], f - 1)
                scalar.activation(a_sqp[f % 2][:, :], a_xpr[f % 2][:, :],
                                  AF.Square).then_inc(s2["sq"], 1)
            row_chain_act(scalar, s2, 0)
            scalar.wait_ge(s0["neg"], 1)
            for h in range(H):
                scalar.activation(lamRc[:, h:h + 1], cst[:, 5:6], AF.Exp,
                                  scale=negs[:, h:h + 1])
                scalar.activation(lamTc[:, h:h + 1], s16[:, h:h + 1], AF.Exp,
                                  scale=cst[:, 3:4])

        @blk.tensor
        def _(tensor):
            tensor.wait_ge(s0["ini"], 1)
            stats_mms(tensor, s0["mm"], s0["sq"],
                      lambda f: a_x16[:, f * NT:(f + 1) * NT],
                      lambda f: a_sq[f % 2])
            tensor.wait_ge(s0["row"], 1)
            tensor.matmul(b[4][:, 0:512], onesrow[:, :], a_bcA[0:1, 0:512],
                          start=True, stop=True)
            tensor.matmul(b[5][:, 0:512], onesrow[:, :], a_bcA[0:1, 512:1024],
                          start=True, stop=True).then_inc(s0["pb"], 1)
            tensor.wait_ge(s0["bc"], 1)
            tensor.matmul(b[4][:, 0:512], onesrow[:, :], a_bcB[0:1, 0:512],
                          start=True, stop=True)
            tensor.matmul(b[5][:, 0:512], onesrow[:, :], a_bcB[0:1, 512:1024],
                          start=True, stop=True).then_inc(s0["pb"], 1)
            stats_mms(tensor, s2["mm"], s2["sq"],
                      lambda f: a_xpr[f % 2], lambda f: a_sqp[f % 2])
            tensor.wait_ge(s2["row"], 1)
            for i, tile in enumerate([a_bcAp, a_bcBp]):
                if i > 0:
                    tensor.wait_ge(s2["bc"], 2 * i - 1)
                tensor.matmul(b[4][:, 0:512], onesrow[:, :], tile[0:1, 0:512],
                              start=True, stop=True).then_inc(s2["pb"], 1)
                tensor.wait_ge(s2["bc"], 2 * i)
                tensor.matmul(b[5][:, 0:512], onesrow[:, :], tile[0:1, 512:1024],
                              start=True, stop=True).then_inc(s2["pb"], 1)

        @blk.vector
        def _(vector):
            vector.wait_ge(s0["ld"], 208)
            vector.wait_ge(s0["ini"], 1)
            vector.tensor_tensor(fw_s[:, :], nw_s[:, :], pnw_s[:, :], ALU.mult)
            vector.tensor_scalar(negs[:, :], s16[:, :], -1.0, None,
                                 ALU.mult).then_inc(s0["neg"], 1)
            vector.wait_ge(s0["mm"], 16)
            row_chain_dve(vector, s0, a_bcA, a_bcB, bcRQ)
            vector.tensor_copy(absr[:, 0:1], onescol[:, :]).then_inc(s0["row"], 1)
            vector.wait_ge(s0["pb"], 1)
            vector.tensor_copy(a_bcA[:, 0:512], b[4][:, 0:512])
            vector.tensor_copy(a_bcA[:, 512:1024], b[5][:, 0:512]).then_inc(s0["bc"], 1)
            vector.wait_ge(s0["pb"], 2)
            vector.tensor_copy(a_bcB[:, 0:512], b[4][:, 0:512])
            vector.tensor_copy(a_bcB[:, 512:1024], b[5][:, 0:512]).then_inc(s0["bc"], 1)
            for f in range(NF):
                vector.tensor_tensor(a_sq[f % 2][:, :], a_x16[:, f * NT:(f + 1) * NT],
                                     a_bcA[:, :], ALU.mult).then_inc(s0["xn"], 1)
                vector.tensor_tensor(xn[:, f * NT:(f + 1) * NT],
                                     a_sq[f % 2][:, :], a_bcB[:, :],
                                     ALU.subtract).then_inc(s0["xn"], 1)
            vector.wait_ge(s2["mm"], 16)
            row_chain_dve(vector, s2, a_bcAp, a_bcBp, bcRQp)
            vector.tensor_copy(absr[:, 1:2], onescol[:, :]).then_inc(s2["row"], 1)
            for i, tile in enumerate([a_bcAp, a_bcBp]):
                vector.wait_ge(s2["pb"], 2 * i + 1)
                vector.tensor_copy(tile[:, 0:512], b[4][:, 0:512]).then_inc(s2["bc"], 1)
                vector.wait_ge(s2["pb"], 2 * i + 2)
                vector.tensor_copy(tile[:, 512:1024], b[5][:, 0:512]).then_inc(s2["bc"], 1)

    TSL = []
    for h_ in range(16):
        TSL.append(("q", 16 + h_))
        TSL.append(("q", 32 + h_))
    TSL += [("q", i) for i in range(16)] + [("p", i) for i in range(POC)]
    NSL = len(TSL)
    TPOS = {kv_: j for j, kv_ in enumerate(TSL)}

    def slab_src(j):
        kind, i = TSL[j]
        dd = d_qkv if kind == "q" else d_proj
        return dd.ap()[i * 128:(i + 1) * 128, :]

    def slab_dst(j):
        kind, i = TSL[j]
        dd = d_scq if kind == "q" else d_scp
        return dd.ap()[i * 128:(i + 1) * 128, :]

    SCHED = [[] for _ in range(129)]
    tern_slots = []
    for q_ in range(4):
        for mi_ in range(16):
            if q_ * 16 + mi_ >= 27:
                tern_slots.append(q_ * 32 + mi_)
        for o2_ in range(16):
            if q_ == 1:
                tern_slots.append(q_ * 32 + 16 + o2_)
            elif q_ >= 2:
                tern_slots += [q_ * 32 + 16 + o2_] * 2
    for i in range(16):
        SCHED[4 + i // 2].append(("xnp", i))
    kv_slot = {}
    for n in range(32):
        kv_slot[n] = tern_slots[n] + 1
        SCHED[kv_slot[n]].append(("kv", n))
    for n in range(32):
        kv_slot[32 + n] = min(94 + n, 127)
        SCHED[kv_slot[32 + n]].append(("kv", 32 + n))
    for i in range(16):
        SCHED[80 + i // 2].append(("xnp", 16 + i))
    ptr = 20
    for P in range(32):
        if P == 16:
            ptr = 96
        g_ = P // 16
        ready = kv_slot[g_ * 32 + 2 * (P % 16) + 1] + 1
        ptr = max(ptr, ready)
        SCHED[min(ptr, 128)].append(("ch", P))
        ptr += 1
    _ORD = {"xnp": 0, "ch": 1, "kv": 2}
    for sl_ in SCHED:
        sl_.sort(key=lambda a: _ORD[a[0]])

    with nc.Block() as blk:
        @blk.sync
        def _(sync):
            for j in range(NSL):
                if j < 2:
                    sync.wait_ge(s0["xn"], 32)
                if j >= 2:
                    sync.wait_ge(s1["ta"], j - 1)
                sync.dma_start(a_tb[j % 2][:, :], slab_src(j)).then_inc(par(s1, "tl", j), 16)
            def ts2_load(n):
                sync.wait_ge(par(s1, "ts", n), 16 * (n // 2 + 1))
                if n < 2:
                    sync.wait_ge(s2["xn"], 64)
                if n >= 2:
                    sync.wait_ge(s2["kmm"], 32 + n - 1)
                kind, oc = TSL[n]
                sync.dma_start(a_ts2[n % 2][:, :],
                               d_scq.ap()[oc * 128:(oc + 1) * 128, :]).then_inc(par(s2, "tl", n), 16)

            for j in range(NSL):
                sync.wait_ge(s1["ta"], min(j + 63, NSL))
                if j >= 2:
                    sync.wait_ge(s1["tt"], j - 1)
                sync.dma_start(a_tb[j % 2][:, :], slab_src(j)).then_inc(par(s1, "tl", j), 16)
            for n in range(32):
                ts2_load(n)

        @blk.gpsimd
        def _(gpsimd):
            for q in range(4):
                for mi in range(16):
                    m = q * 16 + mi
                    if m >= 1:
                        if m >= 2:
                            gpsimd.wait_ge(s1["g"], m - 1)
                        gpsimd.dma_start(a_wg[m % 2][:, :],
                                         d_wg.ap()[m * 128:(m + 1) * 128, :]).then_inc(par(s1, "gw", m), 16)
                        if m >= 2:
                            gpsimd.wait_ge(s1["u"], m - 1)
                        gpsimd.dma_start(a_wu[m % 2][:, :],
                                         d_wu.ap()[m * 128:(m + 1) * 128, :]).then_inc(par(s1, "uw", m), 16)
                    if q == 0 and 3 <= mi <= 10:
                        for i2 in (2 * (mi - 3), 2 * (mi - 3) + 1):
                            gpsimd.wait_ge(s2["mm"], 16)
                            if i2 >= 2:
                                gpsimd.wait_ge(s2["xn"], 2 * (i2 - 1))
                            gpsimd.dma_start(
                                a_xpb[i2 % 2][:, :],
                                d_xp.ap()[i2 * 128:(i2 + 1) * 128,
                                          0:512]).then_inc(par(s2, "pd", i2), 16)
                for o2 in range(16):
                    d_idx = q * 16 + o2
                    if d_idx < 2:
                        gpsimd.wait_ge(s0["xn"], 32)
                    if d_idx >= 2:
                        gpsimd.wait_ge(s1["dmm"], d_idx - 1)
                    gpsimd.dma_start(a_wd[o2 % 2][:, :],
                                     d_wd.ap()[d_idx * 128:(d_idx + 1) * 128, :]).then_inc(par(s1, "dw", d_idx), 16)


        @blk.scalar
        def _(scalar):
            n_abs = [0]
            n_st = [0]

            def abs_unit(k):
                for _ in range(k):
                    if n_abs[0] >= NSL:
                        return
                    j = n_abs[0]
                    scalar.wait_ge(par(s1, "tl", j), 16 * (j // 2 + 1))
                    kind, i = TSL[j]
                    acc = absq[:, i:i + 1] if kind == "q" else absp[:, i:i + 1]
                    scalar.activation(a_abs[:, :], a_tb[j % 2][:, :], AF.Abs,
                                      accum_out=acc).then_inc(s1["ta"], 1)
                    n_abs[0] += 1

            def store_unit(k):
                for _ in range(k):
                    if n_st[0] >= NSL:
                        return
                    j = n_st[0]
                    scalar.wait_ge(s1["tt"], j + 1)
                    scalar.dma_start(slab_dst(j), a_tern[j % 2][:, :]).then_inc(par(s1, "ts", j), 16)
                    n_st[0] += 1

            for q in range(4):
                for mi in range(16):
                    m = q * 16 + mi
                    if m >= 1:
                        scalar.wait_ge(s1["uev"], m)
                    scalar.wait_ge(s1["g"], m + 1)
                    scalar.activation(a_gs[:, 0:512], b[0][:, 0:512], AF.Silu,
                                      scale=sg_s[:, m:m + 1]).then_inc(s1["gel"], 1)
                    scalar.activation(a_gs[:, 512:1024], b[1][:, 0:512], AF.Silu,
                                      scale=sg_s[:, m:m + 1]).then_inc(s1["gev"], 1)
                    if m < 16:
                        abs_unit(3)
                    elif m >= 28:
                        store_unit(1)
                    if 28 <= m < 44:
                        for h in (m - 28,):
                            scalar.wait_ge(s2["mskd"], h + 1)
                            scalar.activation(maskT[:, h * 128:(h + 1) * 128],
                                              a_mtmp[h % 2][:, :],
                                              AF.Exp).then_inc(s2["mska"], 1)
                            scalar.activation(lamTt[:, h * 128:(h + 1) * 128],
                                              cT[:, :], AF.Exp,
                                              scale=negs[:, h:h + 1])
                            scalar.activation(lamRt[:, h * 128:(h + 1) * 128],
                                              cR[:, :], AF.Exp,
                                              scale=negs[:, h:h + 1])
                for o2 in range(16):
                    d_idx = q * 16 + o2
                    scalar.wait_ge(s1["dev"], d_idx + 1)
                    scalar.dma_start(d_scf.ap()[d_idx * 128:(d_idx + 1) * 128, :],
                                     a_dt[d_idx % 2][:, :]).then_inc(par(s1, "dst", d_idx), 16)
                    if q == 0:
                        abs_unit(1)
                    else:
                        store_unit(1)
                        if q >= 2:
                            store_unit(1)
                    if q == 2 and o2 < 8:
                        for i2 in (16 + 2 * o2, 16 + 2 * o2 + 1):
                            i = i2 - 16
                            scalar.wait_ge(s2["xn"], 2 * (i2 - 1))
                            scalar.dma_start(
                                a_xpb[i2 % 2][:, :],
                                d_xp.ap()[i * 128:(i + 1) * 128,
                                          512:1024]).then_inc(par(s2, "pe", i2), 16)
            store_unit(NSL)

        @blk.tensor
        def _(tensor):
            def t_kv(n):
                g, sl = n // 32, n % 32
                if g == 0:
                    tensor.wait_ge(s1["tt"], sl + 1)
                else:
                    tensor.wait_ge(par(s2, "tl", sl), 16 * (sl // 2 + 1))
                tensor.wait_ge(s2["xn"], 32 * (g + 1))
                if n == 0:
                    tensor.wait_ge(s2["bc"], 4)
                if n >= 1:
                    tensor.wait_ge(s2["kev"], n)
                src = a_tern[sl % 2] if g == 0 else a_ts2[sl % 2]
                for fc in range(NF):
                    mm = tensor.matmul(b[5][:, :], src[:, fc * 128:(fc + 1) * 128],
                                       a_xnp[:, fc * 512:(fc + 1) * 512],
                                       start=(fc == 0), stop=(fc == NF - 1))
                mm.then_inc(s2["kmm"], 1)

            def t_ch(P):
                tensor.wait_ge(s2["kev"], 2 * P + 2)
                if P >= 1:
                    tensor.wait_ge(s2["te"], P)
                for cl in range(4):
                    tensor.transpose(pt[:, cl * 128:(cl + 1) * 128],
                                     a_vsb[P % 2][:, cl * 128:(cl + 1) * 128],
                                     idn[:, :])
                    mm = tensor.transpose(pt[:, 512 + cl * 128:512 + (cl + 1) * 128],
                                          a_ksb[P % 2][:, cl * 128:(cl + 1) * 128],
                                          idn[:, :])
                mm.then_inc(s2["tr"], 1)
                tensor.wait_ge(s2["te"], P + 1)
                if P >= 1:
                    tensor.wait_ge(s2["su"], P)
                for cl in range(4):
                    mm = tensor.matmul(b[6][:, cl * 128:(cl + 1) * 128],
                                       a_kT[:, cl * 128:(cl + 1) * 128],
                                       a_vT[:, cl * 128:(cl + 1) * 128],
                                       start=True, stop=True)
                mm.then_inc(s2["kvm"], 1)

            def t_inj(act):
                if act[0] == "kv":
                    t_kv(act[1])
                elif act[0] == "ch":
                    t_ch(act[1])

            for q in range(4):
                for mi in range(16):
                    m = q * 16 + mi
                    tensor.wait_ge(par(s1, "gw", m), 16 * (m // 2 + 1))
                    if m == 0:
                        tensor.wait_ge(s0["xn"], 32)
                        tensor.wait_ge(s2["row"], 1)
                    if m >= 1:
                        tensor.wait_ge(s1["gev"], m)
                    for f in range(NF):
                        st = (f == 0)
                        sp = (f == NF - 1)
                        stat = a_wg[m % 2][:, f * 128:(f + 1) * 128]
                        mm = tensor.matmul(b[1][:, :], stat,
                                           xn[:, f * NT + 512:f * NT + 1024],
                                           start=st, stop=sp)
                    if m >= 1:
                        tensor.wait_ge(s1["uev"], m)
                    for f in range(NF):
                        st = (f == 0)
                        sp = (f == NF - 1)
                        stat = a_wg[m % 2][:, f * 128:(f + 1) * 128]
                        mm = tensor.matmul(b[0][:, :], stat, xn[:, f * NT:f * NT + 512],
                                           start=st, stop=sp)
                    mm.then_inc(s1["g"], 1)
                    tensor.wait_ge(par(s1, "uw", m), 16 * (m // 2 + 1))
                    if m >= 1:
                        tensor.wait_ge(s1["uev"], m)
                    for f in range(NF):
                        st = (f == 0)
                        sp = (f == NF - 1)
                        stat = a_wu[m % 2][:, f * 128:(f + 1) * 128]
                        mm = tensor.matmul(b[2][:, :], stat, xn[:, f * NT:f * NT + 512],
                                           start=st, stop=sp)
                    tensor.wait_ge(s1["gel"], m + 1)
                    for f in range(NF):
                        st = (f == 0)
                        sp = (f == NF - 1)
                        stat = a_wu[m % 2][:, f * 128:(f + 1) * 128]
                        mm = tensor.matmul(b[0][:, :], stat,
                                           xn[:, f * NT + 512:f * NT + 1024],
                                           start=st, stop=sp)
                    mm.then_inc(s1["u"], 1)
                    if m == 21:
                        tensor.wait_ge(s1["ar2"], 1)
                        tensor.matmul(b[6][0:1, 0:1], onescol[:, :],
                                      absr[:, 0:1], start=True, stop=True)
                        tensor.matmul(b[6][0:1, 1:2], onescol[:, :],
                                      absr[:, 1:2], start=True,
                                      stop=True).then_inc(s1["ab"], 1)
                    if m == 22:
                        tensor.wait_ge(s1["cr"], 1)
                        tensor.matmul(b[6][:, 0:2], onesrow[:, :],
                                      invrow[0:1, 0:2], start=True,
                                      stop=True).then_inc(s1["pb"], 1)
                    if m == 23:
                        tensor.wait_ge(s1["rqr"], 2)
                        tensor.wait_ge(s1["bce"], 1)
                        tensor.matmul(b[6][:, 0:512], onesrow[:, :],
                                      bcRQ[0:1, 0:512], start=True,
                                      stop=True).then_inc(s1["pb"], 1)
                    if m == 24:
                        tensor.wait_ge(s1["bce"], 2)
                        tensor.matmul(b[6][:, 0:512], onesrow[:, :],
                                      bcRQ[0:1, 512:1024], start=True,
                                      stop=True).then_inc(s1["pb"], 1)
                    if m == 25:
                        tensor.wait_ge(s1["bce"], 3)
                        tensor.matmul(b[6][:, 0:512], onesrow[:, :],
                                      bcRQp[0:1, 0:512], start=True,
                                      stop=True).then_inc(s2["pb"], 1)
                    if m == 26:
                        tensor.wait_ge(s2["iv"], 1)
                        tensor.matmul(b[6][:, 0:512], onesrow[:, :],
                                      bcRQp[0:1, 512:1024], start=True,
                                      stop=True).then_inc(s2["pb"], 1)
                    for act in SCHED[q * 32 + mi]:
                        t_inj(act)
                tensor.wait_ge(s1["uev"], 16 * (q + 1))
                for o2 in range(16):
                    d_idx = q * 16 + o2
                    tensor.wait_ge(par(s1, "dw", d_idx), 16 * (d_idx // 2 + 1))
                    if d_idx == 0:
                        tensor.wait_ge(s2["bc"], 4)
                    if d_idx >= 1:
                        tensor.wait_ge(s1["dev"], d_idx)
                    for mi2 in range(16):
                        st = (mi2 == 0)
                        sp = (mi2 == 15)
                        stat = a_wd[o2 % 2][:, mi2 * 128:(mi2 + 1) * 128]
                        tensor.matmul(b[3][:, :], stat,
                                      a_fuse[:, mi2 * NT:mi2 * NT + 512],
                                      start=st, stop=sp)
                        mm = tensor.matmul(b[4][:, :], stat,
                                           a_fuse[:, mi2 * NT + 512:mi2 * NT + 1024],
                                           start=st, stop=sp)
                    mm.then_inc(s1["dmm"], 1)
                    for act in SCHED[q * 32 + 16 + o2]:
                        t_inj(act)
            for act in SCHED[128]:
                t_inj(act)

        @blk.vector
        def _(vector):
            n_tern = [0]

            def tern_unit(k):
                for _ in range(k):
                    if n_tern[0] >= NSL:
                        return
                    j = n_tern[0]
                    vector.wait_ge(par(s1, "tl", j), 16 * (32 + j // 2 + 1))
                    if j >= 2:
                        vector.wait_ge(par(s1, "ts", j), 16 * (j // 2))
                        if j - 2 < 32:
                            vector.wait_ge(s2["kmm"], j - 1)
                    kind, i = TSL[j]
                    col = 0 if kind == "q" else 1
                    vector.tensor_scalar(a_abs[:, :], a_tb[j % 2][:, :],
                                         negc_qp[:, col:col + 1], None, ALU.is_lt)
                    vector.scalar_tensor_tensor(a_tb[j % 2][:, :], a_tb[j % 2][:, :],
                                                c_qp[:, col:col + 1], a_abs[:, :],
                                                ALU.is_gt, ALU.subtract)
                    if kind == "q":
                        vector.tensor_tensor(a_tern[j % 2][:, :], a_tb[j % 2][:, :],
                                             a_patq[:, :], ALU.mult).then_inc(s1["tt"], 1)
                    else:
                        for fc in range(NF):
                            tt_ = vector.tensor_scalar(
                                a_tern[j % 2][:, fc * 128:(fc + 1) * 128],
                                a_tb[j % 2][:, fc * 128:(fc + 1) * 128],
                                fw_s[:, fc:fc + 1], None, ALU.mult)
                        tt_.then_inc(s1["tt"], 1)
                    n_tern[0] += 1

            def v_kv(n):
                g, sl = n // 32, n % 32
                P = g * 16 + sl // 2
                vector.wait_ge(s2["kmm"], n + 1)
                if n == 0:
                    vector.wait_ge(s2["iv"], 2)
                if sl % 2 == 0 and P >= 2:
                    vector.wait_ge(s2["tr"], P - 1)
                dst = a_ksb[P % 2] if sl % 2 == 0 else a_vsb[P % 2]
                vector.tensor_tensor(dst[:, :], b[5][:, :],
                                     bcRQp[:, g * 512:g * 512 + 512],
                                     ALU.mult).then_inc(s2["kev"], 1)

            def v_ch(P):
                h = P % 16
                vector.wait_ge(s2["tr"], P + 1)
                vector.tensor_copy(a_vT[:, :], pt[:, 0:512])
                vector.tensor_scalar(a_kT[:, :], pt[:, 512:1024], lamRc[:, h:h + 1],
                                     None, ALU.mult).then_inc(s2["te"], 1)
                vector.wait_ge(s2["kvm"], P + 1)
                for cl in range(4):
                    tt_ = vector.scalar_tensor_tensor(
                        S[:, h * 128:(h + 1) * 128],
                        S[:, h * 128:(h + 1) * 128],
                        lamTc[:, h:h + 1], b[6][:, cl * 128:(cl + 1) * 128],
                        ALU.mult, ALU.add)
                tt_.then_inc(s2["su"], 1)
                if P >= 16:
                    vector.tensor_copy(Sb[:, h * 128:(h + 1) * 128],
                                       S[:, h * 128:(h + 1) * 128]).then_inc(s2["sb"], 1)

            def v_xnp(i2):
                g, i = i2 // 16, i2 % 16
                if i2 < 16:
                    vector.wait_ge(par(s2, "pd", i2), 16 * (i2 // 2 + 1))
                else:
                    vector.wait_ge(par(s2, "pe", i2), 16 * ((i2 - 16) // 2 + 1))
                if i2 == 16:
                    vector.wait_ge(s2["kmm"], 32)
                hs = slice(g * 512, g * 512 + 512)
                vector.tensor_tensor(a_xpb[i2 % 2][:, :], a_xpb[i2 % 2][:, :],
                                     a_bcAp[:, hs], ALU.mult)
                vector.tensor_tensor(a_xnp[:, i * 512:(i + 1) * 512],
                                     a_xpb[i2 % 2][:, :], a_bcBp[:, hs],
                                     ALU.subtract).then_inc(s2["xn"], 2)

            def v_inj(act):
                if act[0] == "kv":
                    v_kv(act[1])
                elif act[0] == "ch":
                    v_ch(act[1])
                else:
                    v_xnp(act[1])

            for fc in range(16):
                vector.tensor_scalar(a_patq[:, fc * 128:(fc + 1) * 128], ones_t[:, :],
                                     qn_s[:, fc:fc + 1], None, ALU.mult)

            for q in range(4):
                for mi in range(16):
                    m = q * 16 + mi
                    vector.wait_ge(s1["u"], m + 1)
                    vector.wait_ge(s1["gev"], m + 1)
                    if mi == 0 and q >= 1:
                        vector.wait_ge(s1["dmm"], 16 * q)
                    vector.scalar_tensor_tensor(a_fuse[:, mi * NT:mi * NT + 512],
                                                b[2][:, :], su_s[:, m:m + 1],
                                                a_gs[:, 0:512], ALU.mult, ALU.mult)
                    vector.scalar_tensor_tensor(a_fuse[:, mi * NT + 512:(mi + 1) * NT],
                                                b[0][:, :], su_s[:, m:m + 1],
                                                a_gs[:, 512:1024], ALU.mult,
                                                ALU.mult).then_inc(s1["uev"], 1)
                    if m == 20:
                        vector.wait_ge(s1["ta"], NSL)
                        vector.reduce_sum(absr[:, 0:1], absq[:, :], axis=AX.X)
                        vector.reduce_sum(absr[:, 1:2], absp[:, :],
                                          axis=AX.X).then_inc(s1["ar2"], 1)
                    if m == 21:
                        vector.wait_ge(s1["ab"], 1)
                        vector.tensor_scalar(invrow[0:1, 0:1], b[6][0:1, 0:1],
                                             1.0 / QKV_N, 1e-5, ALU.mult, ALU.max)
                        vector.tensor_scalar(invrow[0:1, 1:2], b[6][0:1, 1:2],
                                             1.0 / PROJ_N, 1e-5, ALU.mult,
                                             ALU.max).then_inc(s1["cr"], 1)
                        vector.tensor_scalar(bcRQ[0:1, :], bcRQ[0:1, :],
                                             invrow[0:1, 0:1], None,
                                             ALU.mult).then_inc(s1["rqr"], 1)
                        vector.tensor_scalar(bcRQp[0:1, :], bcRQp[0:1, :],
                                             invrow[0:1, 0:1], None,
                                             ALU.mult).then_inc(s1["rqr"], 1)
                    if m == 22:
                        vector.wait_ge(s1["pb"], 1)
                        vector.tensor_scalar(c_qp[:, :], b[6][:, 0:2], 0.5, None,
                                             ALU.mult)
                        vector.tensor_scalar(negc_qp[:, :], c_qp[:, :], -1.0,
                                             None, ALU.mult).then_inc(s1["bce"], 1)
                    if m == 23:
                        vector.wait_ge(s1["pb"], 2)
                        vector.tensor_copy(bcRQ[:, 0:512],
                                           b[6][:, 0:512]).then_inc(s1["bce"], 1)
                    if m == 24:
                        vector.wait_ge(s1["pb"], 3)
                        vector.tensor_copy(bcRQ[:, 512:1024],
                                           b[6][:, 0:512]).then_inc(s1["bce"], 1)
                    if m == 25:
                        vector.wait_ge(s2["pb"], 5)
                        vector.tensor_copy(bcRQp[:, 0:512],
                                           b[6][:, 0:512]).then_inc(s2["iv"], 1)
                    if m == 26:
                        vector.wait_ge(s2["pb"], 6)
                        vector.tensor_copy(bcRQp[:, 512:1024],
                                           b[6][:, 0:512]).then_inc(s2["iv"], 1)
                    if m >= 27:
                        tern_unit(1)
                    if 27 <= m < 43:
                        for h in (m - 27,):
                            if h >= 2:
                                vector.wait_ge(s2["mska"], h - 1)
                            mt = a_mtmp[h % 2]
                            vector.tensor_scalar(mt[:, :], cD[:, :],
                                                 negs[:, h:h + 1], None, ALU.mult)
                            vector.tensor_tensor(mt[:, :], mt[:, :],
                                                 cPEN[:, :],
                                                 ALU.add).then_inc(s2["mskd"], 1)
                    for act in SCHED[q * 32 + mi]:
                        v_inj(act)
                for o2 in range(16):
                    d_idx = q * 16 + o2
                    vector.wait_ge(s1["dmm"], d_idx + 1)
                    if d_idx >= 2:
                        vector.wait_ge(par(s1, "dst", d_idx), 16 * (d_idx // 2))
                    vector.tensor_copy(a_dt[d_idx % 2][:, 0:512], b[3][:, :])
                    vector.tensor_copy(a_dt[d_idx % 2][:, 512:1024],
                                       b[4][:, :]).then_inc(s1["dev"], 1)
                    if q >= 1:
                        tern_unit(1)
                        if q >= 2:
                            tern_unit(1)
                    for act in SCHED[q * 32 + 16 + o2]:
                        v_inj(act)
            tern_unit(NSL)
            for act in SCHED[128]:
                v_inj(act)

    G1OC = []
    for hb_ in range(4):
        for base_ in (0, 16, 32):
            G1OC += [base_ + 4 * hb_ + i_ for i_ in range(4)]

    with nc.Block() as blk:
        @blk.sync
        def _(sync):
            for jj in range(2 * QOC):
                oc = G1OC[jj - QOC] if jj >= QOC else jj
                if jj < 2:
                    sync.wait_ge(s1["g"], 64)
                if jj >= 2:
                    sync.wait_ge(s3["qmm"], jj - 1)
                sync.dma_start(a_ts3[jj % 2][:, :],
                               d_scq.ap()[oc * 128:(oc + 1) * 128, :]).then_inc(par(s3, "tl", jj), 16)

        @blk.tensor
        def _(tensor):
            def t_qkv(jj, g):
                tensor.wait_ge(par(s3, "tl", jj), 16 * (jj // 2 + 1))
                if jj >= 2:
                    tensor.wait_ge(s3["qev"], jj - 1)
                bank = b[0] if jj % 2 == 0 else b[1]
                for fc in range(NF):
                    mm = tensor.matmul(bank[:, :],
                                       a_ts3[jj % 2][:, fc * 128:(fc + 1) * 128],
                                       xn[:, fc * NT + g * 512:fc * NT + g * 512 + 512],
                                       start=(fc == 0), stop=(fc == NF - 1))
                mm.then_inc(s3["qmm"], 1)

            def t_inner(g, hb, cl):
                        bi = g * 16 + hb * 4 + cl
                        hs = [4 * hb + i_ for i_ in range(4)]
                        if bi >= 1:
                            tensor.wait_ge(s3["me"], bi)
                        for i, h in enumerate(hs):
                            ksl = a_qkv[:, (16 + h) * 512 + cl * 128:(16 + h) * 512 + cl * 128 + 128]
                            qsl = a_qkv[:, h * 512 + cl * 128:h * 512 + cl * 128 + 128]
                            mm = tensor.matmul(b[2][:, i * 128:(i + 1) * 128],
                                               ksl, qsl, start=True, stop=True)
                        mm.then_inc(s3["sc"], 1)
                        tensor.wait_ge(s3["kp"], bi + 1)
                        if bi >= 1:
                            tensor.wait_ge(s3["te"], bi)
                        for i, h in enumerate(hs):
                            vsl = a_qkv[:, (32 + h) * 512 + cl * 128:(32 + h) * 512 + cl * 128 + 128]
                            ksl = a_qkv[:, (16 + h) * 512 + cl * 128:(16 + h) * 512 + cl * 128 + 128]
                            tensor.transpose(pt[:, i * 128:(i + 1) * 128], vsl,
                                             idn[:, :])
                            mm = tensor.transpose(pt[:, 512 + i * 128:512 + (i + 1) * 128],
                                                  ksl, idn[:, :])
                        mm.then_inc(s3["tr"], 1)
                        tensor.wait_ge(s3["me"], bi + 1)
                        tensor.wait_ge(s3["te"], bi + 1)
                        if g == 0:
                            tensor.wait_ge(s2["sb"], 4 * hb + 4)
                        if bi >= 1:
                            tensor.wait_ge(s3["oe"], bi)
                        for i, h in enumerate(hs):
                            tensor.matmul(b[4][:, i * 128:(i + 1) * 128],
                                          a_vT3[:, i * 128:(i + 1) * 128],
                                          a_sm[:, i * 128:(i + 1) * 128],
                                          start=True, stop=False)
                            mm = tensor.matmul(b[4][:, i * 128:(i + 1) * 128],
                                               Sb[:, h * 128:(h + 1) * 128],
                                               a_qt[:, i * 128:(i + 1) * 128],
                                               start=False, stop=True)
                        mm.then_inc(s3["ic"], 1)
                        if bi >= 1:
                            tensor.wait_ge(s3["su"], bi)
                        for i in range(4):
                            mm = tensor.matmul(b[3][:, i * 128:(i + 1) * 128],
                                               a_kT3[:, i * 128:(i + 1) * 128],
                                               a_vT3[:, i * 128:(i + 1) * 128],
                                               start=True, stop=True)
                        mm.then_inc(s3["kv"], 1)

            for jj in range(QOC):
                t_qkv(jj, 0)
            tensor.wait_ge(s3["qev"], QOC)
            for hb in range(4):
                for cl in range(4):
                    t_inner(0, hb, cl)
                for r in range(12):
                    t_qkv(QOC + 12 * hb + r, 1)
            tensor.wait_ge(s3["qev"], 2 * QOC)
            for hb in range(4):
                for cl in range(4):
                    t_inner(1, hb, cl)

        @blk.vector
        def _(vector):
            def v_qkv(jj, g):
                oc = G1OC[jj - QOC] if jj >= QOC else jj
                vector.wait_ge(s3["qmm"], jj + 1)
                if jj >= QOC:
                    vector.wait_ge(s3["kv"], 4 * ((jj - QOC) // 12) + 4)
                bank = b[0] if jj % 2 == 0 else b[1]
                vector.tensor_tensor(a_qkv[:, oc * 512:(oc + 1) * 512],
                                     bank[:, :],
                                     bcRQ[:, g * 512:g * 512 + 512],
                                     ALU.mult).then_inc(s3["qev"], 1)

            def v_inner(g, hb, cl):
                        bi = g * 16 + hb * 4 + cl
                        hs = [4 * hb + i_ for i_ in range(4)]
                        for i, h in enumerate(hs):
                            qsl = a_qkv[:, h * 512 + cl * 128:h * 512 + cl * 128 + 128]
                            tt_ = vector.tensor_tensor(a_qt[:, i * 128:(i + 1) * 128],
                                                       qsl,
                                                       lamTt[:, h * 128:(h + 1) * 128],
                                                       ALU.mult)
                        tt_.then_inc(s3["kp"], 1)
                        vector.wait_ge(s3["sc"], bi + 1)
                        for i, h in enumerate(hs):
                            tt_ = vector.tensor_tensor(a_sm[:, i * 128:(i + 1) * 128],
                                                       b[2][:, i * 128:(i + 1) * 128],
                                                       maskT[:, h * 128:(h + 1) * 128],
                                                       ALU.mult)
                        tt_.then_inc(s3["me"], 1)
                        vector.wait_ge(s3["tr"], bi + 1)
                        vector.tensor_copy(a_vT3[:, :], pt[:, 0:512])
                        for i, h in enumerate(hs):
                            tt_ = vector.tensor_scalar(a_kT3[:, i * 128:(i + 1) * 128],
                                                       pt[:, 512 + i * 128:512 + (i + 1) * 128],
                                                       lamRc[:, h:h + 1], None,
                                                       ALU.mult)
                        tt_.then_inc(s3["te"], 1)
                        vector.wait_ge(s3["kv"], bi + 1)
                        for i, h in enumerate(hs):
                            vector.scalar_tensor_tensor(S[:, h * 128:(h + 1) * 128],
                                                        S[:, h * 128:(h + 1) * 128],
                                                        lamTc[:, h:h + 1],
                                                        b[3][:, i * 128:(i + 1) * 128],
                                                        ALU.mult, ALU.add)
                        vector.tensor_copy(Sb[:, 4 * hb * 128:(4 * hb + 4) * 128],
                                           S[:, 4 * hb * 128:(4 * hb + 4) * 128]).then_inc(s3["su"], 1)

            for jj in range(QOC):
                v_qkv(jj, 0)
            for hb in range(4):
                for cl in range(4):
                    v_inner(0, hb, cl)
                for r in range(12):
                    v_qkv(QOC + 12 * hb + r, 1)
            for hb in range(4):
                for cl in range(4):
                    v_inner(1, hb, cl)

        @blk.scalar
        def _(scalar):
            for g in range(2):
                for hb in range(4):
                    for cl in range(4):
                        bi = g * 16 + hb * 4 + cl
                        scalar.wait_ge(s3["ic"], bi + 1)
                        for i, h in enumerate(4 * hb + i_ for i_ in range(4)):
                            dst = a_o16[:, h * NT + g * 512 + cl * 128:
                                        h * NT + g * 512 + cl * 128 + 128]
                            aa = scalar.activation(dst, b[4][:, i * 128:(i + 1) * 128],
                                                   AF.Copy, scale=cst[:, 4:5])
                        aa.then_inc(s3["oe"], 1)

    with nc.Block() as blk:
        @blk.scalar
        def _(scalar):
            for f in range(NF):
                if f >= 1:
                    scalar.wait_ge(s4["mma"], f)
                scalar.activation(a_st[0][:, :], a_o16[:, f * NT:(f + 1) * NT],
                                  AF.Square).then_inc(s4["sqa"], 1)
                if f >= 1:
                    scalar.wait_ge(s4["mmb"], f)
                scalar.activation(a_st[1][:, :], a_o16[:, f * NT:(f + 1) * NT],
                                  AF.Square,
                                  scale=nw_s[:, f:f + 1]).then_inc(s4["sqb"], 1)
            R1 = rows[0:1, 512:1024]
            R2 = rows[0:1, 0:512]
            for half in range(2):
                scalar.wait_ge(s4["r1"], 1 + 4 * half)
                scalar.activation(R2, R1, AF.Sqrt,
                                  bias=cst[0:1, 2:3]).then_inc(s4["r2"], 1)
                scalar.wait_ge(s4["r1"], 4 + 4 * half)
                scalar.activation(R2, R1, AF.Sqrt,
                                  bias=cst[0:1, 1:2]).then_inc(s4["r2"], 1)

        @blk.tensor
        def _(tensor):
            for f in range(NF):
                tensor.wait_ge(s4["sqa"], f + 1)
                st = (f == 0)
                sp = (f == NF - 1)
                tensor.matmul(b[0][0:1, 0:512], onescol[:, :], a_st[0][:, 0:512],
                              start=st, stop=sp)
                tensor.matmul(b[1][0:1, 0:512], onescol[:, :], a_st[0][:, 512:1024],
                              start=st, stop=sp).then_inc(s4["mma"], 1)
                tensor.wait_ge(s4["sqb"], f + 1)
                tensor.matmul(b[2][0:1, 0:512], onescol[:, :], a_st[1][:, 0:512],
                              start=st, stop=sp)
                tensor.matmul(b[3][0:1, 0:512], onescol[:, :], a_st[1][:, 512:1024],
                              start=st, stop=sp).then_inc(s4["mmb"], 1)
            tensor.wait_ge(s4["row"], 1)
            tensor.matmul(b[4][:, 0:512], onesrow[:, :], a_bcRT[0:1, 0:512],
                          start=True, stop=True).then_inc(s4["pb"], 1)
            tensor.wait_ge(s4["bc"], 1)
            tensor.matmul(b[5][:, 0:512], onesrow[:, :], a_bcRT[0:1, 512:1024],
                          start=True, stop=True).then_inc(s4["pb"], 1)
            tensor.wait_ge(s4["bc"], 2)
            for o2 in range(POC):
                tensor.wait_ge(par(s4, "tl", o2), 16 * (o2 // 2 + 1))
                if o2 >= 2:
                    tensor.wait_ge(s4["cmb"], o2 - 1)
                blo = b[0] if o2 % 2 == 0 else b[2]
                bhi = b[1] if o2 % 2 == 0 else b[3]
                for fc in range(NF):
                    st = (fc == 0)
                    sp = (fc == NF - 1)
                    stat = a_ts4[o2 % 2][:, fc * 128:(fc + 1) * 128]
                    tensor.matmul(blo[:, :], stat,
                                  a_o16[:, fc * NT:fc * NT + 512], start=st, stop=sp)
                    mm = tensor.matmul(bhi[:, :], stat,
                                       a_o16[:, fc * NT + 512:(fc + 1) * NT],
                                       start=st, stop=sp)
                mm.then_inc(s4["pm"], 1)

        @blk.vector
        def _(vector):
            vector.wait_ge(s4["mma"], 16)
            vector.wait_ge(s4["mmb"], 16)
            R0 = rows[0:1, 0:512]
            R1 = rows[0:1, 512:1024]
            R2 = rows[0:1, 0:512]
            for half in range(2):
                sA = (b[0] if half == 0 else b[1])[0:1, 0:512]
                sB = (b[2] if half == 0 else b[3])[0:1, 0:512]
                hs = slice(half * 512, half * 512 + 512)
                rrow = a_bcRT[0:1, hs]
                vector.tensor_scalar(R1, sA, 1.0 / 2048.0, None,
                                     ALU.mult).then_inc(s4["r1"], 1)
                vector.wait_ge(s4["r2"], 1 + 2 * half)
                vector.reciprocal(rrow, R2)
                vector.tensor_tensor(R0, rrow, rrow, ALU.mult)
                vector.scalar_tensor_tensor(R1, sB, 1.0 / 2048.0, R0,
                                            ALU.mult, ALU.mult).then_inc(s4["r1"], 3)
                vector.wait_ge(s4["r2"], 2 + 2 * half)
                vector.reciprocal(R0, R2)
                vector.tensor_tensor(R1, rrow, R0, ALU.mult)
                vector.tensor_scalar(rrow, R1, invrow[0:1, 1:2], None, ALU.mult)
            vector.tensor_copy(absr[:, 1:2], onescol[:, :]).then_inc(s4["row"], 1)
            vector.wait_ge(s4["pb"], 1)
            vector.tensor_copy(a_bcRT[:, 0:512], b[4][:, 0:512]).then_inc(s4["bc"], 1)
            vector.wait_ge(s4["pb"], 2)
            vector.tensor_copy(a_bcRT[:, 512:1024], b[5][:, 0:512]).then_inc(s4["bc"], 1)
            for o2 in range(POC):
                vector.wait_ge(s4["pm"], o2 + 1)
                vector.wait_ge(par(s4, "ld", o2), 80 * (o2 // 2 + 1))
                if o2 >= 2:
                    vector.wait_ge(par(s4, "st", o2), 16 * (o2 // 2))
                pb5 = a_p5[o2 % 2]
                blo = b[0] if o2 % 2 == 0 else b[2]
                bhi = b[1] if o2 % 2 == 0 else b[3]
                for half in range(2):
                    bank = blo if half == 0 else bhi
                    hs = slice(half * 512, half * 512 + 512)
                    vector.tensor_tensor(
                        a_pt1[:, :], pb5[:, NT + half * 512:NT + half * 512 + 512],
                        pb5[:, 2 * NT + half * 512:2 * NT + half * 512 + 512], ALU.add)
                    vector.tensor_tensor(
                        a_pt2[:, :], pb5[:, 3 * NT + half * 512:3 * NT + half * 512 + 512],
                        pb5[:, 4 * NT + half * 512:4 * NT + half * 512 + 512], ALU.add)
                    vector.tensor_tensor(a_pt1[:, :], a_pt1[:, :], a_pt2[:, :],
                                         ALU.add)
                    vector.tensor_tensor(a_pt2[:, :], bank[:, :], a_bcRT[:, hs],
                                         ALU.mult)
                    vector.scalar_tensor_tensor(a_pt1[:, :], a_pt1[:, :],
                                                sd_s[:, o2:o2 + 1], a_pt2[:, :],
                                                ALU.mult, ALU.add)
                    mm = vector.tensor_tensor(a_ob[o2 % 2][:, hs], a_pt1[:, :],
                                              pb5[:, half * 512:half * 512 + 512],
                                              ALU.add)
                mm.then_inc(s4["cmb"], 1)

        @blk.gpsimd
        def _(gpsimd):
            gpsimd.wait_ge(s3["su"], 32)
            for o2 in range(1, POC, 2):
                if o2 >= 3:
                    gpsimd.wait_ge(s4["cmb"], o2 - 1)
                pb5 = a_p5[o2 % 2]
                gpsimd.dma_start(pb5[:, 0:NT],
                                 d_x.ap()[o2 * 128:(o2 + 1) * 128, :]).then_inc(par(s4, "ld", o2), 16)
                for q in range(4):
                    gpsimd.dma_start(pb5[:, (q + 1) * NT:(q + 2) * NT],
                                     d_scf.ap()[q * 2048 + o2 * 128:
                                                q * 2048 + (o2 + 1) * 128, :]).then_inc(par(s4, "ld", o2), 16)

        @blk.sync
        def _(sync):
            sync.wait_ge(s1["dst0"], 16 * 32)
            sync.wait_ge(s1["dst1"], 16 * 32)
            for o2 in range(POC + 2):
                if o2 < POC:
                    if o2 % 2 == 0:
                        if o2 < 2:
                            sync.wait_ge(s3["su"], 32)
                        if o2 >= 2:
                            sync.wait_ge(s4["cmb"], o2 - 1)
                        pb5 = a_p5[o2 % 2]
                        sync.dma_start(pb5[:, 0:NT],
                                       d_x.ap()[o2 * 128:(o2 + 1) * 128, :]).then_inc(par(s4, "ld", o2), 16)
                        for q in range(4):
                            sync.dma_start(pb5[:, (q + 1) * NT:(q + 2) * NT],
                                           d_scf.ap()[q * 2048 + o2 * 128:
                                                      q * 2048 + (o2 + 1) * 128, :]).then_inc(par(s4, "ld", o2), 16)
                    if o2 < 2:
                        sync.wait_ge(s3["qmm"], 96)
                    if o2 >= 2:
                        sync.wait_ge(s4["pm"], o2 - 1)
                    pos = TPOS[("p", o2)]
                    sync.wait_ge(par(s1, "ts", pos), 16 * (pos // 2 + 1))
                    sync.dma_start(a_ts4[o2 % 2][:, :],
                                   d_scp.ap()[o2 * 128:(o2 + 1) * 128, :]).then_inc(par(s4, "tl", o2), 16)
                if o2 >= 2:
                    j = o2 - 2
                    sync.wait_ge(s4["cmb"], j + 1)
                    sync.dma_start(d_out.ap()[j * 128:(j + 1) * 128, :],
                                   a_ob[j % 2][:, :]).then_inc(par(s4, "st", j), 16)
            sync.wait_ge(s4["st0"], 16 * 8)
            sync.wait_ge(s4["st1"], 16 * 8)

    ctx.pop_all()
    nc.finalize()
    return nc


_PROG = None


def _host_inputs(L0_attn_x, L0_attn_qkv_w, L0_attn_qkv_norm_w, L0_attn_proj_w,
                 L0_attn_proj_norm_w, L0_attn_norm_w, L0_attn_s, L0_ffn_w_gate,
                 L0_ffn_w_up, L0_ffn_w_down, L0_ffn_s_gate, L0_ffn_s_up,
                 L0_ffn_s_down):
    f32 = np.float32
    x = np.asarray(L0_attn_x, f32)
    qkv_w = np.asarray(L0_attn_qkv_w, f32)
    proj_w = np.asarray(L0_attn_proj_w, f32)
    wg = np.asarray(L0_ffn_w_gate, np.int8)
    wu = np.asarray(L0_ffn_w_up, np.int8)
    wd = np.asarray(L0_ffn_w_down, np.int8)

    qkvt = np.ascontiguousarray(
        qkv_w.reshape(48, 128, 16, 128).transpose(0, 3, 2, 1).reshape(48 * 128, 2048))
    projt = np.ascontiguousarray(
        proj_w.reshape(16, 128, 16, 128).transpose(0, 3, 2, 1).reshape(16 * 128, 2048))
    wgr = np.ascontiguousarray(
        wg.reshape(16, 128, 64, 128).transpose(2, 1, 0, 3).reshape(64 * 128, 2048))
    wur = np.ascontiguousarray(
        wu.reshape(16, 128, 64, 128).transpose(2, 1, 0, 3).reshape(64 * 128, 2048))
    wdr = np.ascontiguousarray(
        wd.reshape(4, 16, 128, 16, 128).transpose(0, 3, 2, 1, 4).reshape(64 * 128, 2048))

    col16 = lambda v: np.ascontiguousarray(np.asarray(v, f32).reshape(16, 128).T)
    qn = col16(L0_attn_qkv_norm_w)
    nw = col16(L0_attn_norm_w)
    pnw = col16(L0_attn_proj_norm_w)
    sd = np.ascontiguousarray(np.asarray(L0_ffn_s_down, f32).reshape(16, 128).T)
    sg = np.ascontiguousarray(np.asarray(L0_ffn_s_gate, f32).reshape(64, 128).T)
    su = np.ascontiguousarray(np.asarray(L0_ffn_s_up, f32).reshape(64, 128).T)
    s16 = np.ascontiguousarray(np.tile(np.asarray(L0_attn_s, f32), (128, 1)))

    ar = np.arange(128, dtype=f32)
    cDm = np.ascontiguousarray(ar[None, :] - ar[:, None])
    cPENm = np.ascontiguousarray(np.where(cDm >= 0, 0.0, -1e30).astype(f32))
    cTm = np.ascontiguousarray(np.broadcast_to(ar, (128, 128)))
    cRm = np.ascontiguousarray((128.0 - cTm).astype(f32))
    idn = np.eye(128, dtype=f32).astype(ml_dtypes.bfloat16)
    cstm = np.zeros((128, 8), f32)
    cstm[:, 0] = 1e-5
    cstm[:, 1] = EPS32
    cstm[:, 2] = 1e-6
    cstm[:, 3] = -128.0
    cstm[:, 4] = SCALE
    cstm[:, 5] = 128.0 - ar

    shared = dict(qkvt=qkvt, projt=projt, wgr=wgr, wur=wur, wdr=wdr, qn=qn,
                  nw=nw, pnw=pnw, sdv=sd, sgv=sg, suv=su, s16=s16, cD=cDm,
                  cPEN=cPENm, cT=cTm, cR=cRm, idn=idn, cstv=cstm)
    zeros_prev = np.zeros((2048, 1024), f32)
    in_maps = []
    for c in range(NCORE):
        bb, half = c // 2, c % 2
        xT = np.ascontiguousarray(x[bb, half * 1024:(half + 1) * 1024, :].T)
        if half == 1:
            xpT = np.ascontiguousarray(x[bb, 0:1024, :].T)
        else:
            xpT = zeros_prev
        m = dict(shared)
        m["xT"] = xT
        m["xprevT"] = xpT
        in_maps.append(m)
    return in_maps


def kernel(**inputs):
    global _PROG
    in_maps = _host_inputs(**inputs)
    if _PROG is None:
        _PROG = build_program()
    res = run_bass_kernel_spmd(_PROG, in_maps, core_ids=list(range(NCORE)))
    out = np.empty((4, 2048, 2048), np.float32)
    for c in range(NCORE):
        bb, half = c // 2, c % 2
        out[bb, half * 1024:(half + 1) * 1024, :] = res.results[c]["outT"].T
    return out


# version 13
# speedup vs baseline: 1.0761x; 1.0257x over previous
import numpy as np
import ml_dtypes
import concourse.bass as bass
import concourse.mybir as mybir
from concourse.bass_utils import run_bass_kernel_spmd

F32 = mybir.dt.float32
BF16 = mybir.dt.bfloat16
I8 = mybir.dt.int8
U8 = mybir.dt.uint8
AF = mybir.ActivationFunctionType
ALU = mybir.AluOpType
AX = mybir.AxisListType

NCORE = 8
D = 2048
NT = 1024
H = 16
NF = 16
QOC = 48
POC = 16
EPS32 = float(np.finfo(np.float32).eps)
SCALE = float(128.0 ** -0.5)
QKV_N = float(6144 * 2048)
PROJ_N = float(2048 * 2048)


class Arena:
    def __init__(self, nc, backing):
        self.nc = nc
        self.base = nc.lookup_mloc(backing).addr
        self.cap = backing.shape[1]

    def at(self, name, shape, dtype, off):
        per = 1
        for s in shape[1:]:
            per *= s
        per *= mybir.dt.size(dtype)
        assert off % 32 == 0 and off + per <= self.cap, (name, off, per, self.cap)
        return self.nc.alloc_sbuf_tensor_at(name, shape, dtype,
                                            offset=self.base + off)


def build_program():
    nc = bass.Bass("TRN2", target_bir_lowering=False,
                   detect_race_conditions=False)

    d_x = nc.dram_tensor("xT", [D, NT], F32, kind="ExternalInput")
    d_xp = nc.dram_tensor("xprevT", [D, NT], F32, kind="ExternalInput")
    d_qkv = nc.dram_tensor("qkvt", [QOC * 128, 2048], F32, kind="ExternalInput")
    d_proj = nc.dram_tensor("projt", [POC * 128, 2048], F32, kind="ExternalInput")
    d_wg = nc.dram_tensor("wgr", [64 * 128, 2048], I8, kind="ExternalInput")
    d_wu = nc.dram_tensor("wur", [64 * 128, 2048], I8, kind="ExternalInput")
    d_wd = nc.dram_tensor("wdr", [64 * 128, 2048], I8, kind="ExternalInput")
    d_qn = nc.dram_tensor("qn", [128, 16], F32, kind="ExternalInput")
    d_nw = nc.dram_tensor("nw", [128, 16], F32, kind="ExternalInput")
    d_pnw = nc.dram_tensor("pnw", [128, 16], F32, kind="ExternalInput")
    d_sd = nc.dram_tensor("sdv", [128, 16], F32, kind="ExternalInput")
    d_sg = nc.dram_tensor("sgv", [128, 64], F32, kind="ExternalInput")
    d_su = nc.dram_tensor("suv", [128, 64], F32, kind="ExternalInput")
    d_s16 = nc.dram_tensor("s16", [128, 16], F32, kind="ExternalInput")
    d_cD = nc.dram_tensor("cD", [128, 128], F32, kind="ExternalInput")
    d_cPEN = nc.dram_tensor("cPEN", [128, 128], F32, kind="ExternalInput")
    d_cT = nc.dram_tensor("cT", [128, 128], F32, kind="ExternalInput")
    d_cR = nc.dram_tensor("cR", [128, 128], F32, kind="ExternalInput")
    d_idn = nc.dram_tensor("idn", [128, 128], BF16, kind="ExternalInput")
    d_cst = nc.dram_tensor("cstv", [128, 8], F32, kind="ExternalInput")
    d_out = nc.dram_tensor("outT", [D, NT], F32, kind="ExternalOutput")
    d_scq = nc.dram_tensor("scr_q", [QOC * 128, 2048], BF16)
    d_scp = nc.dram_tensor("scr_p", [POC * 128, 2048], BF16)
    d_scf = nc.dram_tensor("scr_f", [4 * 2048, NT], F32)

    from contextlib import ExitStack
    ctx = ExitStack()
    sb = lambda name, shape, dt: ctx.enter_context(nc.sbuf_tensor(name, shape, dt))
    ps = lambda name, shape, dt: ctx.enter_context(nc.psum_tensor(name, shape, dt))
    sem = lambda name: ctx.enter_context(nc.semaphore(name))

    xn = sb("xn", [128, NF * NT], BF16)
    bcRQ = sb("bcRQ", [128, NT], F32)
    S = sb("S", [128, H * 128], F32)
    Sb = sb("Sbuf16", [128, H * 128], BF16)
    maskT = sb("maskT", [128, H * 128], BF16)
    lamTt = sb("lamTt", [128, H * 128], BF16)
    lamRt = sb("lamRt", [128, H * 128], BF16)
    lamTc = sb("lamTc", [128, H], F32)
    negs = sb("negsv", [128, H], F32)
    s16 = sb("s16v", [128, H], F32)
    qn_s = sb("qn_s", [128, 16], F32)
    nw_s = sb("nw_s", [128, 16], F32)
    pnw_s = sb("pnw_s", [128, 16], F32)
    fw_s = sb("fw_s", [128, 16], F32)
    sd_s = sb("sd_s", [128, 16], F32)
    sg_s = sb("sg_s", [128, 64], F32)
    su_s = sb("su_s", [128, 64], F32)
    cD = sb("cDs", [128, 128], F32)
    cPEN = sb("cPENs", [128, 128], F32)
    cT = sb("cTs", [128, 128], F32)
    cR = sb("cRs", [128, 128], F32)
    idn = sb("idns", [128, 128], BF16)
    ones_t = sb("ones_t", [128, 128], F32)
    onescol = sb("onescol", [128, 1], F32)
    onesrow = sb("onesrow", [1, 128], F32)
    cst = sb("cstt", [128, 8], F32)
    lamRc = sb("lamRc", [128, H], F32)
    rows = sb("rowscr", [1, 1024], F32)
    c_qp = sb("c_qp", [128, 2], F32)
    negc_qp = sb("negc_qp", [128, 2], F32)
    absq = sb("absq", [128, 48], F32)
    absp = sb("absp", [128, 16], F32)
    absr = sb("absr", [128, 2], F32)
    invrow = sb("invrow", [1, 2], F32)
    bcRQp = sb("bcRQp", [128, NT], F32)

    AR_BYTES = 138240
    backing = sb("arena", [128, AR_BYTES], U8)
    ar = Arena(nc, backing)
    a_x16 = ar.at("x16", [128, NF * NT], F32, 0)
    a_sq = [ar.at("sq", [128, NT], F32, 65536),
            ar.at("sq2", [128, NT], F32, 69632)]
    a_bcA = ar.at("bcA", [128, NT], F32, 73728)
    a_bcB = ar.at("bcB", [128, NT], F32, 77824)
    a_fuse = ar.at("fuse", [128, 16 * NT], BF16, 0)
    a_tb = [ar.at("tb", [128, 2048], F32, 32768),
            ar.at("tb2", [128, 2048], F32, 40960)]
    a_tern = [ar.at("tern", [128, 2048], BF16, 49152),
              ar.at("tern2", [128, 2048], BF16, 53248)]
    a_gs = ar.at("gsil", [128, NT], BF16, 57344)
    a_dt = [ar.at("dt", [128, NT], F32, 59392),
            ar.at("dt2", [128, NT], F32, 63488)]
    a_abs = ar.at("absdump", [128, 2048], BF16, 67584)
    a_wd = [ar.at("wd", [128, 2048], BF16, 71680),
            ar.at("wd2", [128, 2048], BF16, 75776)]
    a_wg = [ar.at("wg", [128, 2048], BF16, 81920),
            ar.at("wg2", [128, 2048], BF16, 86016)]
    a_wu = [ar.at("wu", [128, 2048], BF16, 90112),
            ar.at("wu2", [128, 2048], BF16, 94208)]
    a_xpr = [ar.at("xpr", [128, NT], F32, 98304),
             ar.at("xpr2", [128, NT], F32, 102400)]
    a_sqp = [ar.at("sqp", [128, NT], F32, 106496),
             ar.at("sqp2", [128, NT], F32, 110592)]
    a_ksb = [ar.at("ksb0", [128, 512], BF16, 98304),
             ar.at("ksb1", [128, 512], BF16, 99328)]
    a_vsb = [ar.at("vsb0", [128, 512], BF16, 100352),
             ar.at("vsb1", [128, 512], BF16, 101376)]
    a_xpb = [ar.at("xpb", [128, 512], F32, 102400),
             ar.at("xpb2", [128, 512], F32, 104448)]
    a_xnp = ar.at("xnp", [128, 16 * 512], BF16, 106496)
    a_ts2 = [ar.at("ts2a", [128, 2048], BF16, 122880),
             ar.at("ts2b", [128, 2048], BF16, 126976)]
    a_bcAp = ar.at("bcAp", [128, NT], F32, 122880)
    a_bcBp = ar.at("bcBp", [128, NT], F32, 126976)
    a_patq = ar.at("patq", [128, 2048], BF16, 131072)
    a_vT = ar.at("vT", [128, 512], BF16, 135168)
    a_kT = ar.at("kT", [128, 512], BF16, 136192)
    a_mtmp = [ar.at("mtmp", [128, 128], F32, 137216),
              ar.at("mtmp2", [128, 128], F32, 137728)]
    a_o16 = ar.at("o16", [128, NF * NT], BF16, 0)
    a_qkv = ar.at("qkvb", [128, QOC * 512], BF16, 32768)
    a_ts3 = [ar.at("ts3a", [128, 2048], BF16, 81920),
             ar.at("ts3b", [128, 2048], BF16, 86016)]
    a_qt = ar.at("qtmp", [128, 512], BF16, 90112)
    a_sm = ar.at("scoresm", [128, 512], BF16, 91136)
    a_vT3 = ar.at("vT3", [128, 512], BF16, 92160)
    a_kT3 = ar.at("kT3", [128, 512], BF16, 93184)
    a_bcRT = ar.at("bcRT", [128, NT], F32, 32768)
    a_st = [ar.at("stmp", [128, NT], F32, 36864),
            ar.at("stmp2", [128, NT], F32, 40960)]
    a_ob = [ar.at("ob", [128, NT], F32, 36864),
            ar.at("ob2", [128, NT], F32, 40960)]
    a_p5 = [ar.at("p5a", [128, 5 * NT], F32, 45056),
            ar.at("p5b", [128, 5 * NT], F32, 65536)]
    a_ts4 = [ar.at("ts4a", [128, 2048], BF16, 86016),
             ar.at("ts4b", [128, 2048], BF16, 90112)]
    a_pt1 = ar.at("ptmp1", [128, 512], F32, 94208)
    a_pt2 = ar.at("ptmp2", [128, 512], F32, 96256)

    b = [ps(f"pb{i}", [128, 512], F32) for i in range(7)]
    b7 = b[6]
    pt = ps("ptr", [128, 1024], BF16)

    s0 = {k: sem("s0_" + k) for k in
          ["ld", "ini", "xq0", "xq1", "xq2", "xq3", "sq", "mm", "r1", "r2",
           "row", "pb", "bc", "xn", "neg"]}
    s1 = {k: sem("s1_" + k) for k in
          ["tl0", "tl1", "ta", "ar2", "ab", "cr", "rqr", "pb", "bce", "tt",
           "ts0", "ts1", "gw0", "gw1", "uw0", "uw1", "dw0", "dw1",
           "g", "gel", "gev", "u", "uev", "dmm", "dev", "dst0", "dst1"]}
    s2 = {k: sem("s2_" + k) for k in
          ["mskd", "mska", "sq", "mm", "r1", "r2", "row", "pb", "bc",
           "xd0", "xd1", "pd0", "pd1", "xn", "tl0", "tl1",
           "kmm", "kev", "tr", "te", "kvm", "su", "sb", "iv", "pe0", "pe1"]}
    s3 = {k: sem("s3_" + k) for k in
          ["tl0", "tl1", "qmm", "qev", "kp", "sc", "me", "tr", "te", "ic",
           "oe", "kv", "su"]}
    s4 = {k: sem("s4_" + k) for k in
          ["sqa", "sqb", "mma", "mmb", "r1", "r2", "row", "pb", "bc",
           "tl0", "tl1", "pm", "ld0", "ld1", "cmb", "st0", "st1"]}

    def par(sd, base, i):
        return sd[base + str(i % 2)]

    def stats_mms(tensor, smm, ssq, x_of, sq_of):
        for f in range(NF):
            tensor.wait_ge(ssq, f + 1)
            st = (f == 0)
            sp = (f == NF - 1)
            xs = x_of(f)
            ss_ = sq_of(f)
            tensor.matmul(b[0][0:1, 0:512], onescol[:, :], xs[:, 0:512],
                          start=st, stop=sp)
            tensor.matmul(b[1][0:1, 0:512], onescol[:, :], xs[:, 512:1024],
                          start=st, stop=sp)
            tensor.matmul(b[2][0:1, 0:512], onescol[:, :], ss_[:, 0:512],
                          start=st, stop=sp)
            tensor.matmul(b[3][0:1, 0:512], onescol[:, :], ss_[:, 512:1024],
                          start=st, stop=sp).then_inc(smm, 1)

    def row_chain_dve(vector, ss, bcA_, bcB_, bcRQ_):
        R0 = rows[0:1, 0:512]
        R1 = rows[0:1, 512:1024]
        R2 = rows[0:1, 0:512]
        for half in range(2):
            sx = (b[0] if half == 0 else b[1])[0:1, 0:512]
            sxx = (b[2] if half == 0 else b[3])[0:1, 0:512]
            hs = slice(half * 512, half * 512 + 512)
            mrow = bcB_[0:1, hs]
            rln = bcA_[0:1, hs]
            vector.tensor_scalar(mrow, sx, 1.0 / 2048.0, None, ALU.mult)
            vector.tensor_tensor(R0, mrow, mrow, ALU.mult)
            vector.scalar_tensor_tensor(R1, sxx, 1.0 / 2048.0, R0,
                                        ALU.mult, ALU.subtract).then_inc(ss["r1"], 3)
            vector.wait_ge(ss["r2"], 1 + 2 * half)
            vector.reciprocal(rln, R2)
            vector.tensor_tensor(R0, R1, rln, ALU.mult)
            vector.tensor_tensor(R1, R0, rln, ALU.mult).then_inc(ss["r1"], 3)
            vector.wait_ge(ss["r2"], 2 + 2 * half)
            vector.reciprocal(bcRQ_[0:1, hs], R2)
            vector.tensor_tensor(R0, mrow, rln, ALU.mult)
            vector.tensor_copy(mrow, R0)

    def row_chain_act(scalar, ss, epscol):
        R1 = rows[0:1, 512:1024]
        R2 = rows[0:1, 0:512]
        for half in range(2):
            scalar.wait_ge(ss["r1"], 3 + 6 * half)
            scalar.activation(R2, R1, AF.Sqrt,
                              bias=cst[0:1, epscol:epscol + 1]).then_inc(ss["r2"], 1)
            scalar.wait_ge(ss["r1"], 6 + 6 * half)
            scalar.activation(R2, R1, AF.Sqrt,
                              bias=cst[0:1, 1:2]).then_inc(ss["r2"], 1)

    with nc.Block() as blk:
        @blk.sync
        def _(sync):
            for dd, ss_ in [(d_qn, qn_s), (d_nw, nw_s), (d_pnw, pnw_s),
                            (d_sd, sd_s), (d_sg, sg_s), (d_su, su_s),
                            (d_s16, s16), (d_cD, cD), (d_cPEN, cPEN),
                            (d_cT, cT), (d_cR, cR), (d_idn, idn),
                            (d_cst, cst)]:
                sync.dma_start(ss_[:, :], dd.ap()).then_inc(s0["ld"], 16)
            for f in range(NF):
                sync.dma_start(a_x16[:, f * NT:(f + 1) * NT],
                               d_x.ap()[f * 128:(f + 1) * 128, :]).then_inc(
                                   s0["xq" + str(f // 4)], 16)
            for f in range(NF):
                if f >= 2:
                    sync.wait_ge(s2["mm"], f - 1)
                sync.dma_start(a_xpr[f % 2][:, :],
                               d_xp.ap()[f * 128:(f + 1) * 128, :]).then_inc(par(s2, "xd", f), 16)

        @blk.gpsimd
        def _(gpsimd):
            gpsimd.memset(ones_t[:, :], 1.0)
            gpsimd.memset(onescol[:, :], 1.0)
            gpsimd.memset(onesrow[:, :], 1.0)
            gpsimd.memset(S[:, :], 0.0)
            gpsimd.memset(Sb[:, :], 0.0).then_inc(s0["ini"], 1)
            gpsimd.dma_start(a_wg[0][:, :],
                             d_wg.ap()[0:128, :]).then_inc(s1["gw0"], 16)
            gpsimd.dma_start(a_wu[0][:, :],
                             d_wu.ap()[0:128, :]).then_inc(s1["uw0"], 16)

        @blk.scalar
        def _(scalar):
            for f in range(NF):
                scalar.wait_ge(s0["xq" + str(f // 4)], 64)
                if f >= 2:
                    scalar.wait_ge(s0["mm"], f - 1)
                scalar.activation(a_sq[f % 2][:, :], a_x16[:, f * NT:(f + 1) * NT],
                                  AF.Square).then_inc(s0["sq"], 1)
            scalar.wait_ge(s0["ld"], 208)
            row_chain_act(scalar, s0, 0)
            for f in range(NF):
                scalar.wait_ge(par(s2, "xd", f), 16 * (f // 2 + 1))
                if f >= 2:
                    scalar.wait_ge(s2["mm"], f - 1)
                scalar.activation(a_sqp[f % 2][:, :], a_xpr[f % 2][:, :],
                                  AF.Square).then_inc(s2["sq"], 1)
            row_chain_act(scalar, s2, 0)
            scalar.wait_ge(s0["neg"], 1)
            for h in range(H):
                scalar.activation(lamRc[:, h:h + 1], cst[:, 5:6], AF.Exp,
                                  scale=negs[:, h:h + 1])
                scalar.activation(lamTc[:, h:h + 1], s16[:, h:h + 1], AF.Exp,
                                  scale=cst[:, 3:4])

        @blk.tensor
        def _(tensor):
            tensor.wait_ge(s0["ini"], 1)
            stats_mms(tensor, s0["mm"], s0["sq"],
                      lambda f: a_x16[:, f * NT:(f + 1) * NT],
                      lambda f: a_sq[f % 2])
            tensor.wait_ge(s0["row"], 1)
            tensor.matmul(b[4][:, 0:512], onesrow[:, :], a_bcA[0:1, 0:512],
                          start=True, stop=True)
            tensor.matmul(b[5][:, 0:512], onesrow[:, :], a_bcA[0:1, 512:1024],
                          start=True, stop=True).then_inc(s0["pb"], 1)
            tensor.wait_ge(s0["bc"], 1)
            tensor.matmul(b[4][:, 0:512], onesrow[:, :], a_bcB[0:1, 0:512],
                          start=True, stop=True)
            tensor.matmul(b[5][:, 0:512], onesrow[:, :], a_bcB[0:1, 512:1024],
                          start=True, stop=True).then_inc(s0["pb"], 1)
            stats_mms(tensor, s2["mm"], s2["sq"],
                      lambda f: a_xpr[f % 2], lambda f: a_sqp[f % 2])
            tensor.wait_ge(s2["row"], 1)
            for i, tile in enumerate([a_bcAp, a_bcBp]):
                if i > 0:
                    tensor.wait_ge(s2["bc"], 2 * i - 1)
                tensor.matmul(b[4][:, 0:512], onesrow[:, :], tile[0:1, 0:512],
                              start=True, stop=True).then_inc(s2["pb"], 1)
                tensor.wait_ge(s2["bc"], 2 * i)
                tensor.matmul(b[5][:, 0:512], onesrow[:, :], tile[0:1, 512:1024],
                              start=True, stop=True).then_inc(s2["pb"], 1)

        @blk.vector
        def _(vector):
            vector.wait_ge(s0["ld"], 208)
            vector.wait_ge(s0["ini"], 1)
            vector.tensor_tensor(fw_s[:, :], nw_s[:, :], pnw_s[:, :], ALU.mult)
            vector.tensor_scalar(negs[:, :], s16[:, :], -1.0, None,
                                 ALU.mult).then_inc(s0["neg"], 1)
            vector.wait_ge(s0["mm"], 16)
            row_chain_dve(vector, s0, a_bcA, a_bcB, bcRQ)
            vector.tensor_copy(absr[:, 0:1], onescol[:, :]).then_inc(s0["row"], 1)
            vector.wait_ge(s0["pb"], 1)
            vector.tensor_copy(a_bcA[:, 0:512], b[4][:, 0:512])
            vector.tensor_copy(a_bcA[:, 512:1024], b[5][:, 0:512]).then_inc(s0["bc"], 1)
            vector.wait_ge(s0["pb"], 2)
            vector.tensor_copy(a_bcB[:, 0:512], b[4][:, 0:512])
            vector.tensor_copy(a_bcB[:, 512:1024], b[5][:, 0:512]).then_inc(s0["bc"], 1)
            for f in range(NF):
                vector.tensor_tensor(a_sq[f % 2][:, :], a_x16[:, f * NT:(f + 1) * NT],
                                     a_bcA[:, :], ALU.mult).then_inc(s0["xn"], 1)
                vector.tensor_tensor(xn[:, f * NT:(f + 1) * NT],
                                     a_sq[f % 2][:, :], a_bcB[:, :],
                                     ALU.subtract).then_inc(s0["xn"], 1)
            vector.wait_ge(s2["mm"], 16)
            row_chain_dve(vector, s2, a_bcAp, a_bcBp, bcRQp)
            vector.tensor_copy(absr[:, 1:2], onescol[:, :]).then_inc(s2["row"], 1)
            for i, tile in enumerate([a_bcAp, a_bcBp]):
                vector.wait_ge(s2["pb"], 2 * i + 1)
                vector.tensor_copy(tile[:, 0:512], b[4][:, 0:512]).then_inc(s2["bc"], 1)
                vector.wait_ge(s2["pb"], 2 * i + 2)
                vector.tensor_copy(tile[:, 512:1024], b[5][:, 0:512]).then_inc(s2["bc"], 1)

    TSL = []
    for h_ in range(16):
        TSL.append(("q", 16 + h_))
        TSL.append(("q", 32 + h_))
    TSL += [("q", i) for i in range(16)] + [("p", i) for i in range(POC)]
    NSL = len(TSL)
    TPOS = {kv_: j for j, kv_ in enumerate(TSL)}

    def slab_src(j):
        kind, i = TSL[j]
        dd = d_qkv if kind == "q" else d_proj
        return dd.ap()[i * 128:(i + 1) * 128, :]

    def slab_dst(j):
        kind, i = TSL[j]
        dd = d_scq if kind == "q" else d_scp
        return dd.ap()[i * 128:(i + 1) * 128, :]

    SCHED = [[] for _ in range(129)]
    tern_slots = []
    for q_ in range(4):
        for mi_ in range(16):
            if q_ * 16 + mi_ >= 27:
                tern_slots.append(q_ * 32 + mi_)
        for o2_ in range(16):
            if q_ == 1:
                tern_slots.append(q_ * 32 + 16 + o2_)
            elif q_ >= 2:
                tern_slots += [q_ * 32 + 16 + o2_] * 2
    for i in range(16):
        SCHED[4 + i // 2].append(("xnp", i))
    kv_slot = {}
    for n in range(32):
        kv_slot[n] = tern_slots[n] + 1
        SCHED[kv_slot[n]].append(("kv", n))
    for n in range(32):
        kv_slot[32 + n] = min(94 + n, 127)
        SCHED[kv_slot[32 + n]].append(("kv", 32 + n))
    for i in range(16):
        SCHED[80 + i // 2].append(("xnp", 16 + i))
    ptr = 20
    for P in range(32):
        if P == 16:
            ptr = 96
        g_ = P // 16
        ready = kv_slot[g_ * 32 + 2 * (P % 16) + 1] + 1
        ptr = max(ptr, ready)
        SCHED[min(ptr, 128)].append(("ch", P))
        ptr += 1
    _ORD = {"xnp": 0, "ch": 1, "kv": 2}
    for sl_ in SCHED:
        sl_.sort(key=lambda a: _ORD[a[0]])

    with nc.Block() as blk:
        @blk.sync
        def _(sync):
            for j in range(NSL):
                if j < 2:
                    sync.wait_ge(s0["xn"], 32)
                if j >= 2:
                    sync.wait_ge(s1["ta"], j - 1)
                sync.dma_start(a_tb[j % 2][:, :], slab_src(j)).then_inc(par(s1, "tl", j), 16)
            def ts2_load(n):
                sync.wait_ge(par(s1, "ts", n), 16 * (n // 2 + 1))
                if n < 2:
                    sync.wait_ge(s2["xn"], 64)
                if n >= 2:
                    sync.wait_ge(s2["kmm"], 32 + n - 1)
                kind, oc = TSL[n]
                sync.dma_start(a_ts2[n % 2][:, :],
                               d_scq.ap()[oc * 128:(oc + 1) * 128, :]).then_inc(par(s2, "tl", n), 16)

            for j in range(NSL):
                sync.wait_ge(s1["ta"], min(j + 63, NSL))
                if j >= 2:
                    sync.wait_ge(s1["tt"], j - 1)
                sync.dma_start(a_tb[j % 2][:, :], slab_src(j)).then_inc(par(s1, "tl", j), 16)
            for n in range(32):
                ts2_load(n)

        @blk.gpsimd
        def _(gpsimd):
            for q in range(4):
                for mi in range(16):
                    m = q * 16 + mi
                    if m >= 1:
                        if m >= 2:
                            gpsimd.wait_ge(s1["g"], m - 1)
                        gpsimd.dma_start(a_wg[m % 2][:, :],
                                         d_wg.ap()[m * 128:(m + 1) * 128, :]).then_inc(par(s1, "gw", m), 16)
                        if m >= 2:
                            gpsimd.wait_ge(s1["u"], m - 1)
                        gpsimd.dma_start(a_wu[m % 2][:, :],
                                         d_wu.ap()[m * 128:(m + 1) * 128, :]).then_inc(par(s1, "uw", m), 16)
                    if q == 0 and 3 <= mi <= 10:
                        for i2 in (2 * (mi - 3), 2 * (mi - 3) + 1):
                            gpsimd.wait_ge(s2["mm"], 16)
                            if i2 >= 2:
                                gpsimd.wait_ge(s2["xn"], 2 * (i2 - 1))
                            gpsimd.dma_start(
                                a_xpb[i2 % 2][:, :],
                                d_xp.ap()[i2 * 128:(i2 + 1) * 128,
                                          0:512]).then_inc(par(s2, "pd", i2), 16)
                for o2 in range(16):
                    d_idx = q * 16 + o2
                    if d_idx < 2:
                        gpsimd.wait_ge(s0["xn"], 32)
                    if d_idx >= 2:
                        gpsimd.wait_ge(s1["dmm"], d_idx - 1)
                    gpsimd.dma_start(a_wd[o2 % 2][:, :],
                                     d_wd.ap()[d_idx * 128:(d_idx + 1) * 128, :]).then_inc(par(s1, "dw", d_idx), 16)


        @blk.scalar
        def _(scalar):
            n_abs = [0]
            n_st = [0]

            def abs_unit(k):
                for _ in range(k):
                    if n_abs[0] >= NSL:
                        return
                    j = n_abs[0]
                    scalar.wait_ge(par(s1, "tl", j), 16 * (j // 2 + 1))
                    kind, i = TSL[j]
                    acc = absq[:, i:i + 1] if kind == "q" else absp[:, i:i + 1]
                    scalar.activation(a_abs[:, :], a_tb[j % 2][:, :], AF.Abs,
                                      accum_out=acc).then_inc(s1["ta"], 1)
                    n_abs[0] += 1

            def store_unit(k):
                for _ in range(k):
                    if n_st[0] >= NSL:
                        return
                    j = n_st[0]
                    scalar.wait_ge(s1["tt"], j + 1)
                    scalar.dma_start(slab_dst(j), a_tern[j % 2][:, :]).then_inc(par(s1, "ts", j), 16)
                    n_st[0] += 1

            for q in range(4):
                for mi in range(16):
                    m = q * 16 + mi
                    if m >= 1:
                        scalar.wait_ge(s1["uev"], m)
                    scalar.wait_ge(s1["g"], m + 1)
                    scalar.activation(a_gs[:, 0:512], b[0][:, 0:512], AF.Silu,
                                      scale=sg_s[:, m:m + 1]).then_inc(s1["gel"], 1)
                    scalar.activation(a_gs[:, 512:1024], b[1][:, 0:512], AF.Silu,
                                      scale=sg_s[:, m:m + 1]).then_inc(s1["gev"], 1)
                    if m < 16:
                        abs_unit(3)
                    elif m >= 28:
                        store_unit(1)
                    if 28 <= m < 44:
                        for h in (m - 28,):
                            scalar.wait_ge(s2["mskd"], h + 1)
                            scalar.activation(maskT[:, h * 128:(h + 1) * 128],
                                              a_mtmp[h % 2][:, :],
                                              AF.Exp).then_inc(s2["mska"], 1)
                            scalar.activation(lamTt[:, h * 128:(h + 1) * 128],
                                              cT[:, :], AF.Exp,
                                              scale=negs[:, h:h + 1])
                            scalar.activation(lamRt[:, h * 128:(h + 1) * 128],
                                              cR[:, :], AF.Exp,
                                              scale=negs[:, h:h + 1])
                for o2 in range(16):
                    d_idx = q * 16 + o2
                    scalar.wait_ge(s1["dev"], d_idx + 1)
                    scalar.dma_start(d_scf.ap()[d_idx * 128:(d_idx + 1) * 128, :],
                                     a_dt[d_idx % 2][:, :]).then_inc(par(s1, "dst", d_idx), 16)
                    if q == 0:
                        abs_unit(1)
                    else:
                        store_unit(1)
                        if q >= 2:
                            store_unit(1)
                    if q == 2 and o2 < 8:
                        for i2 in (16 + 2 * o2, 16 + 2 * o2 + 1):
                            i = i2 - 16
                            scalar.wait_ge(s2["xn"], 2 * (i2 - 1))
                            scalar.dma_start(
                                a_xpb[i2 % 2][:, :],
                                d_xp.ap()[i * 128:(i + 1) * 128,
                                          512:1024]).then_inc(par(s2, "pe", i2), 16)
            store_unit(NSL)

        @blk.tensor
        def _(tensor):
            def t_kv(n):
                g, sl = n // 32, n % 32
                if g == 0:
                    tensor.wait_ge(s1["tt"], sl + 1)
                else:
                    tensor.wait_ge(par(s2, "tl", sl), 16 * (sl // 2 + 1))
                tensor.wait_ge(s2["xn"], 32 * (g + 1))
                if n == 0:
                    tensor.wait_ge(s2["bc"], 4)
                if n >= 1:
                    tensor.wait_ge(s2["kev"], n)
                src = a_tern[sl % 2] if g == 0 else a_ts2[sl % 2]
                for fc in range(NF):
                    mm = tensor.matmul(b[5][:, :], src[:, fc * 128:(fc + 1) * 128],
                                       a_xnp[:, fc * 512:(fc + 1) * 512],
                                       start=(fc == 0), stop=(fc == NF - 1))
                mm.then_inc(s2["kmm"], 1)

            def t_ch(P):
                tensor.wait_ge(s2["kev"], 2 * P + 2)
                if P >= 1:
                    tensor.wait_ge(s2["te"], P)
                for cl in range(4):
                    tensor.transpose(pt[:, cl * 128:(cl + 1) * 128],
                                     a_vsb[P % 2][:, cl * 128:(cl + 1) * 128],
                                     idn[:, :])
                    mm = tensor.transpose(pt[:, 512 + cl * 128:512 + (cl + 1) * 128],
                                          a_ksb[P % 2][:, cl * 128:(cl + 1) * 128],
                                          idn[:, :])
                mm.then_inc(s2["tr"], 1)
                tensor.wait_ge(s2["te"], P + 1)
                if P >= 1:
                    tensor.wait_ge(s2["su"], P)
                for cl in range(4):
                    mm = tensor.matmul(b[6][:, cl * 128:(cl + 1) * 128],
                                       a_kT[:, cl * 128:(cl + 1) * 128],
                                       a_vT[:, cl * 128:(cl + 1) * 128],
                                       start=True, stop=True)
                mm.then_inc(s2["kvm"], 1)

            def t_inj(act):
                if act[0] == "kv":
                    t_kv(act[1])
                elif act[0] == "ch":
                    t_ch(act[1])

            for q in range(4):
                for mi in range(16):
                    m = q * 16 + mi
                    tensor.wait_ge(par(s1, "gw", m), 16 * (m // 2 + 1))
                    if m == 0:
                        tensor.wait_ge(s0["xn"], 32)
                        tensor.wait_ge(s2["row"], 1)
                    if m >= 1:
                        tensor.wait_ge(s1["gev"], m)
                    for f in range(NF):
                        st = (f == 0)
                        sp = (f == NF - 1)
                        stat = a_wg[m % 2][:, f * 128:(f + 1) * 128]
                        mm = tensor.matmul(b[1][:, :], stat,
                                           xn[:, f * NT + 512:f * NT + 1024],
                                           start=st, stop=sp)
                    if m >= 1:
                        tensor.wait_ge(s1["uev"], m)
                    for f in range(NF):
                        st = (f == 0)
                        sp = (f == NF - 1)
                        stat = a_wg[m % 2][:, f * 128:(f + 1) * 128]
                        mm = tensor.matmul(b[0][:, :], stat, xn[:, f * NT:f * NT + 512],
                                           start=st, stop=sp)
                    mm.then_inc(s1["g"], 1)
                    tensor.wait_ge(par(s1, "uw", m), 16 * (m // 2 + 1))
                    if m >= 1:
                        tensor.wait_ge(s1["uev"], m)
                    for f in range(NF):
                        st = (f == 0)
                        sp = (f == NF - 1)
                        stat = a_wu[m % 2][:, f * 128:(f + 1) * 128]
                        mm = tensor.matmul(b[2][:, :], stat, xn[:, f * NT:f * NT + 512],
                                           start=st, stop=sp)
                    tensor.wait_ge(s1["gel"], m + 1)
                    for f in range(NF):
                        st = (f == 0)
                        sp = (f == NF - 1)
                        stat = a_wu[m % 2][:, f * 128:(f + 1) * 128]
                        mm = tensor.matmul(b[0][:, :], stat,
                                           xn[:, f * NT + 512:f * NT + 1024],
                                           start=st, stop=sp)
                    mm.then_inc(s1["u"], 1)
                    if m == 21:
                        tensor.wait_ge(s1["ar2"], 1)
                        tensor.matmul(b[6][0:1, 0:1], onescol[:, :],
                                      absr[:, 0:1], start=True, stop=True)
                        tensor.matmul(b[6][0:1, 1:2], onescol[:, :],
                                      absr[:, 1:2], start=True,
                                      stop=True).then_inc(s1["ab"], 1)
                    if m == 22:
                        tensor.wait_ge(s1["cr"], 1)
                        tensor.matmul(b[6][:, 0:2], onesrow[:, :],
                                      invrow[0:1, 0:2], start=True,
                                      stop=True).then_inc(s1["pb"], 1)
                    if m == 23:
                        tensor.wait_ge(s1["rqr"], 2)
                        tensor.wait_ge(s1["bce"], 1)
                        tensor.matmul(b[6][:, 0:512], onesrow[:, :],
                                      bcRQ[0:1, 0:512], start=True,
                                      stop=True).then_inc(s1["pb"], 1)
                    if m == 24:
                        tensor.wait_ge(s1["bce"], 2)
                        tensor.matmul(b[6][:, 0:512], onesrow[:, :],
                                      bcRQ[0:1, 512:1024], start=True,
                                      stop=True).then_inc(s1["pb"], 1)
                    if m == 25:
                        tensor.wait_ge(s1["bce"], 3)
                        tensor.matmul(b[6][:, 0:512], onesrow[:, :],
                                      bcRQp[0:1, 0:512], start=True,
                                      stop=True).then_inc(s2["pb"], 1)
                    if m == 26:
                        tensor.wait_ge(s2["iv"], 1)
                        tensor.matmul(b[6][:, 0:512], onesrow[:, :],
                                      bcRQp[0:1, 512:1024], start=True,
                                      stop=True).then_inc(s2["pb"], 1)
                    for act in SCHED[q * 32 + mi]:
                        t_inj(act)
                tensor.wait_ge(s1["uev"], 16 * (q + 1))
                for o2 in range(16):
                    d_idx = q * 16 + o2
                    tensor.wait_ge(par(s1, "dw", d_idx), 16 * (d_idx // 2 + 1))
                    if d_idx == 0:
                        tensor.wait_ge(s2["bc"], 4)
                    if d_idx >= 1:
                        tensor.wait_ge(s1["dev"], d_idx)
                    for mi2 in range(16):
                        st = (mi2 == 0)
                        sp = (mi2 == 15)
                        stat = a_wd[o2 % 2][:, mi2 * 128:(mi2 + 1) * 128]
                        tensor.matmul(b[3][:, :], stat,
                                      a_fuse[:, mi2 * NT:mi2 * NT + 512],
                                      start=st, stop=sp)
                        mm = tensor.matmul(b[4][:, :], stat,
                                           a_fuse[:, mi2 * NT + 512:mi2 * NT + 1024],
                                           start=st, stop=sp)
                    mm.then_inc(s1["dmm"], 1)
                    for act in SCHED[q * 32 + 16 + o2]:
                        t_inj(act)
            for act in SCHED[128]:
                t_inj(act)

        @blk.vector
        def _(vector):
            n_tern = [0]

            def tern_unit(k):
                for _ in range(k):
                    if n_tern[0] >= NSL:
                        return
                    j = n_tern[0]
                    vector.wait_ge(par(s1, "tl", j), 16 * (32 + j // 2 + 1))
                    if j >= 2:
                        vector.wait_ge(par(s1, "ts", j), 16 * (j // 2))
                        if j - 2 < 32:
                            vector.wait_ge(s2["kmm"], j - 1)
                    kind, i = TSL[j]
                    col = 0 if kind == "q" else 1
                    vector.tensor_scalar(a_abs[:, :], a_tb[j % 2][:, :],
                                         negc_qp[:, col:col + 1], None, ALU.is_lt)
                    vector.scalar_tensor_tensor(a_tb[j % 2][:, :], a_tb[j % 2][:, :],
                                                c_qp[:, col:col + 1], a_abs[:, :],
                                                ALU.is_gt, ALU.subtract)
                    if kind == "q":
                        vector.tensor_tensor(a_tern[j % 2][:, :], a_tb[j % 2][:, :],
                                             a_patq[:, :], ALU.mult).then_inc(s1["tt"], 1)
                    else:
                        for fc in range(NF):
                            tt_ = vector.tensor_scalar(
                                a_tern[j % 2][:, fc * 128:(fc + 1) * 128],
                                a_tb[j % 2][:, fc * 128:(fc + 1) * 128],
                                fw_s[:, fc:fc + 1], None, ALU.mult)
                        tt_.then_inc(s1["tt"], 1)
                    n_tern[0] += 1

            def v_kv(n):
                g, sl = n // 32, n % 32
                P = g * 16 + sl // 2
                vector.wait_ge(s2["kmm"], n + 1)
                if n == 0:
                    vector.wait_ge(s2["iv"], 2)
                if sl % 2 == 0 and P >= 2:
                    vector.wait_ge(s2["tr"], P - 1)
                dst = a_ksb[P % 2] if sl % 2 == 0 else a_vsb[P % 2]
                vector.tensor_tensor(dst[:, :], b[5][:, :],
                                     bcRQp[:, g * 512:g * 512 + 512],
                                     ALU.mult).then_inc(s2["kev"], 1)

            def v_ch(P):
                h = P % 16
                vector.wait_ge(s2["tr"], P + 1)
                vector.tensor_copy(a_vT[:, :], pt[:, 0:512])
                vector.tensor_scalar(a_kT[:, :], pt[:, 512:1024], lamRc[:, h:h + 1],
                                     None, ALU.mult).then_inc(s2["te"], 1)
                vector.wait_ge(s2["kvm"], P + 1)
                for cl in range(4):
                    tt_ = vector.scalar_tensor_tensor(
                        S[:, h * 128:(h + 1) * 128],
                        S[:, h * 128:(h + 1) * 128],
                        lamTc[:, h:h + 1], b[6][:, cl * 128:(cl + 1) * 128],
                        ALU.mult, ALU.add)
                tt_.then_inc(s2["su"], 1)
                if P >= 16:
                    vector.tensor_copy(Sb[:, h * 128:(h + 1) * 128],
                                       S[:, h * 128:(h + 1) * 128]).then_inc(s2["sb"], 1)

            def v_xnp(i2):
                g, i = i2 // 16, i2 % 16
                if i2 < 16:
                    vector.wait_ge(par(s2, "pd", i2), 16 * (i2 // 2 + 1))
                else:
                    vector.wait_ge(par(s2, "pe", i2), 16 * ((i2 - 16) // 2 + 1))
                if i2 == 16:
                    vector.wait_ge(s2["kmm"], 32)
                hs = slice(g * 512, g * 512 + 512)
                vector.tensor_tensor(a_xpb[i2 % 2][:, :], a_xpb[i2 % 2][:, :],
                                     a_bcAp[:, hs], ALU.mult)
                vector.tensor_tensor(a_xnp[:, i * 512:(i + 1) * 512],
                                     a_xpb[i2 % 2][:, :], a_bcBp[:, hs],
                                     ALU.subtract).then_inc(s2["xn"], 2)

            def v_inj(act):
                if act[0] == "kv":
                    v_kv(act[1])
                elif act[0] == "ch":
                    v_ch(act[1])
                else:
                    v_xnp(act[1])

            for fc in range(16):
                vector.tensor_scalar(a_patq[:, fc * 128:(fc + 1) * 128], ones_t[:, :],
                                     qn_s[:, fc:fc + 1], None, ALU.mult)

            for q in range(4):
                for mi in range(16):
                    m = q * 16 + mi
                    vector.wait_ge(s1["u"], m + 1)
                    vector.wait_ge(s1["gev"], m + 1)
                    if mi == 0 and q >= 1:
                        vector.wait_ge(s1["dmm"], 16 * q)
                    vector.scalar_tensor_tensor(a_fuse[:, mi * NT:mi * NT + 512],
                                                b[2][:, :], su_s[:, m:m + 1],
                                                a_gs[:, 0:512], ALU.mult, ALU.mult)
                    vector.scalar_tensor_tensor(a_fuse[:, mi * NT + 512:(mi + 1) * NT],
                                                b[0][:, :], su_s[:, m:m + 1],
                                                a_gs[:, 512:1024], ALU.mult,
                                                ALU.mult).then_inc(s1["uev"], 1)
                    if m == 20:
                        vector.wait_ge(s1["ta"], NSL)
                        vector.reduce_sum(absr[:, 0:1], absq[:, :], axis=AX.X)
                        vector.reduce_sum(absr[:, 1:2], absp[:, :],
                                          axis=AX.X).then_inc(s1["ar2"], 1)
                    if m == 21:
                        vector.wait_ge(s1["ab"], 1)
                        vector.tensor_scalar(invrow[0:1, 0:1], b[6][0:1, 0:1],
                                             1.0 / QKV_N, 1e-5, ALU.mult, ALU.max)
                        vector.tensor_scalar(invrow[0:1, 1:2], b[6][0:1, 1:2],
                                             1.0 / PROJ_N, 1e-5, ALU.mult,
                                             ALU.max).then_inc(s1["cr"], 1)
                        vector.tensor_scalar(bcRQ[0:1, :], bcRQ[0:1, :],
                                             invrow[0:1, 0:1], None,
                                             ALU.mult).then_inc(s1["rqr"], 1)
                        vector.tensor_scalar(bcRQp[0:1, :], bcRQp[0:1, :],
                                             invrow[0:1, 0:1], None,
                                             ALU.mult).then_inc(s1["rqr"], 1)
                    if m == 22:
                        vector.wait_ge(s1["pb"], 1)
                        vector.tensor_scalar(c_qp[:, :], b[6][:, 0:2], 0.5, None,
                                             ALU.mult)
                        vector.tensor_scalar(negc_qp[:, :], c_qp[:, :], -1.0,
                                             None, ALU.mult).then_inc(s1["bce"], 1)
                    if m == 23:
                        vector.wait_ge(s1["pb"], 2)
                        vector.tensor_copy(bcRQ[:, 0:512],
                                           b[6][:, 0:512]).then_inc(s1["bce"], 1)
                    if m == 24:
                        vector.wait_ge(s1["pb"], 3)
                        vector.tensor_copy(bcRQ[:, 512:1024],
                                           b[6][:, 0:512]).then_inc(s1["bce"], 1)
                    if m == 25:
                        vector.wait_ge(s2["pb"], 5)
                        vector.tensor_copy(bcRQp[:, 0:512],
                                           b[6][:, 0:512]).then_inc(s2["iv"], 1)
                    if m == 26:
                        vector.wait_ge(s2["pb"], 6)
                        vector.tensor_copy(bcRQp[:, 512:1024],
                                           b[6][:, 0:512]).then_inc(s2["iv"], 1)
                    if m >= 27:
                        tern_unit(1)
                    if 27 <= m < 43:
                        for h in (m - 27,):
                            if h >= 2:
                                vector.wait_ge(s2["mska"], h - 1)
                            mt = a_mtmp[h % 2]
                            vector.tensor_scalar(mt[:, :], cD[:, :],
                                                 negs[:, h:h + 1], None, ALU.mult)
                            vector.tensor_tensor(mt[:, :], mt[:, :],
                                                 cPEN[:, :],
                                                 ALU.add).then_inc(s2["mskd"], 1)
                    for act in SCHED[q * 32 + mi]:
                        v_inj(act)
                for o2 in range(16):
                    d_idx = q * 16 + o2
                    vector.wait_ge(s1["dmm"], d_idx + 1)
                    if d_idx >= 2:
                        vector.wait_ge(par(s1, "dst", d_idx), 16 * (d_idx // 2))
                    vector.tensor_copy(a_dt[d_idx % 2][:, 0:512], b[3][:, :])
                    vector.tensor_copy(a_dt[d_idx % 2][:, 512:1024],
                                       b[4][:, :]).then_inc(s1["dev"], 1)
                    if q >= 1:
                        tern_unit(1)
                        if q >= 2:
                            tern_unit(1)
                    for act in SCHED[q * 32 + 16 + o2]:
                        v_inj(act)
            tern_unit(NSL)
            for act in SCHED[128]:
                v_inj(act)

    G1OC = []
    for hb_ in range(4):
        for base_ in (0, 16, 32):
            G1OC += [base_ + 4 * hb_ + i_ for i_ in range(4)]

    with nc.Block() as blk:
        @blk.sync
        def _(sync):
            for jj in range(2 * QOC):
                oc = G1OC[jj - QOC] if jj >= QOC else jj
                if jj < 2:
                    sync.wait_ge(s1["g"], 64)
                if jj >= 2:
                    sync.wait_ge(s3["qmm"], jj - 1)
                sync.dma_start(a_ts3[jj % 2][:, :],
                               d_scq.ap()[oc * 128:(oc + 1) * 128, :]).then_inc(par(s3, "tl", jj), 16)

        @blk.tensor
        def _(tensor):
            def t_qkv(jj, g):
                tensor.wait_ge(par(s3, "tl", jj), 16 * (jj // 2 + 1))
                if jj >= 2:
                    tensor.wait_ge(s3["qev"], jj - 1)
                bank = b[0] if jj % 2 == 0 else b[1]
                for fc in range(NF):
                    mm = tensor.matmul(bank[:, :],
                                       a_ts3[jj % 2][:, fc * 128:(fc + 1) * 128],
                                       xn[:, fc * NT + g * 512:fc * NT + g * 512 + 512],
                                       start=(fc == 0), stop=(fc == NF - 1))
                mm.then_inc(s3["qmm"], 1)

            def t_inner(g, hb, cl):
                        bi = g * 16 + hb * 4 + cl
                        hs = [4 * hb + i_ for i_ in range(4)]
                        if bi >= 1:
                            tensor.wait_ge(s3["me"], bi)
                        for i, h in enumerate(hs):
                            ksl = a_qkv[:, (16 + h) * 512 + cl * 128:(16 + h) * 512 + cl * 128 + 128]
                            qsl = a_qkv[:, h * 512 + cl * 128:h * 512 + cl * 128 + 128]
                            mm = tensor.matmul(b[2][:, i * 128:(i + 1) * 128],
                                               ksl, qsl, start=True, stop=True)
                        mm.then_inc(s3["sc"], 1)
                        tensor.wait_ge(s3["kp"], bi + 1)
                        if bi >= 1:
                            tensor.wait_ge(s3["te"], bi)
                        for i, h in enumerate(hs):
                            vsl = a_qkv[:, (32 + h) * 512 + cl * 128:(32 + h) * 512 + cl * 128 + 128]
                            ksl = a_qkv[:, (16 + h) * 512 + cl * 128:(16 + h) * 512 + cl * 128 + 128]
                            tensor.transpose(pt[:, i * 128:(i + 1) * 128], vsl,
                                             idn[:, :])
                            mm = tensor.transpose(pt[:, 512 + i * 128:512 + (i + 1) * 128],
                                                  ksl, idn[:, :])
                        mm.then_inc(s3["tr"], 1)
                        tensor.wait_ge(s3["me"], bi + 1)
                        tensor.wait_ge(s3["te"], bi + 1)
                        if g == 0:
                            tensor.wait_ge(s2["sb"], 4 * hb + 4)
                        if bi >= 1:
                            tensor.wait_ge(s3["oe"], bi)
                        for i, h in enumerate(hs):
                            tensor.matmul(b[4][:, i * 128:(i + 1) * 128],
                                          a_vT3[:, i * 128:(i + 1) * 128],
                                          a_sm[:, i * 128:(i + 1) * 128],
                                          start=True, stop=False)
                            mm = tensor.matmul(b[4][:, i * 128:(i + 1) * 128],
                                               Sb[:, h * 128:(h + 1) * 128],
                                               a_qt[:, i * 128:(i + 1) * 128],
                                               start=False, stop=True)
                        mm.then_inc(s3["ic"], 1)
                        if bi >= 1:
                            tensor.wait_ge(s3["su"], bi)
                        for i in range(4):
                            mm = tensor.matmul(b[3][:, i * 128:(i + 1) * 128],
                                               a_kT3[:, i * 128:(i + 1) * 128],
                                               a_vT3[:, i * 128:(i + 1) * 128],
                                               start=True, stop=True)
                        mm.then_inc(s3["kv"], 1)

            for jj in range(QOC):
                t_qkv(jj, 0)
            tensor.wait_ge(s3["qev"], QOC)
            for hb in range(4):
                for cl in range(4):
                    t_inner(0, hb, cl)
                    if hb >= 1:
                        for r_ in range(3):
                            t_qkv(QOC + 12 * (hb - 1) + 3 * cl + r_, 1)
            for r_ in range(12):
                t_qkv(QOC + 36 + r_, 1)
            tensor.wait_ge(s3["qev"], 2 * QOC)
            for hb in range(4):
                for cl in range(4):
                    t_inner(1, hb, cl)

        @blk.vector
        def _(vector):
            def v_qkv(jj, g):
                oc = G1OC[jj - QOC] if jj >= QOC else jj
                vector.wait_ge(s3["qmm"], jj + 1)
                if jj >= QOC:
                    vector.wait_ge(s3["kv"], 4 * ((jj - QOC) // 12) + 4)
                bank = b[0] if jj % 2 == 0 else b[1]
                vector.tensor_tensor(a_qkv[:, oc * 512:(oc + 1) * 512],
                                     bank[:, :],
                                     bcRQ[:, g * 512:g * 512 + 512],
                                     ALU.mult).then_inc(s3["qev"], 1)

            def v_inner(g, hb, cl):
                        bi = g * 16 + hb * 4 + cl
                        hs = [4 * hb + i_ for i_ in range(4)]
                        for i, h in enumerate(hs):
                            qsl = a_qkv[:, h * 512 + cl * 128:h * 512 + cl * 128 + 128]
                            tt_ = vector.tensor_tensor(a_qt[:, i * 128:(i + 1) * 128],
                                                       qsl,
                                                       lamTt[:, h * 128:(h + 1) * 128],
                                                       ALU.mult)
                        tt_.then_inc(s3["kp"], 1)
                        vector.wait_ge(s3["sc"], bi + 1)
                        for i, h in enumerate(hs):
                            tt_ = vector.tensor_tensor(a_sm[:, i * 128:(i + 1) * 128],
                                                       b[2][:, i * 128:(i + 1) * 128],
                                                       maskT[:, h * 128:(h + 1) * 128],
                                                       ALU.mult)
                        tt_.then_inc(s3["me"], 1)
                        vector.wait_ge(s3["tr"], bi + 1)
                        vector.tensor_copy(a_vT3[:, :], pt[:, 0:512])
                        for i, h in enumerate(hs):
                            tt_ = vector.tensor_scalar(a_kT3[:, i * 128:(i + 1) * 128],
                                                       pt[:, 512 + i * 128:512 + (i + 1) * 128],
                                                       lamRc[:, h:h + 1], None,
                                                       ALU.mult)
                        tt_.then_inc(s3["te"], 1)
                        vector.wait_ge(s3["kv"], bi + 1)
                        for i, h in enumerate(hs):
                            vector.scalar_tensor_tensor(S[:, h * 128:(h + 1) * 128],
                                                        S[:, h * 128:(h + 1) * 128],
                                                        lamTc[:, h:h + 1],
                                                        b[3][:, i * 128:(i + 1) * 128],
                                                        ALU.mult, ALU.add)
                        vector.tensor_copy(Sb[:, 4 * hb * 128:(4 * hb + 4) * 128],
                                           S[:, 4 * hb * 128:(4 * hb + 4) * 128]).then_inc(s3["su"], 1)

            for jj in range(QOC):
                v_qkv(jj, 0)
            for hb in range(4):
                for cl in range(4):
                    v_inner(0, hb, cl)
                    if hb >= 1:
                        for r_ in range(3):
                            v_qkv(QOC + 12 * (hb - 1) + 3 * cl + r_, 1)
            for r_ in range(12):
                v_qkv(QOC + 36 + r_, 1)
            for hb in range(4):
                for cl in range(4):
                    v_inner(1, hb, cl)

        @blk.scalar
        def _(scalar):
            for g in range(2):
                for hb in range(4):
                    for cl in range(4):
                        bi = g * 16 + hb * 4 + cl
                        scalar.wait_ge(s3["ic"], bi + 1)
                        for i, h in enumerate(4 * hb + i_ for i_ in range(4)):
                            dst = a_o16[:, h * NT + g * 512 + cl * 128:
                                        h * NT + g * 512 + cl * 128 + 128]
                            aa = scalar.activation(dst, b[4][:, i * 128:(i + 1) * 128],
                                                   AF.Copy, scale=cst[:, 4:5])
                        aa.then_inc(s3["oe"], 1)

    with nc.Block() as blk:
        @blk.scalar
        def _(scalar):
            for f in range(NF):
                if f >= 1:
                    scalar.wait_ge(s4["mma"], f)
                scalar.activation(a_st[0][:, :], a_o16[:, f * NT:(f + 1) * NT],
                                  AF.Square).then_inc(s4["sqa"], 1)
                if f >= 1:
                    scalar.wait_ge(s4["mmb"], f)
                scalar.activation(a_st[1][:, :], a_o16[:, f * NT:(f + 1) * NT],
                                  AF.Square,
                                  scale=nw_s[:, f:f + 1]).then_inc(s4["sqb"], 1)
            R1 = rows[0:1, 512:1024]
            R2 = rows[0:1, 0:512]
            for half in range(2):
                scalar.wait_ge(s4["r1"], 1 + 4 * half)
                scalar.activation(R2, R1, AF.Sqrt,
                                  bias=cst[0:1, 2:3]).then_inc(s4["r2"], 1)
                scalar.wait_ge(s4["r1"], 4 + 4 * half)
                scalar.activation(R2, R1, AF.Sqrt,
                                  bias=cst[0:1, 1:2]).then_inc(s4["r2"], 1)

        @blk.tensor
        def _(tensor):
            for f in range(NF):
                tensor.wait_ge(s4["sqa"], f + 1)
                st = (f == 0)
                sp = (f == NF - 1)
                tensor.matmul(b[0][0:1, 0:512], onescol[:, :], a_st[0][:, 0:512],
                              start=st, stop=sp)
                tensor.matmul(b[1][0:1, 0:512], onescol[:, :], a_st[0][:, 512:1024],
                              start=st, stop=sp).then_inc(s4["mma"], 1)
                tensor.wait_ge(s4["sqb"], f + 1)
                tensor.matmul(b[2][0:1, 0:512], onescol[:, :], a_st[1][:, 0:512],
                              start=st, stop=sp)
                tensor.matmul(b[3][0:1, 0:512], onescol[:, :], a_st[1][:, 512:1024],
                              start=st, stop=sp).then_inc(s4["mmb"], 1)
            tensor.wait_ge(s4["row"], 1)
            tensor.matmul(b[4][:, 0:512], onesrow[:, :], a_bcRT[0:1, 0:512],
                          start=True, stop=True).then_inc(s4["pb"], 1)
            tensor.wait_ge(s4["bc"], 1)
            tensor.matmul(b[5][:, 0:512], onesrow[:, :], a_bcRT[0:1, 512:1024],
                          start=True, stop=True).then_inc(s4["pb"], 1)
            tensor.wait_ge(s4["bc"], 2)
            for o2 in range(POC):
                tensor.wait_ge(par(s4, "tl", o2), 16 * (o2 // 2 + 1))
                if o2 >= 2:
                    tensor.wait_ge(s4["cmb"], o2 - 1)
                blo = b[0] if o2 % 2 == 0 else b[2]
                bhi = b[1] if o2 % 2 == 0 else b[3]
                for fc in range(NF):
                    st = (fc == 0)
                    sp = (fc == NF - 1)
                    stat = a_ts4[o2 % 2][:, fc * 128:(fc + 1) * 128]
                    tensor.matmul(blo[:, :], stat,
                                  a_o16[:, fc * NT:fc * NT + 512], start=st, stop=sp)
                    mm = tensor.matmul(bhi[:, :], stat,
                                       a_o16[:, fc * NT + 512:(fc + 1) * NT],
                                       start=st, stop=sp)
                mm.then_inc(s4["pm"], 1)

        @blk.vector
        def _(vector):
            vector.wait_ge(s4["mma"], 16)
            vector.wait_ge(s4["mmb"], 16)
            R0 = rows[0:1, 0:512]
            R1 = rows[0:1, 512:1024]
            R2 = rows[0:1, 0:512]
            for half in range(2):
                sA = (b[0] if half == 0 else b[1])[0:1, 0:512]
                sB = (b[2] if half == 0 else b[3])[0:1, 0:512]
                hs = slice(half * 512, half * 512 + 512)
                rrow = a_bcRT[0:1, hs]
                vector.tensor_scalar(R1, sA, 1.0 / 2048.0, None,
                                     ALU.mult).then_inc(s4["r1"], 1)
                vector.wait_ge(s4["r2"], 1 + 2 * half)
                vector.reciprocal(rrow, R2)
                vector.tensor_tensor(R0, rrow, rrow, ALU.mult)
                vector.scalar_tensor_tensor(R1, sB, 1.0 / 2048.0, R0,
                                            ALU.mult, ALU.mult).then_inc(s4["r1"], 3)
                vector.wait_ge(s4["r2"], 2 + 2 * half)
                vector.reciprocal(R0, R2)
                vector.tensor_tensor(R1, rrow, R0, ALU.mult)
                vector.tensor_scalar(rrow, R1, invrow[0:1, 1:2], None, ALU.mult)
            vector.tensor_copy(absr[:, 1:2], onescol[:, :]).then_inc(s4["row"], 1)
            vector.wait_ge(s4["pb"], 1)
            vector.tensor_copy(a_bcRT[:, 0:512], b[4][:, 0:512]).then_inc(s4["bc"], 1)
            vector.wait_ge(s4["pb"], 2)
            vector.tensor_copy(a_bcRT[:, 512:1024], b[5][:, 0:512]).then_inc(s4["bc"], 1)
            for o2 in range(POC):
                vector.wait_ge(s4["pm"], o2 + 1)
                vector.wait_ge(par(s4, "ld", o2), 80 * (o2 // 2 + 1))
                if o2 >= 2:
                    vector.wait_ge(par(s4, "st", o2), 16 * (o2 // 2))
                pb5 = a_p5[o2 % 2]
                blo = b[0] if o2 % 2 == 0 else b[2]
                bhi = b[1] if o2 % 2 == 0 else b[3]
                for half in range(2):
                    bank = blo if half == 0 else bhi
                    hs = slice(half * 512, half * 512 + 512)
                    vector.tensor_tensor(
                        a_pt1[:, :], pb5[:, NT + half * 512:NT + half * 512 + 512],
                        pb5[:, 2 * NT + half * 512:2 * NT + half * 512 + 512], ALU.add)
                    vector.tensor_tensor(
                        a_pt2[:, :], pb5[:, 3 * NT + half * 512:3 * NT + half * 512 + 512],
                        pb5[:, 4 * NT + half * 512:4 * NT + half * 512 + 512], ALU.add)
                    vector.tensor_tensor(a_pt1[:, :], a_pt1[:, :], a_pt2[:, :],
                                         ALU.add)
                    vector.tensor_tensor(a_pt2[:, :], bank[:, :], a_bcRT[:, hs],
                                         ALU.mult)
                    vector.scalar_tensor_tensor(a_pt1[:, :], a_pt1[:, :],
                                                sd_s[:, o2:o2 + 1], a_pt2[:, :],
                                                ALU.mult, ALU.add)
                    mm = vector.tensor_tensor(a_ob[o2 % 2][:, hs], a_pt1[:, :],
                                              pb5[:, half * 512:half * 512 + 512],
                                              ALU.add)
                mm.then_inc(s4["cmb"], 1)

        @blk.gpsimd
        def _(gpsimd):
            gpsimd.wait_ge(s3["su"], 32)
            for o2 in range(1, POC, 2):
                if o2 >= 3:
                    gpsimd.wait_ge(s4["cmb"], o2 - 1)
                pb5 = a_p5[o2 % 2]
                gpsimd.dma_start(pb5[:, 0:NT],
                                 d_x.ap()[o2 * 128:(o2 + 1) * 128, :]).then_inc(par(s4, "ld", o2), 16)
                for q in range(4):
                    gpsimd.dma_start(pb5[:, (q + 1) * NT:(q + 2) * NT],
                                     d_scf.ap()[q * 2048 + o2 * 128:
                                                q * 2048 + (o2 + 1) * 128, :]).then_inc(par(s4, "ld", o2), 16)

        @blk.sync
        def _(sync):
            sync.wait_ge(s1["dst0"], 16 * 32)
            sync.wait_ge(s1["dst1"], 16 * 32)
            for o2 in range(POC + 2):
                if o2 < POC:
                    if o2 % 2 == 0:
                        if o2 < 2:
                            sync.wait_ge(s3["su"], 32)
                        if o2 >= 2:
                            sync.wait_ge(s4["cmb"], o2 - 1)
                        pb5 = a_p5[o2 % 2]
                        sync.dma_start(pb5[:, 0:NT],
                                       d_x.ap()[o2 * 128:(o2 + 1) * 128, :]).then_inc(par(s4, "ld", o2), 16)
                        for q in range(4):
                            sync.dma_start(pb5[:, (q + 1) * NT:(q + 2) * NT],
                                           d_scf.ap()[q * 2048 + o2 * 128:
                                                      q * 2048 + (o2 + 1) * 128, :]).then_inc(par(s4, "ld", o2), 16)
                    if o2 < 2:
                        sync.wait_ge(s3["qmm"], 96)
                    if o2 >= 2:
                        sync.wait_ge(s4["pm"], o2 - 1)
                    pos = TPOS[("p", o2)]
                    sync.wait_ge(par(s1, "ts", pos), 16 * (pos // 2 + 1))
                    sync.dma_start(a_ts4[o2 % 2][:, :],
                                   d_scp.ap()[o2 * 128:(o2 + 1) * 128, :]).then_inc(par(s4, "tl", o2), 16)
                if o2 >= 2:
                    j = o2 - 2
                    sync.wait_ge(s4["cmb"], j + 1)
                    sync.dma_start(d_out.ap()[j * 128:(j + 1) * 128, :],
                                   a_ob[j % 2][:, :]).then_inc(par(s4, "st", j), 16)
            sync.wait_ge(s4["st0"], 16 * 8)
            sync.wait_ge(s4["st1"], 16 * 8)

    ctx.pop_all()
    nc.finalize()
    return nc


_PROG = None


def _host_inputs(L0_attn_x, L0_attn_qkv_w, L0_attn_qkv_norm_w, L0_attn_proj_w,
                 L0_attn_proj_norm_w, L0_attn_norm_w, L0_attn_s, L0_ffn_w_gate,
                 L0_ffn_w_up, L0_ffn_w_down, L0_ffn_s_gate, L0_ffn_s_up,
                 L0_ffn_s_down):
    f32 = np.float32
    x = np.asarray(L0_attn_x, f32)
    qkv_w = np.asarray(L0_attn_qkv_w, f32)
    proj_w = np.asarray(L0_attn_proj_w, f32)
    wg = np.asarray(L0_ffn_w_gate, np.int8)
    wu = np.asarray(L0_ffn_w_up, np.int8)
    wd = np.asarray(L0_ffn_w_down, np.int8)

    qkvt = np.ascontiguousarray(
        qkv_w.reshape(48, 128, 16, 128).transpose(0, 3, 2, 1).reshape(48 * 128, 2048))
    projt = np.ascontiguousarray(
        proj_w.reshape(16, 128, 16, 128).transpose(0, 3, 2, 1).reshape(16 * 128, 2048))
    wgr = np.ascontiguousarray(
        wg.reshape(16, 128, 64, 128).transpose(2, 1, 0, 3).reshape(64 * 128, 2048))
    wur = np.ascontiguousarray(
        wu.reshape(16, 128, 64, 128).transpose(2, 1, 0, 3).reshape(64 * 128, 2048))
    wdr = np.ascontiguousarray(
        wd.reshape(4, 16, 128, 16, 128).transpose(0, 3, 2, 1, 4).reshape(64 * 128, 2048))

    col16 = lambda v: np.ascontiguousarray(np.asarray(v, f32).reshape(16, 128).T)
    qn = col16(L0_attn_qkv_norm_w)
    nw = col16(L0_attn_norm_w)
    pnw = col16(L0_attn_proj_norm_w)
    sd = np.ascontiguousarray(np.asarray(L0_ffn_s_down, f32).reshape(16, 128).T)
    sg = np.ascontiguousarray(np.asarray(L0_ffn_s_gate, f32).reshape(64, 128).T)
    su = np.ascontiguousarray(np.asarray(L0_ffn_s_up, f32).reshape(64, 128).T)
    s16 = np.ascontiguousarray(np.tile(np.asarray(L0_attn_s, f32), (128, 1)))

    ar = np.arange(128, dtype=f32)
    cDm = np.ascontiguousarray(ar[None, :] - ar[:, None])
    cPENm = np.ascontiguousarray(np.where(cDm >= 0, 0.0, -1e30).astype(f32))
    cTm = np.ascontiguousarray(np.broadcast_to(ar, (128, 128)))
    cRm = np.ascontiguousarray((128.0 - cTm).astype(f32))
    idn = np.eye(128, dtype=f32).astype(ml_dtypes.bfloat16)
    cstm = np.zeros((128, 8), f32)
    cstm[:, 0] = 1e-5
    cstm[:, 1] = EPS32
    cstm[:, 2] = 1e-6
    cstm[:, 3] = -128.0
    cstm[:, 4] = SCALE
    cstm[:, 5] = 128.0 - ar

    shared = dict(qkvt=qkvt, projt=projt, wgr=wgr, wur=wur, wdr=wdr, qn=qn,
                  nw=nw, pnw=pnw, sdv=sd, sgv=sg, suv=su, s16=s16, cD=cDm,
                  cPEN=cPENm, cT=cTm, cR=cRm, idn=idn, cstv=cstm)
    zeros_prev = np.zeros((2048, 1024), f32)
    in_maps = []
    for c in range(NCORE):
        bb, half = c // 2, c % 2
        xT = np.ascontiguousarray(x[bb, half * 1024:(half + 1) * 1024, :].T)
        if half == 1:
            xpT = np.ascontiguousarray(x[bb, 0:1024, :].T)
        else:
            xpT = zeros_prev
        m = dict(shared)
        m["xT"] = xT
        m["xprevT"] = xpT
        in_maps.append(m)
    return in_maps


def kernel(**inputs):
    global _PROG
    in_maps = _host_inputs(**inputs)
    if _PROG is None:
        _PROG = build_program()
    res = run_bass_kernel_spmd(_PROG, in_maps, core_ids=list(range(NCORE)))
    out = np.empty((4, 2048, 2048), np.float32)
    for c in range(NCORE):
        bb, half = c // 2, c % 2
        out[bb, half * 1024:(half + 1) * 1024, :] = res.results[c]["outT"].T
    return out
